# version 25
# baseline (speedup 1.0000x reference)
import contextlib
import numpy as np
import concourse.bass as bass
import concourse.mybir as mybir
from concourse.bass_utils import run_bass_kernel_spmd

F32 = mybir.dt.float32
BF16 = mybir.dt.bfloat16
AF = mybir.ActivationFunctionType
ALU = mybir.AluOpType
AX = mybir.AxisListType.X

DM = 1024
H = 16
D = 64
CH = 128
TG = 512
DFF = 4096
HB = 8 * (D + 1)
NW2 = 6


class Plan:
    ENGS = ("sync", "vector", "scalar", "tensor", "gpsimd")
    QKEY = {"sync": "DS", "scalar": "DA", "gpsimd": "DG"}
    KDMA = 6

    def __init__(self):
        self.q = {e: [] for e in self.ENGS}
        self.cnt = {}
        self.sems = {}
        self.waited = {e: {} for e in self.ENGS}
        self.dmai = {"DS": 0, "DA": 0, "DG": 0}

    def op(self, eng, fn, inc=None, inc_n=1):
        if inc is None:
            self.q[eng].append(fn)
            return None
        self.cnt[inc] = self.cnt.get(inc, 0) + inc_n
        sem = self.sems[inc]

        def wrapped(e, fn=fn, sem=sem, n=inc_n):
            fn(e).then_inc(sem, n)

        self.q[eng].append(wrapped)
        return self.cnt[inc]

    def dma(self, eng, fn):
        qk = self.QKEY[eng]
        i = self.dmai[qk]
        self.dmai[qk] = i + 1
        name = f"{qk}{i % self.KDMA}"
        prev = 16 * (i // self.KDMA)
        if prev:
            self.wait(eng, name, prev)
        self.cnt[name] = self.cnt.get(name, 0) + 16
        sem = self.sems[name]

        def wrapped(e, fn=fn, sem=sem):
            fn(e).then_inc(sem, 16)

        self.q[eng].append(wrapped)
        return (name, self.cnt[name])

    def dma_barrier(self, eng):
        ms = []
        for qk in ("DS", "DA", "DG"):
            for j in range(self.KDMA):
                name = f"{qk}{j}"
                v = self.cnt.get(name, 0)
                if v:
                    self.wait(eng, name, v)
                    ms.append((name, v))
        return ms

    def wait(self, eng, sem_name, val):
        if not val:
            return
        if self.waited[eng].get(sem_name, 0) >= val:
            return
        self.waited[eng][sem_name] = val
        sem = self.sems[sem_name]

        def w(e, sem=sem, v=val):
            e.wait_ge(sem, v)

        self.q[eng].append(w)

    def waitm(self, eng, m):
        if m is not None:
            self.wait(eng, m[0], m[1])

    def now(self, sem_name):
        return self.cnt.get(sem_name, 0)


def build(span=4096):
    NG = span // TG
    NC = span // CH
    JPG = TG // CH
    nc = bass.Bass("TRN2", target_bir_lowering=False, debug=False)

    x_in = nc.dram_tensor("x", [span, DM], F32, kind="ExternalInput").ap()
    wqT_in = nc.dram_tensor("wqT", [DM, DM], F32, kind="ExternalInput").ap()
    wkT_in = nc.dram_tensor("wkT", [DM, DM], F32, kind="ExternalInput").ap()
    wvT_in = nc.dram_tensor("wvT", [DM, DM], F32, kind="ExternalInput").ap()
    wpT_in = nc.dram_tensor("wpT", [DM, DM], F32, kind="ExternalInput").ap()
    W1T_in = nc.dram_tensor("W1T", [DM, DFF], F32, kind="ExternalInput").ap()
    b1_in = nc.dram_tensor("b1", [CH, 32], F32, kind="ExternalInput").ap()
    W2T_in = nc.dram_tensor("W2T", [DFF, DM], F32, kind="ExternalInput").ap()
    b2_in = nc.dram_tensor("b2", [DM], F32, kind="ExternalInput").ap()
    ident_in = nc.dram_tensor("ident", [CH, CH], F32, kind="ExternalInput").ap()
    maskT_in = nc.dram_tensor("maskT", [CH, CH], F32, kind="ExternalInput").ap()
    pmask_in = nc.dram_tensor("pmask", [CH, 8], F32, kind="ExternalInput").ap()
    out_d = nc.dram_tensor("out", [span, DM], F32, kind="ExternalOutput").ap()

    xnT_sp = nc.dram_tensor("xnT_sp", [DM, span], BF16).ap()
    kT_sp = nc.dram_tensor("kT_sp", [DM, span], BF16).ap()
    va_sp = nc.dram_tensor("va_sp", [span, H * (D + 1)], BF16).ap()
    snap_sp = nc.dram_tensor("snap_sp", [NC + 1, CH, HB], BF16).ap()
    acc_sp = nc.dram_tensor("acc_sp", [span, DM], F32).ap()
    w2qt_sp = nc.dram_tensor("w2qt_sp", [DFF, DM], BF16).ap()
    ag_in_t = nc.dram_tensor("ag_in", [CH, HB], F32)
    ag_out_t = nc.dram_tensor("ag_out", [CH * 8, HB], F32)
    ag_in = ag_in_t.ap()
    ag_out = ag_out_t.ap()

    with contextlib.ExitStack() as ctx:
        E = ctx.enter_context
        ident = E(nc.sbuf_tensor("ident_sb", [CH, CH], BF16))
        maskT = E(nc.sbuf_tensor("maskT_sb", [CH, CH], BF16))
        pmask = E(nc.sbuf_tensor("pmask_sb", [CH, 8], F32))
        ones_r = E(nc.sbuf_tensor("ones_r", [1, CH], F32))
        onesc = E(nc.sbuf_tensor("onesc", [CH, 1], F32))
        sm = E(nc.sbuf_tensor("sm", [CH, 16], F32))
        gacc = E(nc.sbuf_tensor("gacc", [CH, 72], F32))
        thrs = E(nc.sbuf_tensor("thrs", [CH, 4], F32))
        recb = E(nc.sbuf_tensor("recb", [CH, 2, 16], F32))
        cvals = E(nc.sbuf_tensor("cvals", [CH, 4], F32))
        b1col = E(nc.sbuf_tensor("b1col", [CH, 32], F32))
        b2bc = E(nc.sbuf_tensor("b2bc", [CH, DM], F32))
        xg = E(nc.sbuf_tensor("xg", [CH, 2, DM], F32))
        xnb = E(nc.sbuf_tensor("xnb", [CH, 2, DM], BF16))
        xnT = E(nc.sbuf_tensor("xnT", [CH, 2, 8, TG], BF16))
        va = E(nc.sbuf_tensor("va", [CH, 2, JPG, H * (D + 1)], BF16))
        ktok = E(nc.sbuf_tensor("ktok", [CH, 2, DM], BF16))
        poolP = E(nc.sbuf_tensor("poolP", [CH, 32768], BF16))
        poolQ = E(nc.sbuf_tensor("poolQ", [CH, 16384], BF16))
        w2s = E(nc.sbuf_tensor("w2s", [CH, NW2 * TG], BF16))
        wst = E(nc.sbuf_tensor("wst", [CH, 2, DM], F32))
        snapst = E(nc.sbuf_tensor("snapst", [CH, 2, HB], BF16))
        snld = E(nc.sbuf_tensor("snld", [CH, 2, HB], BF16))
        agst = E(nc.sbuf_tensor("agst", [CH, HB], F32))
        gr = E(nc.sbuf_tensor("gr", [CH, 2, HB], F32))
        pp = E(nc.sbuf_tensor("pp", [CH, 2, HB], F32))
        spre = E(nc.sbuf_tensor("spre", [CH, HB], BF16))
        saugT = E(nc.sbuf_tensor("saugT", [CH, 2, HB], BF16))
        mS = E(nc.sbuf_tensor("mS", [CH, 2, H, CH], BF16))
        yTb = E(nc.sbuf_tensor("yTb", [CH, 2, 8, CH], BF16))
        accb = E(nc.sbuf_tensor("accb", [CH, 2, DM], F32))
        eexp = E(nc.sbuf_tensor("eexp", [CH, 2, TG], F32))

        psA = E(nc.psum_tensor("psA", [CH, 1024], BF16))
        psB = E(nc.psum_tensor("psB", [CH, 1024], BF16))
        psC = E(nc.psum_tensor("psC", [CH, 512], F32))
        psD = E(nc.psum_tensor("psD", [CH, 512], F32))
        psE = E(nc.psum_tensor("psE", [CH, 512], F32))
        psF = E(nc.psum_tensor("psF", [CH, 512], F32))
        psG = E(nc.psum_tensor("psG", [CH, 512], F32))
        psH = E(nc.psum_tensor("psH", [CH, 512], F32))
        psAB = [psA, psB]
        psCD = [psC, psD]
        psEF = [psE, psF]
        psGH = [psG, psH]
        ps4 = [psC, psD, psE, psF]
        o2banks = [psE, psF, psG, psH]

        p = Plan()
        semnames = ["T", "V", "S", "G", "CC"]
        for qk in ("DS", "DA", "DG"):
            semnames += [f"{qk}{j}" for j in range(Plan.KDMA)]
        for s in semnames:
            p.sems[s] = E(nc.semaphore("sem" + s))
            p.cnt[s] = 0

        free = {}

        def guard(eng, key):
            p.waitm(eng, free.get(key))

        def wT(slot, dt, c0, n):
            return poolP[:, 8192 * slot + 1024 * dt + c0:8192 * slot + 1024 * dt + c0 + n]

        def w1qt(dt, f0, n):
            return poolP[:, 4096 * dt + f0:4096 * dt + f0 + n]

        def hT(ftile, t0, n):
            return poolQ[:, 512 * ftile + t0:512 * ftile + t0 + n]

        def qT_full(gp, ct):
            return poolQ[:, 4096 * gp + 512 * ct:4096 * gp + 512 * (ct + 1)]

        def qT_h(gp, h, j):
            base = 64 * (h % 2)
            return poolQ[base:base + 64,
                         4096 * gp + 512 * (h // 2) + 128 * j:
                         4096 * gp + 512 * (h // 2) + 128 * (j + 1)]

        def kT_full(gp, ct):
            return poolQ[:, 8192 + 4096 * gp + 512 * ct:8192 + 4096 * gp + 512 * (ct + 1)]

        def kT_h(gp, h, j):
            base = 64 * (h % 2)
            o = 8192 + 4096 * gp + 512 * (h // 2) + 128 * j
            return poolQ[base:base + 64, o:o + 128]

        def kTgrp(gp):
            return poolQ[:, 8192 + 4096 * gp:8192 + 4096 * (gp + 1)].rearrange(
                "p (ct t) -> p ct t", t=TG)

        def va_h(gp, j, h):
            return va[:, gp, j, 65 * h:65 * h + 65]

        def acc4(j):
            return accb[:, j // 2, 512 * (j % 2):512 * (j % 2) + 512]

        def w2slot(s):
            return w2s[:, 512 * s:512 * (s + 1)]

        def w2qv(par):
            return ktok[:, par, :]

        def xgh(j):
            return xg[:, j // 2, 512 * (j % 2):512 * (j % 2) + 512]

        def state_out(h):
            bank = psGH[h // 8]
            base = 64 * (h % 2)
            c0 = 65 * ((h // 2) % 4)
            return bank[base:base + 64, c0:c0 + 65], (h % 2 > 0)

        c_ident = p.dma("gpsimd", lambda g: g.dma_start(out=ident[:], in_=ident_in))
        p.dma("gpsimd", lambda g: g.dma_start(out=maskT[:], in_=maskT_in))
        c_pm = p.dma("gpsimd", lambda g: g.dma_start(out=pmask[:], in_=pmask_in))
        c_b1 = p.dma("gpsimd", lambda g: g.dma_start(out=b1col[:], in_=b1_in))
        c_b2 = p.dma("gpsimd", lambda g: g.dma_start(out=b2bc[0:1, :], in_=b2_in[None, :]))
        p.op("vector", lambda v: v.memset(ones_r[:], 1.0))
        m_ones = p.op("vector", lambda v: v.memset(onesc[:], 1.0), "V")
        p.op("vector", lambda v: v.memset(cvals[:, 0:1], 1e-5))
        p.op("vector", lambda v: v.memset(cvals[:, 1:2], -0.5))
        p.op("vector", lambda v: v.memset(cvals[:, 2:3], 0.5 / (DFF * DM)))
        cv_done = p.op("vector", lambda v: v.memset(cvals[:, 3:4], 0.5e-5), "V")
        v_zero = p.op("vector", lambda v: v.memset(pp[:, 0, :], 0.0), "V")

        p.waitm("tensor", c_b2)
        p.wait("tensor", "V", m_ones)
        p.op("tensor", lambda t: t.matmul(psC[:, 0:512], ones_r[:], b2bc[0:1, 0:512],
                                          start=True, stop=True))
        bm = p.op("tensor", lambda t: t.matmul(psD[:, 0:512], ones_r[:],
                                               b2bc[0:1, 512:1024],
                                               start=True, stop=True), "T")
        p.wait("vector", "T", bm)
        p.op("vector", lambda v: v.tensor_copy(b2bc[:, 0:512], psC[:, 0:512]))
        bv = p.op("vector", lambda v: v.tensor_copy(b2bc[:, 512:1024], psD[:, 0:512]),
                  "V")
        free[("psCD", 0)] = ("V", bv)
        free[("psCD", 1)] = ("V", bv)

        def prep_cast(w_ap, slot):
            cd = None
            for dt in range(8):
                par = dt % 2
                guard("sync", ("wst", par))
                d = p.dma("sync", lambda s, dt=dt, par=par, w_ap=w_ap: s.dma_start(
                    out=wst[:, par, :], in_=w_ap[128 * dt:128 * (dt + 1), :]))
                p.waitm("vector", d)
                cd = p.op("vector", lambda v, par=par, slot=slot, dt=dt: v.tensor_copy(
                    wT(slot, dt, 0, 1024), wst[:, par, :]), "V")
                free[("wst", par)] = ("V", cd)
            return cd

        prep_cast(wkT_in, 0)
        wprep_done = prep_cast(wvT_in, 1)

        gtiles = []
        for r in range(8):
            for fc2 in range(4):
                gtiles.append((W1T_in[128 * r:128 * (r + 1),
                                      1024 * fc2:1024 * (fc2 + 1)], 4 * r + fc2))
        for ft in range(32):
            gtiles.append((W2T_in[128 * ft:128 * (ft + 1), :], 32 + ft))

        spill_xnT = {}
        spill_kT = {}
        spill_va = {}
        snap_ev = {}
        xn_done = {}
        vvs = {}
        gq_d = {}

        def emit_greduce(i):
            ap_, col = gtiles[i]
            par_ = i % 2
            p.waitm("vector", gq_d[i])
            rr = p.op("vector", lambda e, par_=par_, col=col: e.reduce_sum(
                gacc[:, col:col + 1], wst[:, par_, :], axis=AX,
                apply_absolute_value=True), "V")
            free[("wst", par_)] = ("V", rr)

        p.wait("scalar", "V", v_zero)
        z0 = p.op("vector", lambda v: v.memset(snapst[:, 0, :], 0.0), "V")
        p.wait("scalar", "V", z0)
        sp0 = p.dma("scalar", lambda s: s.dma_start(out=snap_sp[0], in_=snapst[:, 0, :]))
        free[("snapst", 0)] = sp0

        for g in range(NG):
            gp = g % 2
            def emit_LN_B(bc):
                c, par, j = bc["c"], bc["par"], bc["j"]
                sl = 8 * par
                s1, s2 = sm[:, sl:sl + 1], sm[:, sl + 1:sl + 2]
                msq, var = sm[:, sl + 2:sl + 3], sm[:, sl + 3:sl + 4]
                lnv, rstd = sm[:, sl + 4:sl + 5], sm[:, sl + 5:sl + 6]
                nbias = sm[:, sl + 6:sl + 7]
                p.wait("vector", "V", bc["mqc"])
                p.wait("vector", "S", bc["sq"])
                vv = p.op("vector", lambda v, s2=s2, msq=msq, var=var:
                          v.scalar_tensor_tensor(var, s2, 1.0 / DM, msq,
                                                 ALU.mult, ALU.add), "V")
                vvs[c] = vv
                p.wait("scalar", "V", vv)
                p.wait("scalar", "V", cv_done)
                lnc = p.op("scalar", lambda s, var=var, lnv=lnv: s.activation(
                    lnv, var, AF.Ln, bias=cvals[:, 0:1]), "S")
                p.wait("scalar", "S", lnc)
                rs = p.op("scalar", lambda s, lnv=lnv, rstd=rstd: s.activation(
                    rstd, lnv, AF.Exp, scale=cvals[:, 1:2]), "S")
                p.wait("scalar", "S", rs)
                p.wait("vector", "S", rs)
                nb = p.op("vector", lambda v, s1=s1, rstd=rstd, nbias=nbias:
                          v.scalar_tensor_tensor(nbias, s1, -1.0 / DM, rstd,
                                                 ALU.mult, ALU.mult), "V")
                p.wait("scalar", "V", nb)
                guard("scalar", ("xnb", par))
                xn_done[c] = p.op("scalar", lambda s, par=par, rstd=rstd, nbias=nbias:
                                  s.activation(xnb[:, par, :], xg[:, par, :], AF.Identity,
                                               bias=nbias, scale=rstd), "S")
                free[("xg", par)] = ("S", xn_done[c])
                p.wait("tensor", "S", xn_done[c])
                p.waitm("tensor", c_ident)
                guard("tensor", ("xnTbuf", gp))
                tr = None
                for dt in range(8):
                    bank = psAB[dt % 2]
                    fslot = (dt // 2) % 4
                    guard("tensor", ("psAB", dt % 2))
                    tr = p.op("tensor", lambda t, bank=bank, fslot=fslot, dt=dt, par=par:
                              t.transpose(bank[:, 128 * fslot:128 * (fslot + 1)],
                                          xnb[:, par, 128 * dt:128 * (dt + 1)], ident[:]),
                              "T" if dt == 7 else None)
                free[("xnb", par)] = ("T", tr)
                p.wait("vector", "T", tr)
                guard("vector", ("xnTbuf", gp))
                for bi in range(2):
                    dst = xnT[:, gp, bi:8:2, 128 * j:128 * (j + 1)]
                    ev = p.op("vector", lambda v, bi=bi, dst=dst: v.tensor_copy(
                        dst, psAB[bi][:, 0:512]), "V")
                    free[("psAB", bi)] = ("V", ev)

            pendLN = None
            for j in range(JPG):
                c = JPG * g + j
                par = c % 2
                guard("sync", ("xg", par))
                d = p.dma("sync", lambda s, c=c, par=par: s.dma_start(
                    out=xg[:, par, :], in_=x_in[CH * c:CH * (c + 1), :]))
                p.waitm("vector", d)
                p.waitm("scalar", d)
                sl = 8 * par
                s1 = sm[:, sl:sl + 1]
                s2 = sm[:, sl + 1:sl + 2]
                msq = sm[:, sl + 2:sl + 3]
                r1c = p.op("vector", lambda v, par=par, s1=s1: v.reduce_sum(
                    s1, xg[:, par, :], axis=AX), "V")
                p.wait("vector", "V", r1c)
                guard("scalar", ("accb0",))
                if c - 2 in vvs:
                    p.wait("scalar", "V", vvs[c - 2])
                sq = p.op("scalar", lambda s, par=par, s2=s2: s.activation(
                    accb[:, par, :], xg[:, par, :], AF.Square, accum_out=s2), "S")
                mqc = p.op("vector", lambda v, s1=s1, msq=msq: v.scalar_tensor_tensor(
                    msq, s1, -1.0 / (DM * DM), s1, ALU.mult, ALU.mult), "V")
                ctxLN = {"c": c, "par": par, "j": j, "sq": sq, "mqc": mqc}
                if pendLN is not None:
                    emit_LN_B(pendLN)
                pendLN = ctxLN
            emit_LN_B(pendLN)
            xnT_done = p.now("V")
            p.wait("scalar", "V", xnT_done)
            spill_xnT[g] = p.dma("scalar", lambda s, g=g, gp=gp: s.dma_start(
                out=xnT_sp.rearrange("(dt p) T -> p dt T", p=CH)[:, :, TG * g:TG * (g + 1)],
                in_=xnT[:, gp, :, :].rearrange("p a b -> p (a b)")))

            p.wait("tensor", "V", xnT_done)
            p.wait("tensor", "V", wprep_done)
            for ct in range(8):
                bank = psCD[ct % 2]
                guard("tensor", ("psCD", ct % 2))
                guard("tensor", ("kTbuf", gp))
                for dt in range(8):
                    p.op("tensor", lambda t, bank=bank, ct=ct, dt=dt, gp=gp: t.matmul(
                        bank[:, 0:512], wT(0, dt, 128 * ct, 128), xnT[:, gp, dt, 0:512],
                        start=(dt == 0), stop=(dt == 7)), "T" if dt == 7 else None)
                mmT = p.now("T")
                ep = ct % 2
                p.wait("scalar", "T", mmT)
                guard("scalar", ("eexp", ep))
                ex = p.op("scalar", lambda s, bank=bank, ep=ep: s.activation(
                    eexp[:, ep, :], bank[:, 0:512], AF.Exp), "S")
                p.wait("vector", "S", ex)
                mnc = p.op("vector", lambda v, ep=ep: v.tensor_scalar(
                    eexp[:, ep, :], eexp[:, ep, :], 1.0, None, ALU.min), "V")
                p.wait("vector", "V", mnc)
                guard("vector", ("kTbuf", gp))
                kev = p.op("vector", lambda v, bank=bank, ep=ep, ct=ct, gp=gp:
                           v.scalar_tensor_tensor(kT_full(gp, ct), bank[:, 0:512], 0.0,
                                                  eexp[:, ep, :], ALU.max, ALU.add), "V")
                free[("eexp", ep)] = ("V", kev)
                free[("psCD", ct % 2)] = ("V", kev)
            kT_all = p.now("V")
            p.wait("scalar", "V", kT_all)
            spill_kT[g] = p.dma("scalar", lambda s, g=g, gp=gp: s.dma_start(
                out=kT_sp.rearrange("(ct p) T -> p ct T", p=CH)[:, :, TG * g:TG * (g + 1)],
                in_=kTgrp(gp)))

            guard("vector", ("vabuf", gp))
            on = p.op("vector", lambda v, gp=gp: v.memset(
                va[:, gp, :, :].rearrange("p j (h e) -> p j h e", e=D + 1)[:, :, :, D:D + 1],
                1.0), "V")

            for j in range(JPG):
                c = JPG * g + j
                par = c % 2
                guard("tensor", ("ktok", par))
                guard("tensor", ("psEF", 0))
                guard("tensor", ("psEF", 1))
                for dt in range(8):
                    for half in range(2):
                        p.op("tensor", lambda t, dt=dt, half=half, gp=gp, j=j: t.matmul(
                            psEF[half][:, 0:512], xnT[:, gp, dt, 128 * j:128 * (j + 1)],
                            wT(1, dt, 512 * half, 512), start=(dt == 0), stop=(dt == 7)),
                            "T" if (dt == 7 and half == 1) else None)
                kvT = p.now("T")
                p.wait("tensor", "V", kT_all)
                tr2 = None
                for ct in range(8):
                    bank = psAB[ct % 2]
                    fslot = (ct // 2) % 4
                    guard("tensor", ("psAB", ct % 2))
                    tr2 = p.op("tensor", lambda t, bank=bank, fslot=fslot, ct=ct,
                               gp=gp, j=j: t.transpose(
                                   bank[:, 128 * fslot:128 * (fslot + 1)],
                                   kT_full(gp, ct)[:, 128 * j:128 * (j + 1)], ident[:]),
                               "T" if ct == 7 else None)
                p.wait("vector", "T", tr2)
                for bi in range(2):
                    dst = ktok[:, par, :].rearrange("p (a b) -> p a b", b=128)[:, bi:8:2, :]
                    kv = p.op("vector", lambda v, bi=bi, dst=dst: v.tensor_copy(
                        dst, psAB[bi][:, 0:512]), "V")
                    free[("psAB", bi)] = ("V", kv)
                p.wait("vector", "T", kvT)
                p.wait("vector", "V", on)
                for half in range(2):
                    dst = va[:, gp, j, :].rearrange("p (h e) -> p h e", e=D + 1)[
                        :, 8 * half:8 * (half + 1), 0:D]
                    vv1 = p.op("vector", lambda v, half=half, dst=dst: v.tensor_copy(
                        dst, psEF[half][:, 0:512]), "V")
                    free[("psEF", half)] = ("V", vv1)
                ktv = p.now("V")

                p.wait("tensor", "V", ktv)
                if c > 0:
                    p.wait("tensor", "V", snap_ev[c - 1])
                if c == 0:
                    for bi in range(2):
                        p.op("tensor", lambda t, bi=bi: t.matmul(
                            psGH[bi][:, 500:501], ident[:], ident[:, 0:1],
                            start=True, stop=False, skip_group_check=True))
                for h in (0, 8, 1, 9, 2, 10, 3, 11, 4, 12, 5, 13, 6, 14, 7, 15):
                    o, up = state_out(h)
                    p.op("tensor", lambda t, h=h, o=o, up=up, par=par, gp=gp, j=j:
                         t.matmul(o, ktok[:, par, 64 * h:64 * (h + 1)], va_h(gp, j, h),
                                  start=False, stop=False, skip_group_check=True,
                                  tile_position=(0, 64) if up else None),
                         "T" if h == 15 else None)
                if c == NC - 1:
                    for bi in range(2):
                        p.op("tensor", lambda t, bi=bi: t.matmul(
                            psGH[bi][:, 500:501], ident[:], ident[:, 0:1],
                            start=False, stop=True, skip_group_check=True))
                st_done = p.now("T")
                free[("ktok", par)] = ("T", st_done)
                p.wait("vector", "T", st_done)
                spar = (c + 1) % 2
                guard("vector", ("snapst", spar))
                for bi in range(2):
                    p.op("vector", lambda v, bi=bi, spar=spar: v.tensor_copy(
                        snapst[:, spar, 260 * bi:260 * (bi + 1)], psGH[bi][:, 0:260]),
                        "V" if bi == 1 else None)
                snap_ev[c] = p.now("V")
                p.wait("scalar", "V", snap_ev[c])
                sd = p.dma("scalar", lambda s, c=c, spar=spar: s.dma_start(
                    out=snap_sp[c + 1], in_=snapst[:, spar, :]))
                free[("snapst", spar)] = sd
                pt = c
                if pt > 0:
                    for i in (2 * pt - 2, 2 * pt - 1):
                        emit_greduce(i)
                for i in (2 * pt, 2 * pt + 1):
                    ap_, _col = gtiles[i]
                    par_ = i % 2
                    qeng = "scalar" if par_ == 0 else "gpsimd"
                    guard(qeng, ("wst", par_))
                    gq_d[i] = p.dma(qeng, lambda s, ap_=ap_, par_=par_: s.dma_start(
                        out=wst[:, par_, :], in_=ap_))
            p.wait("scalar", "V", p.now("V"))
            spill_va[g] = p.dma("scalar", lambda s, g=g, gp=gp: s.dma_start(
                out=va_sp[TG * g:TG * (g + 1), :].rearrange("(j p) n -> p j n", p=CH),
                in_=va[:, gp, :, :]))
            free[("vabuf", gp)] = spill_va[g]
            free[("xnTbuf", gp)] = spill_xnT[g]
            free[("kTbuf", gp)] = spill_kT[g]

        for i in (62, 63):
            emit_greduce(i)

        p1T = p.now("T")

        p.wait("vector", "T", p1T)
        for bi in range(2):
            p.op("vector", lambda v, bi=bi: v.tensor_copy(
                agst[:, 260 * bi:260 * (bi + 1)], psGH[bi][:, 0:260]),
                "V" if bi == 1 else None)
        ag_ready = p.now("V")
        p.wait("gpsimd", "V", ag_ready)
        dg = p.dma("gpsimd", lambda g: g.dma_start(out=ag_in[:], in_=agst[:]))
        p.waitm("gpsimd", dg)
        p.op("gpsimd", lambda g: g.collective_compute(
            "AllGather", ALU.bypass, replica_groups=[list(range(8))],
            ins=[ag_in_t.ap().opt()], outs=[ag_out_t.ap().opt()]), "CC")

        g1, g2 = gacc[:, 64:65], gacc[:, 65:66]
        p.wait("vector", "V", p.now("V"))
        p.wait("vector", "G", p.now("G"))
        gv1 = p.op("vector", lambda v: v.reduce_sum(g1, gacc[:, 0:32], axis=AX), "V")
        gv = p.op("vector", lambda v: v.reduce_sum(g2, gacc[:, 32:64], axis=AX), "V")
        p.wait("tensor", "V", gv)
        p.wait("tensor", "V", ag_ready)
        p.op("tensor", lambda t: t.matmul(psG[0:1, 0:1], g1, onesc[:],
                                          start=True, stop=True))
        tg2 = p.op("tensor", lambda t: t.matmul(psG[0:1, 1:2], g2, onesc[:],
                                                start=True, stop=True), "T")
        p.wait("scalar", "T", tg2)
        p.op("scalar", lambda s: s.activation(
            gacc[0:1, 66:67], psG[0:1, 0:1], AF.Identity, scale=cvals[0:1, 2:3],
            bias=cvals[0:1, 3:4]))
        t2 = p.op("scalar", lambda s: s.activation(
            gacc[0:1, 67:68], psG[0:1, 1:2], AF.Identity, scale=cvals[0:1, 2:3],
            bias=cvals[0:1, 3:4]),
            "S")
        p.wait("tensor", "S", t2)
        p.op("tensor", lambda t: t.matmul(psG[:, 2:3], ones_r[:], gacc[0:1, 66:67],
                                          start=True, stop=True))
        bc = p.op("tensor", lambda t: t.matmul(psG[:, 3:4], ones_r[:], gacc[0:1, 67:68],
                                               start=True, stop=True), "T")
        p.wait("vector", "T", bc)
        p.op("vector", lambda v: v.tensor_copy(thrs[:, 0:1], psG[:, 2:3]))
        p.op("vector", lambda v: v.tensor_scalar(thrs[:, 1:2], psG[:, 2:3], -1.0, None,
                                                 ALU.mult))
        p.op("vector", lambda v: v.tensor_copy(thrs[:, 2:3], psG[:, 3:4]))
        thr_done = p.op("vector", lambda v: v.tensor_scalar(
            thrs[:, 3:4], psG[:, 3:4], -1.0, None, ALU.mult), "V")

        p.wait("sync", "T", p1T)
        p.waitm("sync", spill_xnT[0])
        p.waitm("sync", spill_xnT[NG - 2])
        dxs = {}
        dxs[0] = p.dma("sync", lambda s: s.dma_start(
            out=xnT[:, 0, :, :].rearrange("p a b -> p (a b)"),
            in_=xnT_sp.rearrange("(dt p) T -> p dt T", p=CH)[:, :, 0:TG]))

        p.wait("vector", "V", thr_done)
        w1bufs = [wst[:, 0, :], wst[:, 1, :], xg[:, 0, :], xg[:, 1, :]]
        w1keys = [("wst", 0), ("wst", 1), ("xgq", 0), ("xgq", 1)]
        w1fc = {}
        for i in range(32):
            fc2, r = i // 8, i % 8
            b4 = i % 4
            buf = w1bufs[b4]
            qe1 = ("sync", "scalar", "gpsimd")[i % 3]
            guard(qe1, w1keys[b4])
            d = p.dma(qe1, lambda s, r=r, fc2=fc2, buf=buf: s.dma_start(
                out=buf,
                in_=W1T_in[128 * r:128 * (r + 1), 1024 * fc2:1024 * (fc2 + 1)]))
            p.waitm("vector", d)
            guard("vector", ("accb1",))
            q1c = p.op("vector", lambda v, buf=buf: v.tensor_scalar(
                accb[:, 1, :], buf, thrs[:, 1:2], -1.0, ALU.is_lt,
                ALU.mult), "V")
            p.wait("vector", "V", q1c)
            qz = p.op("vector", lambda v, buf=buf, r=r, fc2=fc2: v.scalar_tensor_tensor(
                w1qt(r, 1024 * fc2, 1024), buf, thrs[:, 0:1], accb[:, 1, :],
                ALU.is_gt, ALU.add), "V")
            free[w1keys[b4]] = ("V", qz)
            free[("accb1",)] = ("V", qz)
            if r == 7:
                w1fc[fc2] = p.now("V")
        w1_ready = p.now("V")

        w2_spill_last = None
        w2sp = {}
        for ft in range(32):
            par = ft % 2
            b4 = ft % 4
            buf = (xg[:, 0, :], xg[:, 1, :], wst[:, 0, :], wst[:, 1, :])[b4]
            bkey = (("xgq", 0), ("xgq", 1), ("wst", 0), ("wst", 1))[b4]
            qe2 = ("gpsimd", "sync", "scalar")[ft % 3]
            guard(qe2, bkey)
            d = p.dma(qe2, lambda s, ft=ft, buf=buf: s.dma_start(
                out=buf, in_=W2T_in[128 * ft:128 * (ft + 1), :]))
            p.waitm("vector", d)
            guard("vector", ("accbq",))
            q1c = p.op("vector", lambda e, buf=buf: e.tensor_scalar(
                accb[:, 0, :], buf, thrs[:, 3:4], -1.0, ALU.is_lt,
                ALU.mult), "V")
            p.wait("vector", "V", q1c)
            guard("vector", ("w2q", par))
            qz = p.op("vector", lambda e, buf=buf, par=par: e.scalar_tensor_tensor(
                w2qv(par), buf, thrs[:, 2:3], accb[:, 0, :],
                ALU.is_gt, ALU.add), "V")
            free[bkey] = ("V", qz)
            free[("accbq",)] = ("V", qz)
            qe3 = ("scalar", "gpsimd", "sync")[ft % 3]
            p.wait(qe3, "V", qz)
            w2_spill_last = p.dma(qe3, lambda s, ft=ft, par=par: s.dma_start(
                out=w2qt_sp[128 * ft:128 * (ft + 1), :], in_=w2qv(par)))
            w2sp[ft] = w2_spill_last
            free[("w2q", par)] = w2_spill_last
        w2q_done = ("V", p.now("V"))
        free[("ybuf", 0)] = free.get(("w2q", 0))
        free[("ybuf", 1)] = free.get(("w2q", 1))
        for j in range(4):
            free[("xgh", j)] = w2q_done
        free[("accs", 0)] = w2q_done
        free[("accs", 1)] = w2q_done
        free[("accs", 2)] = ("V", w1_ready)
        free[("accs", 3)] = ("V", w1_ready)
        free[("accb0",)] = w2q_done

        w2i = 0
        for g in range(NG):
            gp = g % 2
            p.waitm("tensor", dxs[g])
            if g == 0:
                p.waitm("scalar", spill_kT[NG - 1])
            hvs = {}
            for ftile in range(32):
                bank = psCD[ftile % 2]
                if g == 0:
                    p.wait("tensor", "V", w1fc[ftile // 8])
                guard("tensor", ("psCD", ftile % 2))
                for dt in range(8):
                    p.op("tensor", lambda t, bank=bank, ftile=ftile, dt=dt, gp=gp: t.matmul(
                        bank[:, 0:512], w1qt(dt, 128 * ftile, 128), xnT[:, gp, dt, 0:512],
                        start=(dt == 0), stop=(dt == 7)), "T" if dt == 7 else None)
                mmn = p.now("T")
                p.wait("scalar", "T", mmn)
                hv = p.op("scalar", lambda s, bank=bank, ftile=ftile: s.activation(
                    hT(ftile, 0, 512), bank[:, 0:512], AF.Relu,
                    bias=b1col[:, ftile:ftile + 1]), "S")
                hvs[ftile] = hv
                free[("psCD", ftile % 2)] = ("S", hv)
            free[("xnTbuf2", gp)] = ("T", p.now("T"))
            if g + 1 < NG:
                p.waitm("sync", spill_xnT[g + 1])
                guard("sync", ("xnTbuf2", (g + 1) % 2))
                dxs[g + 1] = p.dma("sync", lambda s, g=g: s.dma_start(
                    out=xnT[:, (g + 1) % 2, :, :].rearrange("p a b -> p (a b)"),
                    in_=xnT_sp.rearrange("(dt p) T -> p dt T", p=CH)[
                        :, :, TG * (g + 1):TG * (g + 2)]))

            for dmh in range(2):
                pb = [psE, psF, psG, psH] if dmh == 0 else [psC, psD, psE, psF]
                pbi = [2, 3, 4, 5] if dmh == 0 else [0, 1, 2, 3]
                if dmh == 1:
                    free[("o2b", 0)] = free.get(("psCD", 0))
                    free[("o2b", 1)] = free.get(("psCD", 1))
                dxxs = {}
                for j in range(JPG):
                    c = JPG * g + j
                    guard("scalar", ("xgh", j))
                    dxxs[j] = p.dma("scalar", lambda s, c=c, j=j, dmh=dmh: s.dma_start(
                        out=xgh(j), in_=x_in[CH * c:CH * (c + 1),
                                             512 * dmh:512 * (dmh + 1)]))
                for ft in range(32):
                    slot = w2i % NW2
                    qeng = "sync" if w2i % 2 == 0 else "gpsimd"
                    w2i += 1
                    guard(qeng, ("w2s", slot))
                    if g == 0 and dmh == 0:
                        p.waitm(qeng, w2sp[ft])
                    dw = p.dma(qeng, lambda s, ft=ft, dmh=dmh, slot=slot: s.dma_start(
                        out=w2slot(slot),
                        in_=w2qt_sp[128 * ft:128 * (ft + 1),
                                    512 * dmh:512 * (dmh + 1)]))
                    p.waitm("tensor", dw)
                    p.wait("tensor", "S", hvs[ft])
                    lm = None
                    for j in range(JPG):
                        if ft == 0:
                            guard("tensor", ("o2b", pbi[j]))
                        lm = p.op("tensor", lambda t, ft=ft, j=j, slot=slot, pb=pb: t.matmul(
                            pb[j][:, 0:512], hT(ft, 128 * j, 128), w2slot(slot),
                            start=(ft == 0), stop=(ft == 31)),
                            "T" if j == 3 else None)
                    free[("w2s", slot)] = ("T", lm)
                mm2p = p.now("T")
                for j in range(JPG):
                    c = JPG * g + j
                    p.waitm("vector", dxxs[j])
                    p.wait("vector", "T", mm2p)
                    guard("vector", ("accs", j))
                    a1 = p.op("vector", lambda v, j=j, dmh=dmh, pb=pb: v.tensor_tensor(
                        acc4(j), pb[j][:, 0:512],
                        b2bc[:, 512 * dmh:512 * (dmh + 1)], ALU.add), "V")
                    p.wait("vector", "V", a1)
                    a2 = p.op("vector", lambda v, j=j: v.tensor_tensor(
                        acc4(j), acc4(j), xgh(j), ALU.add), "V")
                    free[("o2b", pbi[j])] = ("V", a2)
                    free[("xgh", j)] = ("V", a2)
                    p.wait("scalar", "V", a2)
                    sa = p.dma("scalar", lambda s, c=c, j=j, dmh=dmh: s.dma_start(
                        out=acc_sp[CH * c:CH * (c + 1), 512 * dmh:512 * (dmh + 1)],
                        in_=acc4(j)))
                    free[("accs", j)] = sa
                if dmh == 1:
                    free[("psCD", 0)] = free[("o2b", 0)]
                    free[("psCD", 1)] = free[("o2b", 1)]
        ffn_T = p.now("T")
        free[("psGH", 0)] = free.get(("o2b", 4))
        free[("psGH", 1)] = free.get(("o2b", 5))
        for b_ in range(4):
            free[("nbank", b_)] = free.get(("o2b", b_))
            free[("sslot", b_)] = free.get(("o2b", b_))

        p.wait("tensor", "V", p.now("V"))
        p.wait("tensor", "S", p.now("S"))
        p.wait("vector", "T", ffn_T)
        prep_cast(wqT_in, 2)
        wqp_done = prep_cast(wpT_in, 3)
        p.wait("sync", "T", ffn_T)
        p.dma_barrier("sync")

        p.wait("gpsimd", "CC", 1)
        for r in range(8):
            guard("gpsimd", ("gr", r % 2))
            dgr = p.dma("gpsimd", lambda g, r=r: g.dma_start(
                out=gr[:, r % 2, :], in_=ag_out[CH * r:CH * (r + 1), :]))
            p.waitm("vector", dgr)
            p.wait("vector", "V", v_zero)
            p.waitm("vector", c_pm)
            cmb = p.op("vector", lambda v, r=r: v.scalar_tensor_tensor(
                pp[:, (r + 1) % 2, :], gr[:, r % 2, :], pmask[:, r:r + 1],
                pp[:, r % 2, :], ALU.mult, ALU.add), "V")
            p.wait("vector", "V", cmb)
            free[("gr", r % 2)] = ("V", cmb)
        prefix_done = p.op("vector", lambda v: v.tensor_copy(spre[:], pp[:, 0, :]), "V")

        y_of = {}

        for g in range(NG):
            gp = g % 2
            guard("sync", ("p2xnT", gp))
            guard("sync", ("p2kT", gp))
            guard("sync", ("p2va", gp))
            p.waitm("sync", spill_kT[g])
            p.waitm("sync", spill_va[g])
            dq = p.dma("sync", lambda s, g=g, gp=gp: s.dma_start(
                out=xnT[:, gp, :, :].rearrange("p a b -> p (a b)"),
                in_=xnT_sp.rearrange("(dt p) T -> p dt T", p=CH)[:, :, TG * g:TG * (g + 1)]))
            dk = p.dma("sync", lambda s, g=g, gp=gp: s.dma_start(
                out=kTgrp(gp),
                in_=kT_sp.rearrange("(ct p) T -> p ct T", p=CH)[:, :, TG * g:TG * (g + 1)]))
            dv = p.dma("sync", lambda s, g=g, gp=gp: s.dma_start(
                out=va[:, gp, :, :],
                in_=va_sp[TG * g:TG * (g + 1), :].rearrange("(j p) n -> p j n", p=CH)))
            p.waitm("tensor", dq)
            p.wait("tensor", "V", wqp_done)
            if g == 0:
                p.wait("tensor", "T", ffn_T)
            for ct in range(8):
                bank = psGH[ct % 2]
                guard("tensor", ("psGH", ct % 2))
                guard("tensor", ("p2qT", gp))
                for dt in range(8):
                    p.op("tensor", lambda t, bank=bank, ct=ct, dt=dt, gp=gp: t.matmul(
                        bank[:, 0:512], wT(2, dt, 128 * ct, 128), xnT[:, gp, dt, 0:512],
                        start=(dt == 0), stop=(dt == 7)), "T" if dt == 7 else None)
                mmq = p.now("T")
                ep = ct % 2
                p.wait("scalar", "T", mmq)
                guard("scalar", ("eexp", ep))
                ex = p.op("scalar", lambda s, bank=bank, ep=ep: s.activation(
                    eexp[:, ep, :], bank[:, 0:512], AF.Exp), "S")
                p.wait("vector", "S", ex)
                mnc = p.op("vector", lambda v, ep=ep: v.tensor_scalar(
                    eexp[:, ep, :], eexp[:, ep, :], 1.0, None, ALU.min), "V")
                p.wait("vector", "V", mnc)
                qe = p.op("vector", lambda v, bank=bank, ep=ep, ct=ct, gp=gp:
                          v.scalar_tensor_tensor(qT_full(gp, ct), bank[:, 0:512], 0.0,
                                                 eexp[:, ep, :], ALU.max, ALU.add), "V")
                free[("eexp", ep)] = ("V", qe)
                free[("psGH", ct % 2)] = ("V", qe)
            q_done = p.now("V")

            for j in range(JPG):
                c = JPG * g + j
                par = c % 2
                guard("sync", ("snld", par))
                dsn = p.dma("sync", lambda s, c=c, par=par: s.dma_start(
                    out=snld[:, par, :], in_=snap_sp[c]))
                guard("sync", ("accbr", par))
                guard("sync", ("accbr2", par))
                dac = p.dma("sync", lambda s, c=c, par=par: s.dma_start(
                    out=accb[:, par, :], in_=acc_sp[CH * c:CH * (c + 1), :]))
                p.waitm("vector", dsn)
                p.wait("vector", "V", prefix_done)
                guard("vector", ("saugT", par))
                sg = p.op("vector", lambda v, par=par: v.tensor_tensor(
                    saugT[:, par, :], snld[:, par, :], spre[:], ALU.add), "V")
                free[("snld", par)] = ("V", sg)

                p.waitm("tensor", dk)
                p.waitm("tensor", dv)
                p.wait("tensor", "V", q_done)
                p.wait("tensor", "V", sg)
                num_last = None
                for w in range(4):
                    scm = None
                    for b in range(4):
                        h = 4 * w + b
                        guard("tensor", ("sslot", b))
                        scm = p.op("tensor", lambda t, b=b, h=h, j=j, gp=gp:
                                   t.matmul(ps4[b][:, 260:388], kT_h(gp, h, j),
                                            qT_h(gp, h, j), start=True, stop=True),
                                   "T" if b == 3 else None)
                    p.wait("vector", "T", scm)
                    msv = None
                    for b in range(4):
                        h = 4 * w + b
                        guard("vector", ("mS", h))
                        msv = p.op("vector", lambda v, b=b, h=h, par=par:
                                   v.tensor_tensor(mS[:, par, h, :], ps4[b][:, 260:388],
                                                   maskT[:], ALU.mult),
                                   "V" if b == 3 else None)
                    for b in range(4):
                        free[("sslot", b)] = ("V", msv)
                    p.wait("tensor", "V", msv)
                    for b in range(4):
                        h = 4 * w + b
                        guard("tensor", ("nbank", b))
                        guard("tensor", ("nbankv", b))
                        o = ps4[b][:, 65 * w:65 * w + 65]
                        sb = 64 * (h % 2)
                        p.op("tensor", lambda t, o=o, h=h, j=j, gp=gp, sb=sb, par=par:
                             t.matmul(o, qT_h(gp, h, j),
                                      saugT[sb:sb + 64, par,
                                            65 * (h // 2):65 * (h // 2) + 65],
                                      start=True, stop=False))
                        num_last = p.op("tensor", lambda t, o=o, h=h, par=par, gp=gp, j=j:
                                        t.matmul(o, mS[:, par, h, :], va_h(gp, j, h),
                                                 start=False, stop=True),
                                        "T" if b == 3 else None)
                        free[("mS", h)] = ("T", p.now("T"))
                p.wait("vector", "T", num_last)
                guard("vector", ("recb", par))
                guard("vector", ("recbs", par))
                dxc = None
                for b in range(4):
                    dxc = p.op("vector", lambda v, b=b, par=par: v.tensor_scalar(
                        recb[:, par, 4 * b:4 * b + 4].rearrange("p (s o) -> p s o", o=1),
                        ps4[b][:, 64:324].rearrange(
                            "p (s o) -> p s o", o=65)[:, :, 0:1],
                        1e-6, None, ALU.add), "V" if b == 3 else None)
                p.wait("vector", "V", dxc)
                rcd = p.op("vector", lambda v, par=par: v.reciprocal(
                    recb[:, par, :], recb[:, par, :]), "V")
                p.wait("vector", "V", rcd)
                p.wait("scalar", "V", rcd)
                guard("scalar", ("ybuf", par))
                guard("vector", ("ybuf", par))
                pysc = {}
                for hp in range(8):
                    yeng = "scalar" if hp % 2 == 0 else "vector"
                    for hh in range(2):
                        h = 2 * hp + hh
                        bank = ps4[h % 4]
                        s_ = h // 4
                        if yeng == "scalar":
                            p.op("scalar", lambda s, bank=bank, h=h, s_=s_, par=par:
                                 s.activation(ktok[:, par, 64 * h:64 * (h + 1)],
                                              bank[:, 65 * s_:65 * s_ + 64],
                                              AF.Identity,
                                              scale=recb[:, par, 4 * (h % 4) + s_:
                                                         4 * (h % 4) + s_ + 1]),
                                 "S")
                        else:
                            p.op("vector", lambda v, bank=bank, h=h, s_=s_, par=par:
                                 v.tensor_scalar(ktok[:, par, 64 * h:64 * (h + 1)],
                                                 bank[:, 65 * s_:65 * s_ + 64],
                                                 recb[:, par, 4 * (h % 4) + s_:
                                                      4 * (h % 4) + s_ + 1],
                                                 None, ALU.mult), "V")
                    pysc[hp] = ("S" if yeng == "scalar" else "V",
                                p.now("S" if yeng == "scalar" else "V"))
                ysc_n = p.now("S")
                yscv_n = p.now("V")
                for b in range(4):
                    free[("nbank", b)] = ("S", ysc_n)
                    free[("nbankv", b)] = ("V", yscv_n)
                free[("recb", par)] = ("V", yscv_n)
                free[("recbs", par)] = ("S", ysc_n)
                tr = None
                for et in range(8):
                    p.wait("tensor", pysc[et][0], pysc[et][1])
                    bank = psAB[et % 2]
                    fslot = (et // 2) % 4
                    guard("tensor", ("psAB", et % 2))
                    tr = p.op("tensor", lambda t, bank=bank, fslot=fslot, et=et, par=par:
                              t.transpose(bank[:, 128 * fslot:128 * (fslot + 1)],
                                          ktok[:, par, 128 * et:128 * (et + 1)], ident[:]),
                              "T" if et == 7 else None)
                free[("ybuf", par)] = ("T", tr)
                p.wait("vector", "T", tr)
                for bi in range(2):
                    dst = yTb[:, par, bi:8:2, :]
                    ev = p.op("vector", lambda v, bi=bi, dst=dst: v.tensor_copy(
                        dst, psAB[bi][:, 0:512]), "V")
                    free[("psAB", bi)] = ("V", ev)
                yT_done = p.now("V")
                p.wait("tensor", "V", yT_done)
                guard("tensor", ("psGH", 0))
                guard("tensor", ("psGH", 1))
                prj = None
                for et in range(8):
                    for eh in range(2):
                        prj = p.op("tensor", lambda t, et=et, eh=eh, par=par: t.matmul(
                            psGH[eh][:, 0:512], yTb[:, par, et, :],
                            wT(3, et, 512 * eh, 512),
                            start=(et == 0), stop=(et == 7)),
                            "T" if (et == 7 and eh == 1) else None)
                free[("yTb", par)] = ("T", prj)
                p.wait("vector", "T", prj)
                p.waitm("vector", dac)
                guard("vector", ("accbr2", par))
                for eh in range(2):
                    p.op("vector", lambda v, eh=eh, par=par: v.tensor_tensor(
                        accb[:, par, 512 * eh:512 * (eh + 1)], psGH[eh][:, 0:512],
                        accb[:, par, 512 * eh:512 * (eh + 1)], ALU.add),
                        "V" if eh == 1 else None)
                fout = p.now("V")
                free[("psGH", 0)] = ("V", fout)
                free[("psGH", 1)] = ("V", fout)
                p.wait("scalar", "V", fout)
                so = p.dma("scalar", lambda s, c=c, par=par: s.dma_start(
                    out=out_d[CH * c:CH * (c + 1), :], in_=accb[:, par, :]))
                free[("accbr", par)] = ("V", fout)
                free[("accbr2", par)] = so
                free[("saugT", par)] = ("T", p.now("T"))
            free[("p2qT", gp)] = ("T", p.now("T"))
            free[("p2xnT", gp)] = ("T", p.now("T"))
            free[("p2kT", gp)] = ("T", p.now("T"))
            free[("p2va", gp)] = ("T", p.now("T"))

        p.dma_barrier("gpsimd")

        with nc.Block() as block:
            @block.sync
            def _(sync):
                for fn in p.q["sync"]:
                    fn(sync)

            @block.vector
            def _(vector):
                for fn in p.q["vector"]:
                    fn(vector)

            @block.scalar
            def _(scalar):
                for fn in p.q["scalar"]:
                    fn(scalar)

            @block.tensor
            def _(tensor):
                for fn in p.q["tensor"]:
                    fn(tensor)

            @block.gpsimd
            def _(gpsimd):
                for fn in p.q["gpsimd"]:
                    fn(gpsimd)

    return nc


def make_inputs(c, x_full, w_attn, w_proj, W1, b1, W2, b2, span=4096):
    b, g = divmod(c, 4)
    m = np.zeros((CH, 8), np.float32)
    for j in range(8):
        if j // 4 == b and j % 4 < g:
            m[:, j] = 1.0
    maskT = np.tril(np.ones((CH, CH), np.float32)).T.copy()
    return {
        "x": np.ascontiguousarray(x_full[b, g * span:(g + 1) * span]),
        "wqT": np.ascontiguousarray(w_attn[0:DM].T),
        "wkT": np.ascontiguousarray(w_attn[DM:2 * DM].T),
        "wvT": np.ascontiguousarray(w_attn[2 * DM:3 * DM].T),
        "wpT": np.ascontiguousarray(w_proj.T),
        "W1T": np.ascontiguousarray(W1.T),
        "b1": np.ascontiguousarray(b1.reshape(32, CH).T),
        "W2T": np.ascontiguousarray(W2.T),
        "b2": np.ascontiguousarray(b2),
        "ident": np.eye(CH, dtype=np.float32),
        "maskT": maskT,
        "pmask": m,
    }


def kernel(L0_attn_x, L0_attn_w_attn, L0_attn_w_proj, L0_ffn_W1, L0_ffn_b1,
           L0_ffn_W2, L0_ffn_b2):
    x = np.asarray(L0_attn_x, dtype=np.float32)
    w_attn = np.asarray(L0_attn_w_attn, dtype=np.float32)
    w_proj = np.asarray(L0_attn_w_proj, dtype=np.float32)
    W1 = np.asarray(L0_ffn_W1, dtype=np.float32)
    b1 = np.asarray(L0_ffn_b1, dtype=np.float32)
    W2 = np.asarray(L0_ffn_W2, dtype=np.float32)
    b2 = np.asarray(L0_ffn_b2, dtype=np.float32)
    span = 4096
    nc = build(span)
    in_maps = [make_inputs(c, x, w_attn, w_proj, W1, b1, W2, b2, span)
               for c in range(8)]
    res = run_bass_kernel_spmd(nc, in_maps, core_ids=list(range(8)))
    out = np.empty((2, 16384, DM), np.float32)
    for c in range(8):
        b, g = divmod(c, 4)
        out[b, g * span:(g + 1) * span] = res.results[c]["out"]
    return out


# version 26
# speedup vs baseline: 1.0990x; 1.0990x over previous
import contextlib
import numpy as np
import concourse.bass as bass
import concourse.mybir as mybir
from concourse.bass_utils import run_bass_kernel_spmd

F32 = mybir.dt.float32
BF16 = mybir.dt.bfloat16
AF = mybir.ActivationFunctionType
ALU = mybir.AluOpType
AX = mybir.AxisListType.X

DM = 1024
H = 16
D = 64
CH = 128
TG = 512
DFF = 4096
HB = 8 * (D + 1)
NW2 = 6


class Plan:
    ENGS = ("sync", "vector", "scalar", "tensor", "gpsimd")
    QKEY = {"sync": "DS", "scalar": "DA", "gpsimd": "DG"}
    KDMA = 6

    def __init__(self):
        self.q = {e: [] for e in self.ENGS}
        self.cnt = {}
        self.sems = {}
        self.waited = {e: {} for e in self.ENGS}
        self.dmai = {"DS": 0, "DA": 0, "DG": 0}

    def op(self, eng, fn, inc=None, inc_n=1):
        if inc is None:
            self.q[eng].append(fn)
            return None
        self.cnt[inc] = self.cnt.get(inc, 0) + inc_n
        sem = self.sems[inc]

        def wrapped(e, fn=fn, sem=sem, n=inc_n):
            fn(e).then_inc(sem, n)

        self.q[eng].append(wrapped)
        return self.cnt[inc]

    def dma(self, eng, fn):
        qk = self.QKEY[eng]
        i = self.dmai[qk]
        self.dmai[qk] = i + 1
        name = f"{qk}{i % self.KDMA}"
        prev = 16 * (i // self.KDMA)
        if prev:
            self.wait(eng, name, prev)
        self.cnt[name] = self.cnt.get(name, 0) + 16
        sem = self.sems[name]

        def wrapped(e, fn=fn, sem=sem):
            fn(e).then_inc(sem, 16)

        self.q[eng].append(wrapped)
        return (name, self.cnt[name])

    def dma_barrier(self, eng):
        ms = []
        for qk in ("DS", "DA", "DG"):
            for j in range(self.KDMA):
                name = f"{qk}{j}"
                v = self.cnt.get(name, 0)
                if v:
                    self.wait(eng, name, v)
                    ms.append((name, v))
        return ms

    def wait(self, eng, sem_name, val):
        if not val:
            return
        if self.waited[eng].get(sem_name, 0) >= val:
            return
        self.waited[eng][sem_name] = val
        sem = self.sems[sem_name]

        def w(e, sem=sem, v=val):
            e.wait_ge(sem, v)

        self.q[eng].append(w)

    def waitm(self, eng, m):
        if m is not None:
            self.wait(eng, m[0], m[1])

    def now(self, sem_name):
        return self.cnt.get(sem_name, 0)


def build(span=4096):
    NG = span // TG
    NC = span // CH
    JPG = TG // CH
    nc = bass.Bass("TRN2", target_bir_lowering=False, debug=False)

    x_in = nc.dram_tensor("x", [span, DM], F32, kind="ExternalInput").ap()
    wqT_in = nc.dram_tensor("wqT", [DM, DM], F32, kind="ExternalInput").ap()
    wkT_in = nc.dram_tensor("wkT", [DM, DM], F32, kind="ExternalInput").ap()
    wvT_in = nc.dram_tensor("wvT", [DM, DM], F32, kind="ExternalInput").ap()
    wpT_in = nc.dram_tensor("wpT", [DM, DM], F32, kind="ExternalInput").ap()
    W1T_in = nc.dram_tensor("W1T", [DM, DFF], F32, kind="ExternalInput").ap()
    b1_in = nc.dram_tensor("b1", [CH, 32], F32, kind="ExternalInput").ap()
    W2T_in = nc.dram_tensor("W2T", [DFF, DM], F32, kind="ExternalInput").ap()
    b2_in = nc.dram_tensor("b2", [DM], F32, kind="ExternalInput").ap()
    ident_in = nc.dram_tensor("ident", [CH, CH], F32, kind="ExternalInput").ap()
    maskT_in = nc.dram_tensor("maskT", [CH, CH], F32, kind="ExternalInput").ap()
    pmask_in = nc.dram_tensor("pmask", [CH, 8], F32, kind="ExternalInput").ap()
    out_d = nc.dram_tensor("out", [span, DM], F32, kind="ExternalOutput").ap()

    xnT_sp = nc.dram_tensor("xnT_sp", [DM, span], BF16).ap()
    kT_sp = nc.dram_tensor("kT_sp", [DM, span], BF16).ap()
    va_sp = nc.dram_tensor("va_sp", [span, H * (D + 1)], BF16).ap()
    snap_sp = nc.dram_tensor("snap_sp", [NC + 1, CH, HB], BF16).ap()
    acc_sp = nc.dram_tensor("acc_sp", [span, DM], F32).ap()
    w2qt_sp = nc.dram_tensor("w2qt_sp", [DFF, DM], BF16).ap()
    ag_in_t = nc.dram_tensor("ag_in", [CH, HB], F32)
    ag_out_t = nc.dram_tensor("ag_out", [CH * 8, HB], F32)
    ag_in = ag_in_t.ap()
    ag_out = ag_out_t.ap()

    with contextlib.ExitStack() as ctx:
        E = ctx.enter_context
        ident = E(nc.sbuf_tensor("ident_sb", [CH, CH], BF16))
        maskT = E(nc.sbuf_tensor("maskT_sb", [CH, CH], BF16))
        pmask = E(nc.sbuf_tensor("pmask_sb", [CH, 8], F32))
        ones_r = E(nc.sbuf_tensor("ones_r", [1, CH], F32))
        onesc = E(nc.sbuf_tensor("onesc", [CH, 1], F32))
        sm = E(nc.sbuf_tensor("sm", [CH, 16], F32))
        gacc = E(nc.sbuf_tensor("gacc", [CH, 72], F32))
        thrs = E(nc.sbuf_tensor("thrs", [CH, 4], F32))
        recb = E(nc.sbuf_tensor("recb", [CH, 2, 16], F32))
        cvals = E(nc.sbuf_tensor("cvals", [CH, 4], F32))
        b1col = E(nc.sbuf_tensor("b1col", [CH, 32], F32))
        b2bc = E(nc.sbuf_tensor("b2bc", [CH, DM], F32))
        xg = E(nc.sbuf_tensor("xg", [CH, 2, DM], F32))
        xnb = E(nc.sbuf_tensor("xnb", [CH, 2, DM], BF16))
        xnT = E(nc.sbuf_tensor("xnT", [CH, 2, 8, TG], BF16))
        va = E(nc.sbuf_tensor("va", [CH, 2, JPG, H * (D + 1)], BF16))
        ktok = E(nc.sbuf_tensor("ktok", [CH, 2, DM], BF16))
        poolP = E(nc.sbuf_tensor("poolP", [CH, 32768], BF16))
        poolQ = E(nc.sbuf_tensor("poolQ", [CH, 16384], BF16))
        w2s = E(nc.sbuf_tensor("w2s", [CH, NW2 * TG], BF16))
        wst = E(nc.sbuf_tensor("wst", [CH, 2, DM], F32))
        snapst = E(nc.sbuf_tensor("snapst", [CH, 2, HB], BF16))
        snld = E(nc.sbuf_tensor("snld", [CH, 2, HB], BF16))
        agst = E(nc.sbuf_tensor("agst", [CH, HB], F32))
        gr = E(nc.sbuf_tensor("gr", [CH, 2, HB], F32))
        pp = E(nc.sbuf_tensor("pp", [CH, 2, HB], F32))
        spre = E(nc.sbuf_tensor("spre", [CH, HB], BF16))
        saugT = E(nc.sbuf_tensor("saugT", [CH, 2, HB], BF16))
        mS = E(nc.sbuf_tensor("mS", [CH, 2, H, CH], BF16))
        yTb = E(nc.sbuf_tensor("yTb", [CH, 2, 8, CH], BF16))
        accb = E(nc.sbuf_tensor("accb", [CH, 2, DM], F32))
        eexp = E(nc.sbuf_tensor("eexp", [CH, 2, TG], F32))

        psA = E(nc.psum_tensor("psA", [CH, 1024], BF16))
        psB = E(nc.psum_tensor("psB", [CH, 1024], BF16))
        psC = E(nc.psum_tensor("psC", [CH, 512], F32))
        psD = E(nc.psum_tensor("psD", [CH, 512], F32))
        psE = E(nc.psum_tensor("psE", [CH, 512], F32))
        psF = E(nc.psum_tensor("psF", [CH, 512], F32))
        psG = E(nc.psum_tensor("psG", [CH, 512], F32))
        psH = E(nc.psum_tensor("psH", [CH, 512], F32))
        psAB = [psA, psB]
        psCD = [psC, psD]
        psEF = [psE, psF]
        psGH = [psG, psH]
        ps4 = [psC, psD, psE, psF]
        o2banks = [psE, psF, psG, psH]

        p = Plan()
        semnames = ["T", "V", "S", "G", "CC"]
        for qk in ("DS", "DA", "DG"):
            semnames += [f"{qk}{j}" for j in range(Plan.KDMA)]
        for s in semnames:
            p.sems[s] = E(nc.semaphore("sem" + s))
            p.cnt[s] = 0

        free = {}

        def guard(eng, key):
            p.waitm(eng, free.get(key))

        def wT(slot, dt, c0, n):
            return poolP[:, 8192 * slot + 1024 * dt + c0:8192 * slot + 1024 * dt + c0 + n]

        def w1qt(dt, f0, n):
            return poolP[:, 4096 * dt + f0:4096 * dt + f0 + n]

        def hT(ftile, t0, n):
            return poolQ[:, 512 * ftile + t0:512 * ftile + t0 + n]

        def qT_full(gp, ct):
            return poolQ[:, 4096 * gp + 512 * ct:4096 * gp + 512 * (ct + 1)]

        def qT_h(gp, h, j):
            base = 64 * (h % 2)
            return poolQ[base:base + 64,
                         4096 * gp + 512 * (h // 2) + 128 * j:
                         4096 * gp + 512 * (h // 2) + 128 * (j + 1)]

        def kT_full(gp, ct):
            return poolQ[:, 8192 + 4096 * gp + 512 * ct:8192 + 4096 * gp + 512 * (ct + 1)]

        def kT_h(gp, h, j):
            base = 64 * (h % 2)
            o = 8192 + 4096 * gp + 512 * (h // 2) + 128 * j
            return poolQ[base:base + 64, o:o + 128]

        def kTgrp(gp):
            return poolQ[:, 8192 + 4096 * gp:8192 + 4096 * (gp + 1)].rearrange(
                "p (ct t) -> p ct t", t=TG)

        def va_h(gp, j, h):
            return va[:, gp, j, 65 * h:65 * h + 65]

        def acc4(j):
            return accb[:, j // 2, 512 * (j % 2):512 * (j % 2) + 512]

        def w2slot(s):
            return w2s[:, 512 * s:512 * (s + 1)]

        def w2qv(par):
            return ktok[:, par, :]

        def xgh(j):
            return xg[:, j // 2, 512 * (j % 2):512 * (j % 2) + 512]

        def state_out(h):
            bank = psGH[h // 8]
            base = 64 * (h % 2)
            c0 = 65 * ((h // 2) % 4)
            return bank[base:base + 64, c0:c0 + 65], (h % 2 > 0)

        c_ident = p.dma("gpsimd", lambda g: g.dma_start(out=ident[:], in_=ident_in))
        p.dma("gpsimd", lambda g: g.dma_start(out=maskT[:], in_=maskT_in))
        c_pm = p.dma("gpsimd", lambda g: g.dma_start(out=pmask[:], in_=pmask_in))
        c_b1 = p.dma("gpsimd", lambda g: g.dma_start(out=b1col[:], in_=b1_in))
        c_b2 = p.dma("gpsimd", lambda g: g.dma_start(out=b2bc[0:1, :], in_=b2_in[None, :]))
        p.op("vector", lambda v: v.memset(ones_r[:], 1.0))
        m_ones = p.op("vector", lambda v: v.memset(onesc[:], 1.0), "V")
        p.op("vector", lambda v: v.memset(cvals[:, 0:1], 1e-5))
        p.op("vector", lambda v: v.memset(cvals[:, 1:2], -0.5))
        p.op("vector", lambda v: v.memset(cvals[:, 2:3], 0.5 / (DFF * DM)))
        cv_done = p.op("vector", lambda v: v.memset(cvals[:, 3:4], 0.5e-5), "V")
        v_zero = p.op("vector", lambda v: v.memset(pp[:, 0, :], 0.0), "V")

        p.waitm("tensor", c_b2)
        p.wait("tensor", "V", m_ones)
        p.op("tensor", lambda t: t.matmul(psC[:, 0:512], ones_r[:], b2bc[0:1, 0:512],
                                          start=True, stop=True))
        bm = p.op("tensor", lambda t: t.matmul(psD[:, 0:512], ones_r[:],
                                               b2bc[0:1, 512:1024],
                                               start=True, stop=True), "T")
        p.wait("vector", "T", bm)
        p.op("vector", lambda v: v.tensor_copy(b2bc[:, 0:512], psC[:, 0:512]))
        bv = p.op("vector", lambda v: v.tensor_copy(b2bc[:, 512:1024], psD[:, 0:512]),
                  "V")
        free[("psCD", 0)] = ("V", bv)
        free[("psCD", 1)] = ("V", bv)

        def prep_cast(w_ap, slot):
            cd = None
            for dt in range(8):
                par = dt % 2
                guard("sync", ("wst", par))
                d = p.dma("sync", lambda s, dt=dt, par=par, w_ap=w_ap: s.dma_start(
                    out=wst[:, par, :], in_=w_ap[128 * dt:128 * (dt + 1), :]))
                p.waitm("vector", d)
                cd = p.op("vector", lambda v, par=par, slot=slot, dt=dt: v.tensor_copy(
                    wT(slot, dt, 0, 1024), wst[:, par, :]), "V")
                free[("wst", par)] = ("V", cd)
            return cd

        prep_cast(wkT_in, 0)
        wprep_done = prep_cast(wvT_in, 1)

        gtiles = []
        for r in range(8):
            for fc2 in range(4):
                gtiles.append((W1T_in[128 * r:128 * (r + 1),
                                      1024 * fc2:1024 * (fc2 + 1)], 4 * r + fc2))
        for ft in range(32):
            gtiles.append((W2T_in[128 * ft:128 * (ft + 1), :], 32 + ft))

        spill_xnT = {}
        spill_kT = {}
        spill_va = {}
        snap_ev = {}
        xn_done = {}
        vvs = {}
        gam_s = {}
        gq_d = {}

        def emit_greduce(i, dumpar):
            ap_, col = gtiles[i]
            par_ = i % 2
            p.waitm("scalar", gq_d[i])
            rr = p.op("scalar", lambda s, par_=par_, col=col, dumpar=dumpar:
                      s.activation(accb[:, dumpar, :], wst[:, par_, :], AF.Abs,
                                   accum_out=gacc[:, col:col + 1]), "S")
            free[("wst", par_)] = ("S", rr)
            return rr

        p.wait("scalar", "V", v_zero)
        z0 = p.op("vector", lambda v: v.memset(snapst[:, 0, :], 0.0), "V")
        p.wait("scalar", "V", z0)
        sp0 = p.dma("scalar", lambda s: s.dma_start(out=snap_sp[0], in_=snapst[:, 0, :]))
        free[("snapst", 0)] = sp0

        for g in range(NG):
            gp = g % 2
            def emit_LN_B(bc):
                c, par, j = bc["c"], bc["par"], bc["j"]
                sl = 8 * par
                s1, s2 = sm[:, sl:sl + 1], sm[:, sl + 1:sl + 2]
                msq, var = sm[:, sl + 2:sl + 3], sm[:, sl + 3:sl + 4]
                lnv, rstd = sm[:, sl + 4:sl + 5], sm[:, sl + 5:sl + 6]
                nbias = sm[:, sl + 6:sl + 7]
                p.wait("vector", "V", bc["mqc"])
                p.wait("vector", "S", bc["sq"])
                vv = p.op("vector", lambda v, s2=s2, msq=msq, var=var:
                          v.scalar_tensor_tensor(var, s2, 1.0 / DM, msq,
                                                 ALU.mult, ALU.add), "V")
                vvs[c] = vv
                p.wait("scalar", "V", vv)
                p.wait("scalar", "V", cv_done)
                lnc = p.op("scalar", lambda s, var=var, lnv=lnv: s.activation(
                    lnv, var, AF.Ln, bias=cvals[:, 0:1]), "S")
                p.wait("scalar", "S", lnc)
                rs = p.op("scalar", lambda s, lnv=lnv, rstd=rstd: s.activation(
                    rstd, lnv, AF.Exp, scale=cvals[:, 1:2]), "S")
                p.wait("scalar", "S", rs)
                p.wait("vector", "S", rs)
                nb = p.op("vector", lambda v, s1=s1, rstd=rstd, nbias=nbias:
                          v.scalar_tensor_tensor(nbias, s1, -1.0 / DM, rstd,
                                                 ALU.mult, ALU.mult), "V")
                p.wait("scalar", "V", nb)
                guard("scalar", ("xnb", par))
                xn_done[c] = p.op("scalar", lambda s, par=par, rstd=rstd, nbias=nbias:
                                  s.activation(xnb[:, par, :], xg[:, par, :], AF.Identity,
                                               bias=nbias, scale=rstd), "S")
                free[("xg", par)] = ("S", xn_done[c])
                p.wait("tensor", "S", xn_done[c])
                p.waitm("tensor", c_ident)
                guard("tensor", ("xnTbuf", gp))
                tr = None
                for dt in range(8):
                    bank = psAB[dt % 2]
                    fslot = (dt // 2) % 4
                    guard("tensor", ("psAB", dt % 2))
                    tr = p.op("tensor", lambda t, bank=bank, fslot=fslot, dt=dt, par=par:
                              t.transpose(bank[:, 128 * fslot:128 * (fslot + 1)],
                                          xnb[:, par, 128 * dt:128 * (dt + 1)], ident[:]),
                              "T" if dt == 7 else None)
                free[("xnb", par)] = ("T", tr)
                p.wait("vector", "T", tr)
                guard("vector", ("xnTbuf", gp))
                for bi in range(2):
                    dst = xnT[:, gp, bi:8:2, 128 * j:128 * (j + 1)]
                    ev = p.op("vector", lambda v, bi=bi, dst=dst: v.tensor_copy(
                        dst, psAB[bi][:, 0:512]), "V")
                    free[("psAB", bi)] = ("V", ev)

            pendLN = None
            for j in range(JPG):
                c = JPG * g + j
                par = c % 2
                guard("sync", ("xg", par))
                d = p.dma("sync", lambda s, c=c, par=par: s.dma_start(
                    out=xg[:, par, :], in_=x_in[CH * c:CH * (c + 1), :]))
                p.waitm("vector", d)
                p.waitm("scalar", d)
                sl = 8 * par
                s1 = sm[:, sl:sl + 1]
                s2 = sm[:, sl + 1:sl + 2]
                msq = sm[:, sl + 2:sl + 3]
                r1c = p.op("vector", lambda v, par=par, s1=s1: v.reduce_sum(
                    s1, xg[:, par, :], axis=AX), "V")
                p.wait("vector", "V", r1c)
                guard("scalar", ("accb0",))
                if c - 2 in vvs:
                    p.wait("scalar", "V", vvs[c - 2])
                if c - 2 in gam_s:
                    p.wait("scalar", "S", gam_s[c - 2])
                sq = p.op("scalar", lambda s, par=par, s2=s2: s.activation(
                    accb[:, par, :], xg[:, par, :], AF.Square, accum_out=s2), "S")
                mqc = p.op("vector", lambda v, s1=s1, msq=msq: v.scalar_tensor_tensor(
                    msq, s1, -1.0 / (DM * DM), s1, ALU.mult, ALU.mult), "V")
                ctxLN = {"c": c, "par": par, "j": j, "sq": sq, "mqc": mqc}
                if pendLN is not None:
                    emit_LN_B(pendLN)
                pendLN = ctxLN
            emit_LN_B(pendLN)
            xnT_done = p.now("V")
            p.wait("scalar", "V", xnT_done)
            spill_xnT[g] = p.dma("scalar", lambda s, g=g, gp=gp: s.dma_start(
                out=xnT_sp.rearrange("(dt p) T -> p dt T", p=CH)[:, :, TG * g:TG * (g + 1)],
                in_=xnT[:, gp, :, :].rearrange("p a b -> p (a b)")))

            p.wait("tensor", "V", xnT_done)
            p.wait("tensor", "V", wprep_done)
            for ct in range(8):
                bank = psCD[ct % 2]
                guard("tensor", ("psCD", ct % 2))
                guard("tensor", ("kTbuf", gp))
                for dt in range(8):
                    p.op("tensor", lambda t, bank=bank, ct=ct, dt=dt, gp=gp: t.matmul(
                        bank[:, 0:512], wT(0, dt, 128 * ct, 128), xnT[:, gp, dt, 0:512],
                        start=(dt == 0), stop=(dt == 7)), "T" if dt == 7 else None)
                mmT = p.now("T")
                ep = ct % 2
                p.wait("scalar", "T", mmT)
                guard("scalar", ("eexp", ep))
                ex = p.op("scalar", lambda s, bank=bank, ep=ep: s.activation(
                    eexp[:, ep, :], bank[:, 0:512], AF.Exp), "S")
                p.wait("vector", "S", ex)
                mnc = p.op("vector", lambda v, ep=ep: v.tensor_scalar(
                    eexp[:, ep, :], eexp[:, ep, :], 1.0, None, ALU.min), "V")
                p.wait("vector", "V", mnc)
                guard("vector", ("kTbuf", gp))
                kev = p.op("vector", lambda v, bank=bank, ep=ep, ct=ct, gp=gp:
                           v.scalar_tensor_tensor(kT_full(gp, ct), bank[:, 0:512], 0.0,
                                                  eexp[:, ep, :], ALU.max, ALU.add), "V")
                free[("eexp", ep)] = ("V", kev)
                free[("psCD", ct % 2)] = ("V", kev)
            kT_all = p.now("V")
            p.wait("scalar", "V", kT_all)
            spill_kT[g] = p.dma("scalar", lambda s, g=g, gp=gp: s.dma_start(
                out=kT_sp.rearrange("(ct p) T -> p ct T", p=CH)[:, :, TG * g:TG * (g + 1)],
                in_=kTgrp(gp)))

            guard("vector", ("vabuf", gp))
            on = p.op("vector", lambda v, gp=gp: v.memset(
                va[:, gp, :, :].rearrange("p j (h e) -> p j h e", e=D + 1)[:, :, :, D:D + 1],
                1.0), "V")

            for j in range(JPG):
                c = JPG * g + j
                par = c % 2
                guard("tensor", ("ktok", par))
                guard("tensor", ("psEF", 0))
                guard("tensor", ("psEF", 1))
                for dt in range(8):
                    for half in range(2):
                        p.op("tensor", lambda t, dt=dt, half=half, gp=gp, j=j: t.matmul(
                            psEF[half][:, 0:512], xnT[:, gp, dt, 128 * j:128 * (j + 1)],
                            wT(1, dt, 512 * half, 512), start=(dt == 0), stop=(dt == 7)),
                            "T" if (dt == 7 and half == 1) else None)
                kvT = p.now("T")
                p.wait("tensor", "V", kT_all)
                tr2 = None
                for ct in range(8):
                    bank = psAB[ct % 2]
                    fslot = (ct // 2) % 4
                    guard("tensor", ("psAB", ct % 2))
                    tr2 = p.op("tensor", lambda t, bank=bank, fslot=fslot, ct=ct,
                               gp=gp, j=j: t.transpose(
                                   bank[:, 128 * fslot:128 * (fslot + 1)],
                                   kT_full(gp, ct)[:, 128 * j:128 * (j + 1)], ident[:]),
                               "T" if ct == 7 else None)
                p.wait("vector", "T", tr2)
                for bi in range(2):
                    dst = ktok[:, par, :].rearrange("p (a b) -> p a b", b=128)[:, bi:8:2, :]
                    kv = p.op("vector", lambda v, bi=bi, dst=dst: v.tensor_copy(
                        dst, psAB[bi][:, 0:512]), "V")
                    free[("psAB", bi)] = ("V", kv)
                p.wait("vector", "T", kvT)
                p.wait("vector", "V", on)
                for half in range(2):
                    dst = va[:, gp, j, :].rearrange("p (h e) -> p h e", e=D + 1)[
                        :, 8 * half:8 * (half + 1), 0:D]
                    vv1 = p.op("vector", lambda v, half=half, dst=dst: v.tensor_copy(
                        dst, psEF[half][:, 0:512]), "V")
                    free[("psEF", half)] = ("V", vv1)
                ktv = p.now("V")

                p.wait("tensor", "V", ktv)
                if c > 0:
                    p.wait("tensor", "V", snap_ev[c - 1])
                if c == 0:
                    for bi in range(2):
                        p.op("tensor", lambda t, bi=bi: t.matmul(
                            psGH[bi][:, 500:501], ident[:], ident[:, 0:1],
                            start=True, stop=False, skip_group_check=True))
                for h in (0, 8, 1, 9, 2, 10, 3, 11, 4, 12, 5, 13, 6, 14, 7, 15):
                    o, up = state_out(h)
                    p.op("tensor", lambda t, h=h, o=o, up=up, par=par, gp=gp, j=j:
                         t.matmul(o, ktok[:, par, 64 * h:64 * (h + 1)], va_h(gp, j, h),
                                  start=False, stop=False, skip_group_check=True,
                                  tile_position=(0, 64) if up else None),
                         "T" if h == 15 else None)
                if c == NC - 1:
                    for bi in range(2):
                        p.op("tensor", lambda t, bi=bi: t.matmul(
                            psGH[bi][:, 500:501], ident[:], ident[:, 0:1],
                            start=False, stop=True, skip_group_check=True))
                st_done = p.now("T")
                free[("ktok", par)] = ("T", st_done)
                p.wait("vector", "T", st_done)
                spar = (c + 1) % 2
                guard("vector", ("snapst", spar))
                for bi in range(2):
                    p.op("vector", lambda v, bi=bi, spar=spar: v.tensor_copy(
                        snapst[:, spar, 260 * bi:260 * (bi + 1)], psGH[bi][:, 0:260]),
                        "V" if bi == 1 else None)
                snap_ev[c] = p.now("V")
                p.wait("scalar", "V", snap_ev[c])
                sd = p.dma("scalar", lambda s, c=c, spar=spar: s.dma_start(
                    out=snap_sp[c + 1], in_=snapst[:, spar, :]))
                free[("snapst", spar)] = sd
                pt = c
                if pt > 0:
                    if pt - 2 in gam_s:
                        p.wait("scalar", "S", gam_s[pt - 2])
                    p.wait("scalar", "V", vvs[pt])
                    for i in (2 * pt - 2, 2 * pt - 1):
                        gam_s[pt] = emit_greduce(i, pt % 2)
                for i in (2 * pt, 2 * pt + 1):
                    ap_, _col = gtiles[i]
                    par_ = i % 2
                    qeng = "scalar" if par_ == 0 else "gpsimd"
                    guard(qeng, ("wst", par_))
                    gq_d[i] = p.dma(qeng, lambda s, ap_=ap_, par_=par_: s.dma_start(
                        out=wst[:, par_, :], in_=ap_))
            p.wait("scalar", "V", p.now("V"))
            spill_va[g] = p.dma("scalar", lambda s, g=g, gp=gp: s.dma_start(
                out=va_sp[TG * g:TG * (g + 1), :].rearrange("(j p) n -> p j n", p=CH),
                in_=va[:, gp, :, :]))
            free[("vabuf", gp)] = spill_va[g]
            free[("xnTbuf", gp)] = spill_xnT[g]
            free[("kTbuf", gp)] = spill_kT[g]

        p.wait("scalar", "V", vvs[NC - 1])
        for i in (62, 63):
            gam_s[NC] = emit_greduce(i, NC % 2)

        p1T = p.now("T")

        p.wait("vector", "T", p1T)
        for bi in range(2):
            p.op("vector", lambda v, bi=bi: v.tensor_copy(
                agst[:, 260 * bi:260 * (bi + 1)], psGH[bi][:, 0:260]),
                "V" if bi == 1 else None)
        ag_ready = p.now("V")
        p.wait("gpsimd", "V", ag_ready)
        dg = p.dma("gpsimd", lambda g: g.dma_start(out=ag_in[:], in_=agst[:]))
        p.waitm("gpsimd", dg)
        p.op("gpsimd", lambda g: g.collective_compute(
            "AllGather", ALU.bypass, replica_groups=[list(range(8))],
            ins=[ag_in_t.ap().opt()], outs=[ag_out_t.ap().opt()]), "CC")

        g1, g2 = gacc[:, 64:65], gacc[:, 65:66]
        p.wait("vector", "V", p.now("V"))
        p.wait("vector", "S", p.now("S"))
        p.wait("vector", "G", p.now("G"))
        gv1 = p.op("vector", lambda v: v.reduce_sum(g1, gacc[:, 0:32], axis=AX), "V")
        gv = p.op("vector", lambda v: v.reduce_sum(g2, gacc[:, 32:64], axis=AX), "V")
        p.wait("tensor", "V", gv)
        p.wait("tensor", "V", ag_ready)
        p.op("tensor", lambda t: t.matmul(psG[0:1, 0:1], g1, onesc[:],
                                          start=True, stop=True))
        tg2 = p.op("tensor", lambda t: t.matmul(psG[0:1, 1:2], g2, onesc[:],
                                                start=True, stop=True), "T")
        p.wait("scalar", "T", tg2)
        p.op("scalar", lambda s: s.activation(
            gacc[0:1, 66:67], psG[0:1, 0:1], AF.Identity, scale=cvals[0:1, 2:3],
            bias=cvals[0:1, 3:4]))
        t2 = p.op("scalar", lambda s: s.activation(
            gacc[0:1, 67:68], psG[0:1, 1:2], AF.Identity, scale=cvals[0:1, 2:3],
            bias=cvals[0:1, 3:4]),
            "S")
        p.wait("tensor", "S", t2)
        p.op("tensor", lambda t: t.matmul(psG[:, 2:3], ones_r[:], gacc[0:1, 66:67],
                                          start=True, stop=True))
        bc = p.op("tensor", lambda t: t.matmul(psG[:, 3:4], ones_r[:], gacc[0:1, 67:68],
                                               start=True, stop=True), "T")
        p.wait("vector", "T", bc)
        p.op("vector", lambda v: v.tensor_copy(thrs[:, 0:1], psG[:, 2:3]))
        p.op("vector", lambda v: v.tensor_scalar(thrs[:, 1:2], psG[:, 2:3], -1.0, None,
                                                 ALU.mult))
        p.op("vector", lambda v: v.tensor_copy(thrs[:, 2:3], psG[:, 3:4]))
        thr_done = p.op("vector", lambda v: v.tensor_scalar(
            thrs[:, 3:4], psG[:, 3:4], -1.0, None, ALU.mult), "V")

        p.wait("sync", "T", p1T)
        p.waitm("sync", spill_xnT[0])
        p.waitm("sync", spill_xnT[NG - 2])
        dxs = {}
        dxs[0] = p.dma("sync", lambda s: s.dma_start(
            out=xnT[:, 0, :, :].rearrange("p a b -> p (a b)"),
            in_=xnT_sp.rearrange("(dt p) T -> p dt T", p=CH)[:, :, 0:TG]))

        p.wait("vector", "V", thr_done)
        w1bufs = [wst[:, 0, :], wst[:, 1, :], xg[:, 0, :], xg[:, 1, :]]
        w1keys = [("wst", 0), ("wst", 1), ("xgq", 0), ("xgq", 1)]
        w1fc = {}
        for i in range(32):
            fc2, r = i // 8, i % 8
            b4 = i % 4
            buf = w1bufs[b4]
            qe1 = ("sync", "scalar", "gpsimd")[i % 3]
            guard(qe1, w1keys[b4])
            d = p.dma(qe1, lambda s, r=r, fc2=fc2, buf=buf: s.dma_start(
                out=buf,
                in_=W1T_in[128 * r:128 * (r + 1), 1024 * fc2:1024 * (fc2 + 1)]))
            p.waitm("vector", d)
            guard("vector", ("accb1",))
            q1c = p.op("vector", lambda v, buf=buf: v.tensor_scalar(
                accb[:, 1, :], buf, thrs[:, 1:2], -1.0, ALU.is_lt,
                ALU.mult), "V")
            p.wait("vector", "V", q1c)
            qz = p.op("vector", lambda v, buf=buf, r=r, fc2=fc2: v.scalar_tensor_tensor(
                w1qt(r, 1024 * fc2, 1024), buf, thrs[:, 0:1], accb[:, 1, :],
                ALU.is_gt, ALU.add), "V")
            free[w1keys[b4]] = ("V", qz)
            free[("accb1",)] = ("V", qz)
            if r == 7:
                w1fc[fc2] = p.now("V")
        w1_ready = p.now("V")

        w2_spill_last = None
        w2sp = {}
        for ft in range(32):
            par = ft % 2
            b4 = ft % 4
            buf = (xg[:, 0, :], xg[:, 1, :], wst[:, 0, :], wst[:, 1, :])[b4]
            bkey = (("xgq", 0), ("xgq", 1), ("wst", 0), ("wst", 1))[b4]
            qe2 = ("gpsimd", "sync", "scalar")[ft % 3]
            guard(qe2, bkey)
            d = p.dma(qe2, lambda s, ft=ft, buf=buf: s.dma_start(
                out=buf, in_=W2T_in[128 * ft:128 * (ft + 1), :]))
            p.waitm("vector", d)
            guard("vector", ("accbq",))
            q1c = p.op("vector", lambda e, buf=buf: e.tensor_scalar(
                accb[:, 0, :], buf, thrs[:, 3:4], -1.0, ALU.is_lt,
                ALU.mult), "V")
            p.wait("vector", "V", q1c)
            guard("vector", ("w2q", par))
            qz = p.op("vector", lambda e, buf=buf, par=par: e.scalar_tensor_tensor(
                w2qv(par), buf, thrs[:, 2:3], accb[:, 0, :],
                ALU.is_gt, ALU.add), "V")
            free[bkey] = ("V", qz)
            free[("accbq",)] = ("V", qz)
            qe3 = ("scalar", "gpsimd", "sync")[ft % 3]
            p.wait(qe3, "V", qz)
            w2_spill_last = p.dma(qe3, lambda s, ft=ft, par=par: s.dma_start(
                out=w2qt_sp[128 * ft:128 * (ft + 1), :], in_=w2qv(par)))
            w2sp[ft] = w2_spill_last
            free[("w2q", par)] = w2_spill_last
        w2q_done = ("V", p.now("V"))
        free[("ybuf", 0)] = free.get(("w2q", 0))
        free[("ybuf", 1)] = free.get(("w2q", 1))
        for j in range(4):
            free[("xgh", j)] = w2q_done
        free[("accs", 0)] = w2q_done
        free[("accs", 1)] = w2q_done
        free[("accs", 2)] = ("V", w1_ready)
        free[("accs", 3)] = ("V", w1_ready)
        free[("accb0",)] = w2q_done

        w2i = 0
        for g in range(NG):
            gp = g % 2
            p.waitm("tensor", dxs[g])
            if g == 0:
                p.waitm("scalar", spill_kT[NG - 1])
            hvs = {}
            for ftile in range(32):
                bank = psCD[ftile % 2]
                if g == 0:
                    p.wait("tensor", "V", w1fc[ftile // 8])
                guard("tensor", ("psCD", ftile % 2))
                for dt in range(8):
                    p.op("tensor", lambda t, bank=bank, ftile=ftile, dt=dt, gp=gp: t.matmul(
                        bank[:, 0:512], w1qt(dt, 128 * ftile, 128), xnT[:, gp, dt, 0:512],
                        start=(dt == 0), stop=(dt == 7)), "T" if dt == 7 else None)
                mmn = p.now("T")
                p.wait("scalar", "T", mmn)
                hv = p.op("scalar", lambda s, bank=bank, ftile=ftile: s.activation(
                    hT(ftile, 0, 512), bank[:, 0:512], AF.Relu,
                    bias=b1col[:, ftile:ftile + 1]), "S")
                hvs[ftile] = hv
                free[("psCD", ftile % 2)] = ("S", hv)
            free[("xnTbuf2", gp)] = ("T", p.now("T"))
            if g + 1 < NG:
                p.waitm("sync", spill_xnT[g + 1])
                guard("sync", ("xnTbuf2", (g + 1) % 2))
                dxs[g + 1] = p.dma("sync", lambda s, g=g: s.dma_start(
                    out=xnT[:, (g + 1) % 2, :, :].rearrange("p a b -> p (a b)"),
                    in_=xnT_sp.rearrange("(dt p) T -> p dt T", p=CH)[
                        :, :, TG * (g + 1):TG * (g + 2)]))

            for dmh in range(2):
                pb = [psE, psF, psG, psH] if dmh == 0 else [psC, psD, psE, psF]
                pbi = [2, 3, 4, 5] if dmh == 0 else [0, 1, 2, 3]
                if dmh == 1:
                    free[("o2b", 0)] = free.get(("psCD", 0))
                    free[("o2b", 1)] = free.get(("psCD", 1))
                dxxs = {}
                for j in range(JPG):
                    c = JPG * g + j
                    guard("scalar", ("xgh", j))
                    dxxs[j] = p.dma("scalar", lambda s, c=c, j=j, dmh=dmh: s.dma_start(
                        out=xgh(j), in_=x_in[CH * c:CH * (c + 1),
                                             512 * dmh:512 * (dmh + 1)]))
                for ft in range(32):
                    slot = w2i % NW2
                    qeng = "sync" if w2i % 2 == 0 else "gpsimd"
                    w2i += 1
                    guard(qeng, ("w2s", slot))
                    if g == 0 and dmh == 0:
                        p.waitm(qeng, w2sp[ft])
                    dw = p.dma(qeng, lambda s, ft=ft, dmh=dmh, slot=slot: s.dma_start(
                        out=w2slot(slot),
                        in_=w2qt_sp[128 * ft:128 * (ft + 1),
                                    512 * dmh:512 * (dmh + 1)]))
                    p.waitm("tensor", dw)
                    p.wait("tensor", "S", hvs[ft])
                    lm = None
                    for j in range(JPG):
                        if ft == 0:
                            guard("tensor", ("o2b", pbi[j]))
                        lm = p.op("tensor", lambda t, ft=ft, j=j, slot=slot, pb=pb: t.matmul(
                            pb[j][:, 0:512], hT(ft, 128 * j, 128), w2slot(slot),
                            start=(ft == 0), stop=(ft == 31)),
                            "T" if j == 3 else None)
                    free[("w2s", slot)] = ("T", lm)
                mm2p = p.now("T")
                for j in range(JPG):
                    c = JPG * g + j
                    p.waitm("vector", dxxs[j])
                    p.wait("vector", "T", mm2p)
                    guard("vector", ("accs", j))
                    a1 = p.op("vector", lambda v, j=j, dmh=dmh, pb=pb: v.tensor_tensor(
                        acc4(j), pb[j][:, 0:512],
                        b2bc[:, 512 * dmh:512 * (dmh + 1)], ALU.add), "V")
                    p.wait("vector", "V", a1)
                    a2 = p.op("vector", lambda v, j=j: v.tensor_tensor(
                        acc4(j), acc4(j), xgh(j), ALU.add), "V")
                    free[("o2b", pbi[j])] = ("V", a2)
                    free[("xgh", j)] = ("V", a2)
                    p.wait("scalar", "V", a2)
                    sa = p.dma("scalar", lambda s, c=c, j=j, dmh=dmh: s.dma_start(
                        out=acc_sp[CH * c:CH * (c + 1), 512 * dmh:512 * (dmh + 1)],
                        in_=acc4(j)))
                    free[("accs", j)] = sa
                if dmh == 1:
                    free[("psCD", 0)] = free[("o2b", 0)]
                    free[("psCD", 1)] = free[("o2b", 1)]
        ffn_T = p.now("T")
        free[("psGH", 0)] = free.get(("o2b", 4))
        free[("psGH", 1)] = free.get(("o2b", 5))
        for b_ in range(4):
            free[("nbank", b_)] = free.get(("o2b", b_))
            free[("sslot", b_)] = free.get(("o2b", b_))

        p.wait("tensor", "V", p.now("V"))
        p.wait("tensor", "S", p.now("S"))
        p.wait("vector", "T", ffn_T)
        prep_cast(wqT_in, 2)
        wqp_done = prep_cast(wpT_in, 3)
        p.wait("sync", "T", ffn_T)
        p.dma_barrier("sync")

        p.wait("gpsimd", "CC", 1)
        for r in range(8):
            guard("gpsimd", ("gr", r % 2))
            dgr = p.dma("gpsimd", lambda g, r=r: g.dma_start(
                out=gr[:, r % 2, :], in_=ag_out[CH * r:CH * (r + 1), :]))
            p.waitm("vector", dgr)
            p.wait("vector", "V", v_zero)
            p.waitm("vector", c_pm)
            cmb = p.op("vector", lambda v, r=r: v.scalar_tensor_tensor(
                pp[:, (r + 1) % 2, :], gr[:, r % 2, :], pmask[:, r:r + 1],
                pp[:, r % 2, :], ALU.mult, ALU.add), "V")
            p.wait("vector", "V", cmb)
            free[("gr", r % 2)] = ("V", cmb)
        prefix_done = p.op("vector", lambda v: v.tensor_copy(spre[:], pp[:, 0, :]), "V")

        y_of = {}

        for g in range(NG):
            gp = g % 2
            guard("sync", ("p2xnT", gp))
            guard("sync", ("p2kT", gp))
            guard("sync", ("p2va", gp))
            p.waitm("sync", spill_kT[g])
            p.waitm("sync", spill_va[g])
            dq = p.dma("sync", lambda s, g=g, gp=gp: s.dma_start(
                out=xnT[:, gp, :, :].rearrange("p a b -> p (a b)"),
                in_=xnT_sp.rearrange("(dt p) T -> p dt T", p=CH)[:, :, TG * g:TG * (g + 1)]))
            dk = p.dma("sync", lambda s, g=g, gp=gp: s.dma_start(
                out=kTgrp(gp),
                in_=kT_sp.rearrange("(ct p) T -> p ct T", p=CH)[:, :, TG * g:TG * (g + 1)]))
            dv = p.dma("sync", lambda s, g=g, gp=gp: s.dma_start(
                out=va[:, gp, :, :],
                in_=va_sp[TG * g:TG * (g + 1), :].rearrange("(j p) n -> p j n", p=CH)))
            p.waitm("tensor", dq)
            p.wait("tensor", "V", wqp_done)
            if g == 0:
                p.wait("tensor", "T", ffn_T)
            for ct in range(8):
                bank = psGH[ct % 2]
                guard("tensor", ("psGH", ct % 2))
                guard("tensor", ("p2qT", gp))
                for dt in range(8):
                    p.op("tensor", lambda t, bank=bank, ct=ct, dt=dt, gp=gp: t.matmul(
                        bank[:, 0:512], wT(2, dt, 128 * ct, 128), xnT[:, gp, dt, 0:512],
                        start=(dt == 0), stop=(dt == 7)), "T" if dt == 7 else None)
                mmq = p.now("T")
                ep = ct % 2
                p.wait("scalar", "T", mmq)
                guard("scalar", ("eexp", ep))
                ex = p.op("scalar", lambda s, bank=bank, ep=ep: s.activation(
                    eexp[:, ep, :], bank[:, 0:512], AF.Exp), "S")
                p.wait("vector", "S", ex)
                mnc = p.op("vector", lambda v, ep=ep: v.tensor_scalar(
                    eexp[:, ep, :], eexp[:, ep, :], 1.0, None, ALU.min), "V")
                p.wait("vector", "V", mnc)
                qe = p.op("vector", lambda v, bank=bank, ep=ep, ct=ct, gp=gp:
                          v.scalar_tensor_tensor(qT_full(gp, ct), bank[:, 0:512], 0.0,
                                                 eexp[:, ep, :], ALU.max, ALU.add), "V")
                free[("eexp", ep)] = ("V", qe)
                free[("psGH", ct % 2)] = ("V", qe)
            q_done = p.now("V")

            for j in range(JPG):
                c = JPG * g + j
                par = c % 2
                guard("sync", ("snld", par))
                dsn = p.dma("sync", lambda s, c=c, par=par: s.dma_start(
                    out=snld[:, par, :], in_=snap_sp[c]))
                guard("sync", ("accbr", par))
                guard("sync", ("accbr2", par))
                dac = p.dma("sync", lambda s, c=c, par=par: s.dma_start(
                    out=accb[:, par, :], in_=acc_sp[CH * c:CH * (c + 1), :]))
                p.waitm("vector", dsn)
                p.wait("vector", "V", prefix_done)
                guard("vector", ("saugT", par))
                sg = p.op("vector", lambda v, par=par: v.tensor_tensor(
                    saugT[:, par, :], snld[:, par, :], spre[:], ALU.add), "V")
                free[("snld", par)] = ("V", sg)

                p.waitm("tensor", dk)
                p.waitm("tensor", dv)
                p.wait("tensor", "V", q_done)
                p.wait("tensor", "V", sg)
                num_last = None
                for w in range(4):
                    scm = None
                    for b in range(4):
                        h = 4 * w + b
                        guard("tensor", ("sslot", b))
                        scm = p.op("tensor", lambda t, b=b, h=h, j=j, gp=gp:
                                   t.matmul(ps4[b][:, 260:388], kT_h(gp, h, j),
                                            qT_h(gp, h, j), start=True, stop=True),
                                   "T" if b == 3 else None)
                    p.wait("vector", "T", scm)
                    msv = None
                    for b in range(4):
                        h = 4 * w + b
                        guard("vector", ("mS", h))
                        msv = p.op("vector", lambda v, b=b, h=h, par=par:
                                   v.tensor_tensor(mS[:, par, h, :], ps4[b][:, 260:388],
                                                   maskT[:], ALU.mult),
                                   "V" if b == 3 else None)
                    for b in range(4):
                        free[("sslot", b)] = ("V", msv)
                    p.wait("tensor", "V", msv)
                    for b in range(4):
                        h = 4 * w + b
                        guard("tensor", ("nbank", b))
                        guard("tensor", ("nbankv", b))
                        o = ps4[b][:, 65 * w:65 * w + 65]
                        sb = 64 * (h % 2)
                        p.op("tensor", lambda t, o=o, h=h, j=j, gp=gp, sb=sb, par=par:
                             t.matmul(o, qT_h(gp, h, j),
                                      saugT[sb:sb + 64, par,
                                            65 * (h // 2):65 * (h // 2) + 65],
                                      start=True, stop=False))
                        num_last = p.op("tensor", lambda t, o=o, h=h, par=par, gp=gp, j=j:
                                        t.matmul(o, mS[:, par, h, :], va_h(gp, j, h),
                                                 start=False, stop=True),
                                        "T" if b == 3 else None)
                        free[("mS", h)] = ("T", p.now("T"))
                p.wait("vector", "T", num_last)
                guard("vector", ("recb", par))
                guard("vector", ("recbs", par))
                dxc = None
                for b in range(4):
                    dxc = p.op("vector", lambda v, b=b, par=par: v.tensor_scalar(
                        recb[:, par, 4 * b:4 * b + 4].rearrange("p (s o) -> p s o", o=1),
                        ps4[b][:, 64:324].rearrange(
                            "p (s o) -> p s o", o=65)[:, :, 0:1],
                        1e-6, None, ALU.add), "V" if b == 3 else None)
                p.wait("vector", "V", dxc)
                rcd = p.op("vector", lambda v, par=par: v.reciprocal(
                    recb[:, par, :], recb[:, par, :]), "V")
                p.wait("vector", "V", rcd)
                p.wait("scalar", "V", rcd)
                guard("scalar", ("ybuf", par))
                guard("vector", ("ybuf", par))
                pysc = {}
                for hp in range(8):
                    yeng = "scalar" if hp % 2 == 0 else "vector"
                    for hh in range(2):
                        h = 2 * hp + hh
                        bank = ps4[h % 4]
                        s_ = h // 4
                        if yeng == "scalar":
                            p.op("scalar", lambda s, bank=bank, h=h, s_=s_, par=par:
                                 s.activation(ktok[:, par, 64 * h:64 * (h + 1)],
                                              bank[:, 65 * s_:65 * s_ + 64],
                                              AF.Identity,
                                              scale=recb[:, par, 4 * (h % 4) + s_:
                                                         4 * (h % 4) + s_ + 1]),
                                 "S")
                        else:
                            p.op("vector", lambda v, bank=bank, h=h, s_=s_, par=par:
                                 v.tensor_scalar(ktok[:, par, 64 * h:64 * (h + 1)],
                                                 bank[:, 65 * s_:65 * s_ + 64],
                                                 recb[:, par, 4 * (h % 4) + s_:
                                                      4 * (h % 4) + s_ + 1],
                                                 None, ALU.mult), "V")
                    pysc[hp] = ("S" if yeng == "scalar" else "V",
                                p.now("S" if yeng == "scalar" else "V"))
                ysc_n = p.now("S")
                yscv_n = p.now("V")
                for b in range(4):
                    free[("nbank", b)] = ("S", ysc_n)
                    free[("nbankv", b)] = ("V", yscv_n)
                free[("recb", par)] = ("V", yscv_n)
                free[("recbs", par)] = ("S", ysc_n)
                tr = None
                for et in range(8):
                    p.wait("tensor", pysc[et][0], pysc[et][1])
                    bank = psAB[et % 2]
                    fslot = (et // 2) % 4
                    guard("tensor", ("psAB", et % 2))
                    tr = p.op("tensor", lambda t, bank=bank, fslot=fslot, et=et, par=par:
                              t.transpose(bank[:, 128 * fslot:128 * (fslot + 1)],
                                          ktok[:, par, 128 * et:128 * (et + 1)], ident[:]),
                              "T" if et == 7 else None)
                free[("ybuf", par)] = ("T", tr)
                p.wait("vector", "T", tr)
                for bi in range(2):
                    dst = yTb[:, par, bi:8:2, :]
                    ev = p.op("vector", lambda v, bi=bi, dst=dst: v.tensor_copy(
                        dst, psAB[bi][:, 0:512]), "V")
                    free[("psAB", bi)] = ("V", ev)
                yT_done = p.now("V")
                p.wait("tensor", "V", yT_done)
                guard("tensor", ("psGH", 0))
                guard("tensor", ("psGH", 1))
                prj = None
                for et in range(8):
                    for eh in range(2):
                        prj = p.op("tensor", lambda t, et=et, eh=eh, par=par: t.matmul(
                            psGH[eh][:, 0:512], yTb[:, par, et, :],
                            wT(3, et, 512 * eh, 512),
                            start=(et == 0), stop=(et == 7)),
                            "T" if (et == 7 and eh == 1) else None)
                free[("yTb", par)] = ("T", prj)
                p.wait("vector", "T", prj)
                p.waitm("vector", dac)
                guard("vector", ("accbr2", par))
                for eh in range(2):
                    p.op("vector", lambda v, eh=eh, par=par: v.tensor_tensor(
                        accb[:, par, 512 * eh:512 * (eh + 1)], psGH[eh][:, 0:512],
                        accb[:, par, 512 * eh:512 * (eh + 1)], ALU.add),
                        "V" if eh == 1 else None)
                fout = p.now("V")
                free[("psGH", 0)] = ("V", fout)
                free[("psGH", 1)] = ("V", fout)
                p.wait("scalar", "V", fout)
                so = p.dma("scalar", lambda s, c=c, par=par: s.dma_start(
                    out=out_d[CH * c:CH * (c + 1), :], in_=accb[:, par, :]))
                free[("accbr", par)] = ("V", fout)
                free[("accbr2", par)] = so
                free[("saugT", par)] = ("T", p.now("T"))
            free[("p2qT", gp)] = ("T", p.now("T"))
            free[("p2xnT", gp)] = ("T", p.now("T"))
            free[("p2kT", gp)] = ("T", p.now("T"))
            free[("p2va", gp)] = ("T", p.now("T"))

        p.dma_barrier("gpsimd")

        with nc.Block() as block:
            @block.sync
            def _(sync):
                for fn in p.q["sync"]:
                    fn(sync)

            @block.vector
            def _(vector):
                for fn in p.q["vector"]:
                    fn(vector)

            @block.scalar
            def _(scalar):
                for fn in p.q["scalar"]:
                    fn(scalar)

            @block.tensor
            def _(tensor):
                for fn in p.q["tensor"]:
                    fn(tensor)

            @block.gpsimd
            def _(gpsimd):
                for fn in p.q["gpsimd"]:
                    fn(gpsimd)

    return nc


def make_inputs(c, x_full, w_attn, w_proj, W1, b1, W2, b2, span=4096):
    b, g = divmod(c, 4)
    m = np.zeros((CH, 8), np.float32)
    for j in range(8):
        if j // 4 == b and j % 4 < g:
            m[:, j] = 1.0
    maskT = np.tril(np.ones((CH, CH), np.float32)).T.copy()
    return {
        "x": np.ascontiguousarray(x_full[b, g * span:(g + 1) * span]),
        "wqT": np.ascontiguousarray(w_attn[0:DM].T),
        "wkT": np.ascontiguousarray(w_attn[DM:2 * DM].T),
        "wvT": np.ascontiguousarray(w_attn[2 * DM:3 * DM].T),
        "wpT": np.ascontiguousarray(w_proj.T),
        "W1T": np.ascontiguousarray(W1.T),
        "b1": np.ascontiguousarray(b1.reshape(32, CH).T),
        "W2T": np.ascontiguousarray(W2.T),
        "b2": np.ascontiguousarray(b2),
        "ident": np.eye(CH, dtype=np.float32),
        "maskT": maskT,
        "pmask": m,
    }


def kernel(L0_attn_x, L0_attn_w_attn, L0_attn_w_proj, L0_ffn_W1, L0_ffn_b1,
           L0_ffn_W2, L0_ffn_b2):
    x = np.asarray(L0_attn_x, dtype=np.float32)
    w_attn = np.asarray(L0_attn_w_attn, dtype=np.float32)
    w_proj = np.asarray(L0_attn_w_proj, dtype=np.float32)
    W1 = np.asarray(L0_ffn_W1, dtype=np.float32)
    b1 = np.asarray(L0_ffn_b1, dtype=np.float32)
    W2 = np.asarray(L0_ffn_W2, dtype=np.float32)
    b2 = np.asarray(L0_ffn_b2, dtype=np.float32)
    span = 4096
    nc = build(span)
    in_maps = [make_inputs(c, x, w_attn, w_proj, W1, b1, W2, b2, span)
               for c in range(8)]
    res = run_bass_kernel_spmd(nc, in_maps, core_ids=list(range(8)))
    out = np.empty((2, 16384, DM), np.float32)
    for c in range(8):
        b, g = divmod(c, 4)
        out[b, g * span:(g + 1) * span] = res.results[c]["out"]
    return out


# version 28
# speedup vs baseline: 1.1052x; 1.0056x over previous
import contextlib
import numpy as np
import concourse.bass as bass
import concourse.mybir as mybir
from concourse.bass_utils import run_bass_kernel_spmd

F32 = mybir.dt.float32
BF16 = mybir.dt.bfloat16
AF = mybir.ActivationFunctionType
ALU = mybir.AluOpType
AX = mybir.AxisListType.X

DM = 1024
H = 16
D = 64
CH = 128
TG = 512
DFF = 4096
HB = 8 * (D + 1)
NW2 = 6


class Plan:
    ENGS = ("sync", "vector", "scalar", "tensor", "gpsimd")
    QKEY = {"sync": "DS", "scalar": "DA", "gpsimd": "DG"}
    KDMA = 6

    def __init__(self):
        self.q = {e: [] for e in self.ENGS}
        self.cnt = {}
        self.sems = {}
        self.waited = {e: {} for e in self.ENGS}
        self.dmai = {"DS": 0, "DA": 0, "DG": 0}

    def op(self, eng, fn, inc=None, inc_n=1):
        if inc is None:
            self.q[eng].append(fn)
            return None
        self.cnt[inc] = self.cnt.get(inc, 0) + inc_n
        sem = self.sems[inc]

        def wrapped(e, fn=fn, sem=sem, n=inc_n):
            fn(e).then_inc(sem, n)

        self.q[eng].append(wrapped)
        return self.cnt[inc]

    def dma(self, eng, fn):
        qk = self.QKEY[eng]
        i = self.dmai[qk]
        self.dmai[qk] = i + 1
        name = f"{qk}{i % self.KDMA}"
        prev = 16 * (i // self.KDMA)
        if prev:
            self.wait(eng, name, prev)
        self.cnt[name] = self.cnt.get(name, 0) + 16
        sem = self.sems[name]

        def wrapped(e, fn=fn, sem=sem):
            fn(e).then_inc(sem, 16)

        self.q[eng].append(wrapped)
        return (name, self.cnt[name])

    def dma_barrier(self, eng):
        ms = []
        for qk in ("DS", "DA", "DG"):
            for j in range(self.KDMA):
                name = f"{qk}{j}"
                v = self.cnt.get(name, 0)
                if v:
                    self.wait(eng, name, v)
                    ms.append((name, v))
        return ms

    def wait(self, eng, sem_name, val):
        if not val:
            return
        if self.waited[eng].get(sem_name, 0) >= val:
            return
        self.waited[eng][sem_name] = val
        sem = self.sems[sem_name]

        def w(e, sem=sem, v=val):
            e.wait_ge(sem, v)

        self.q[eng].append(w)

    def waitm(self, eng, m):
        if m is not None:
            self.wait(eng, m[0], m[1])

    def now(self, sem_name):
        return self.cnt.get(sem_name, 0)


def build(span=4096):
    NG = span // TG
    NC = span // CH
    JPG = TG // CH
    nc = bass.Bass("TRN2", target_bir_lowering=False, debug=False)

    x_in = nc.dram_tensor("x", [span, DM], F32, kind="ExternalInput").ap()
    wqT_in = nc.dram_tensor("wqT", [DM, DM], F32, kind="ExternalInput").ap()
    wkT_in = nc.dram_tensor("wkT", [DM, DM], F32, kind="ExternalInput").ap()
    wvT_in = nc.dram_tensor("wvT", [DM, DM], F32, kind="ExternalInput").ap()
    wpT_in = nc.dram_tensor("wpT", [DM, DM], F32, kind="ExternalInput").ap()
    W1T_in = nc.dram_tensor("W1T", [DM, DFF], F32, kind="ExternalInput").ap()
    b1_in = nc.dram_tensor("b1", [CH, 32], F32, kind="ExternalInput").ap()
    W2T_in = nc.dram_tensor("W2T", [DFF, DM], F32, kind="ExternalInput").ap()
    b2_in = nc.dram_tensor("b2", [DM], F32, kind="ExternalInput").ap()
    ident_in = nc.dram_tensor("ident", [CH, CH], F32, kind="ExternalInput").ap()
    maskT_in = nc.dram_tensor("maskT", [CH, CH], F32, kind="ExternalInput").ap()
    pmask_in = nc.dram_tensor("pmask", [CH, 8], F32, kind="ExternalInput").ap()
    out_d = nc.dram_tensor("out", [span, DM], F32, kind="ExternalOutput").ap()

    xnT_sp = nc.dram_tensor("xnT_sp", [DM, span], BF16).ap()
    kT_sp = nc.dram_tensor("kT_sp", [DM, span], BF16).ap()
    va_sp = nc.dram_tensor("va_sp", [span, H * (D + 1)], BF16).ap()
    snap_sp = nc.dram_tensor("snap_sp", [NC + 1, CH, HB], BF16).ap()
    acc_sp = nc.dram_tensor("acc_sp", [span, DM], F32).ap()
    w2qt_sp = nc.dram_tensor("w2qt_sp", [DFF, DM], BF16).ap()
    ag_in_t = nc.dram_tensor("ag_in", [CH, HB], F32)
    ag_out_t = nc.dram_tensor("ag_out", [CH * 8, HB], F32)
    ag_in = ag_in_t.ap()
    ag_out = ag_out_t.ap()

    with contextlib.ExitStack() as ctx:
        E = ctx.enter_context
        ident = E(nc.sbuf_tensor("ident_sb", [CH, CH], BF16))
        maskT = E(nc.sbuf_tensor("maskT_sb", [CH, CH], BF16))
        pmask = E(nc.sbuf_tensor("pmask_sb", [CH, 8], F32))
        ones_r = E(nc.sbuf_tensor("ones_r", [1, CH], F32))
        onesc = E(nc.sbuf_tensor("onesc", [CH, 1], F32))
        sm = E(nc.sbuf_tensor("sm", [CH, 16], F32))
        gacc = E(nc.sbuf_tensor("gacc", [CH, 72], F32))
        thrs = E(nc.sbuf_tensor("thrs", [CH, 4], F32))
        recb = E(nc.sbuf_tensor("recb", [CH, 2, 16], F32))
        cvals = E(nc.sbuf_tensor("cvals", [CH, 4], F32))
        b1col = E(nc.sbuf_tensor("b1col", [CH, 32], F32))
        b2bc = E(nc.sbuf_tensor("b2bc", [CH, DM], F32))
        xg = E(nc.sbuf_tensor("xg", [CH, 2, DM], F32))
        xnb = E(nc.sbuf_tensor("xnb", [CH, 2, DM], BF16))
        xnT = E(nc.sbuf_tensor("xnT", [CH, 2, 8, TG], BF16))
        va = E(nc.sbuf_tensor("va", [CH, 2, JPG, H * (D + 1)], BF16))
        ktok = E(nc.sbuf_tensor("ktok", [CH, 2, DM], BF16))
        poolP = E(nc.sbuf_tensor("poolP", [CH, 32768], BF16))
        poolQ = E(nc.sbuf_tensor("poolQ", [CH, 16384], BF16))
        w2s = E(nc.sbuf_tensor("w2s", [CH, NW2 * TG], BF16))
        wst = E(nc.sbuf_tensor("wst", [CH, 2, DM], F32))
        snapst = E(nc.sbuf_tensor("snapst", [CH, 2, HB], BF16))
        snld = E(nc.sbuf_tensor("snld", [CH, 2, HB], BF16))
        agst = E(nc.sbuf_tensor("agst", [CH, HB], F32))
        gr = E(nc.sbuf_tensor("gr", [CH, 2, HB], F32))
        pp = E(nc.sbuf_tensor("pp", [CH, 2, HB], F32))
        spre = E(nc.sbuf_tensor("spre", [CH, HB], BF16))
        saugT = E(nc.sbuf_tensor("saugT", [CH, 2, HB], BF16))
        mS = E(nc.sbuf_tensor("mS", [CH, 2, H, CH], BF16))
        yTb = E(nc.sbuf_tensor("yTb", [CH, 2, 8, CH], BF16))
        accb = E(nc.sbuf_tensor("accb", [CH, 2, DM], F32))
        eexp = E(nc.sbuf_tensor("eexp", [CH, 2, TG], F32))

        psA = E(nc.psum_tensor("psA", [CH, 1024], BF16))
        psB = E(nc.psum_tensor("psB", [CH, 1024], BF16))
        psC = E(nc.psum_tensor("psC", [CH, 512], F32))
        psD = E(nc.psum_tensor("psD", [CH, 512], F32))
        psE = E(nc.psum_tensor("psE", [CH, 512], F32))
        psF = E(nc.psum_tensor("psF", [CH, 512], F32))
        psG = E(nc.psum_tensor("psG", [CH, 512], F32))
        psH = E(nc.psum_tensor("psH", [CH, 512], F32))
        psAB = [psA, psB]
        psCD = [psC, psD]
        psEF = [psE, psF]
        psGH = [psG, psH]
        ps4 = [psC, psD, psE, psF]
        o2banks = [psE, psF, psG, psH]

        p = Plan()
        semnames = ["T", "V", "S", "G", "CC"]
        for qk in ("DS", "DA", "DG"):
            semnames += [f"{qk}{j}" for j in range(Plan.KDMA)]
        for s in semnames:
            p.sems[s] = E(nc.semaphore("sem" + s))
            p.cnt[s] = 0

        free = {}

        def guard(eng, key):
            p.waitm(eng, free.get(key))

        def wT(slot, dt, c0, n):
            return poolP[:, 8192 * slot + 1024 * dt + c0:8192 * slot + 1024 * dt + c0 + n]

        def w1qt(dt, f0, n):
            return poolP[:, 4096 * dt + f0:4096 * dt + f0 + n]

        def hT(ftile, t0, n):
            return poolQ[:, 512 * ftile + t0:512 * ftile + t0 + n]

        def qT_full(gp, ct):
            return poolQ[:, 4096 * gp + 512 * ct:4096 * gp + 512 * (ct + 1)]

        def qT_h(gp, h, j):
            base = 64 * (h % 2)
            return poolQ[base:base + 64,
                         4096 * gp + 512 * (h // 2) + 128 * j:
                         4096 * gp + 512 * (h // 2) + 128 * (j + 1)]

        def kT_full(gp, ct):
            return poolQ[:, 8192 + 4096 * gp + 512 * ct:8192 + 4096 * gp + 512 * (ct + 1)]

        def kT_h(gp, h, j):
            base = 64 * (h % 2)
            o = 8192 + 4096 * gp + 512 * (h // 2) + 128 * j
            return poolQ[base:base + 64, o:o + 128]

        def kTgrp(gp):
            return poolQ[:, 8192 + 4096 * gp:8192 + 4096 * (gp + 1)].rearrange(
                "p (ct t) -> p ct t", t=TG)

        def va_h(gp, j, h):
            return va[:, gp, j, 65 * h:65 * h + 65]

        def acc4(j):
            return accb[:, j // 2, 512 * (j % 2):512 * (j % 2) + 512]

        def w2slot(s):
            return w2s[:, 512 * s:512 * (s + 1)]

        def w2qv(par):
            return ktok[:, par, :]

        def xgh(j):
            return xg[:, j // 2, 512 * (j % 2):512 * (j % 2) + 512]

        def state_out(h):
            bank = psGH[h // 8]
            base = 64 * (h % 2)
            c0 = 65 * ((h // 2) % 4)
            return bank[base:base + 64, c0:c0 + 65], (h % 2 > 0)

        c_ident = p.dma("gpsimd", lambda g: g.dma_start(out=ident[:], in_=ident_in))
        p.dma("gpsimd", lambda g: g.dma_start(out=maskT[:], in_=maskT_in))
        c_pm = p.dma("gpsimd", lambda g: g.dma_start(out=pmask[:], in_=pmask_in))
        c_b1 = p.dma("gpsimd", lambda g: g.dma_start(out=b1col[:], in_=b1_in))
        c_b2 = p.dma("gpsimd", lambda g: g.dma_start(out=b2bc[0:1, :], in_=b2_in[None, :]))
        p.op("vector", lambda v: v.memset(ones_r[:], 1.0))
        m_ones = p.op("vector", lambda v: v.memset(onesc[:], 1.0), "V")
        p.op("vector", lambda v: v.memset(cvals[:, 0:1], 1e-5))
        p.op("vector", lambda v: v.memset(cvals[:, 1:2], -0.5))
        p.op("vector", lambda v: v.memset(cvals[:, 2:3], 0.5 / (DFF * DM)))
        cv_done = p.op("vector", lambda v: v.memset(cvals[:, 3:4], 0.5e-5), "V")
        v_zero = p.op("vector", lambda v: v.memset(pp[:, 0, :], 0.0), "V")

        p.waitm("tensor", c_b2)
        p.wait("tensor", "V", m_ones)
        p.op("tensor", lambda t: t.matmul(psC[:, 0:512], ones_r[:], b2bc[0:1, 0:512],
                                          start=True, stop=True))
        bm = p.op("tensor", lambda t: t.matmul(psD[:, 0:512], ones_r[:],
                                               b2bc[0:1, 512:1024],
                                               start=True, stop=True), "T")
        p.wait("vector", "T", bm)
        p.op("vector", lambda v: v.tensor_copy(b2bc[:, 0:512], psC[:, 0:512]))
        bv = p.op("vector", lambda v: v.tensor_copy(b2bc[:, 512:1024], psD[:, 0:512]),
                  "V")
        free[("psCD", 0)] = ("V", bv)
        free[("psCD", 1)] = ("V", bv)

        def prep_cast(w_ap, slot):
            cd = None
            for dt in range(8):
                par = dt % 2
                guard("sync", ("wst", par))
                d = p.dma("sync", lambda s, dt=dt, par=par, w_ap=w_ap: s.dma_start(
                    out=wst[:, par, :], in_=w_ap[128 * dt:128 * (dt + 1), :]))
                p.waitm("vector", d)
                cd = p.op("vector", lambda v, par=par, slot=slot, dt=dt: v.tensor_copy(
                    wT(slot, dt, 0, 1024), wst[:, par, :]), "V")
                free[("wst", par)] = ("V", cd)
            return cd

        prep_cast(wkT_in, 0)
        wprep_done = prep_cast(wvT_in, 1)

        gtiles = []
        for r in range(8):
            for fc2 in range(4):
                gtiles.append((W1T_in[128 * r:128 * (r + 1),
                                      1024 * fc2:1024 * (fc2 + 1)], 4 * r + fc2))
        for ft in range(32):
            gtiles.append((W2T_in[128 * ft:128 * (ft + 1), :], 32 + ft))

        spill_xnT = {}
        spill_kT = {}
        spill_va = {}
        snap_ev = {}
        xn_done = {}
        vvs = {}
        gam_s = {}
        gq_d = {}

        def emit_greduce(i, dumpar):
            ap_, col = gtiles[i]
            par_ = i % 2
            p.waitm("scalar", gq_d[i])
            rr = p.op("scalar", lambda s, par_=par_, col=col, dumpar=dumpar:
                      s.activation(accb[:, dumpar, :], wst[:, par_, :], AF.Abs,
                                   accum_out=gacc[:, col:col + 1]), "S")
            free[("wst", par_)] = ("S", rr)
            return rr

        p.wait("scalar", "V", v_zero)
        z0 = p.op("vector", lambda v: v.memset(snapst[:, 0, :], 0.0), "V")
        p.wait("scalar", "V", z0)
        sp0 = p.dma("scalar", lambda s: s.dma_start(out=snap_sp[0], in_=snapst[:, 0, :]))
        free[("snapst", 0)] = sp0

        for g in range(NG):
            gp = g % 2
            def emit_LN_B(bc):
                c, par, j = bc["c"], bc["par"], bc["j"]
                sl = 8 * par
                s1, s2 = sm[:, sl:sl + 1], sm[:, sl + 1:sl + 2]
                msq, var = sm[:, sl + 2:sl + 3], sm[:, sl + 3:sl + 4]
                lnv, rstd = sm[:, sl + 4:sl + 5], sm[:, sl + 5:sl + 6]
                nbias = sm[:, sl + 6:sl + 7]
                p.wait("vector", "V", bc["mqc"])
                p.wait("vector", "S", bc["sq"])
                vv = p.op("vector", lambda v, s2=s2, msq=msq, var=var:
                          v.scalar_tensor_tensor(var, s2, 1.0 / DM, msq,
                                                 ALU.mult, ALU.add), "V")
                vvs[c] = vv
                p.wait("scalar", "V", vv)
                p.wait("scalar", "V", cv_done)
                lnc = p.op("scalar", lambda s, var=var, lnv=lnv: s.activation(
                    lnv, var, AF.Ln, bias=cvals[:, 0:1]), "S")
                p.wait("scalar", "S", lnc)
                rs = p.op("scalar", lambda s, lnv=lnv, rstd=rstd: s.activation(
                    rstd, lnv, AF.Exp, scale=cvals[:, 1:2]), "S")
                p.wait("scalar", "S", rs)
                p.wait("vector", "S", rs)
                nb = p.op("vector", lambda v, s1=s1, rstd=rstd, nbias=nbias:
                          v.scalar_tensor_tensor(nbias, s1, -1.0 / DM, rstd,
                                                 ALU.mult, ALU.mult), "V")
                p.wait("scalar", "V", nb)
                guard("scalar", ("xnb", par))
                xn_done[c] = p.op("scalar", lambda s, par=par, rstd=rstd, nbias=nbias:
                                  s.activation(xnb[:, par, :], xg[:, par, :], AF.Identity,
                                               bias=nbias, scale=rstd), "S")
                free[("xg", par)] = ("S", xn_done[c])
                p.wait("tensor", "S", xn_done[c])
                p.waitm("tensor", c_ident)
                guard("tensor", ("xnTbuf", gp))
                tr = None
                for dt in range(8):
                    bank = psAB[dt % 2]
                    fslot = (dt // 2) % 4
                    guard("tensor", ("psAB", dt % 2))
                    tr = p.op("tensor", lambda t, bank=bank, fslot=fslot, dt=dt, par=par:
                              t.transpose(bank[:, 128 * fslot:128 * (fslot + 1)],
                                          xnb[:, par, 128 * dt:128 * (dt + 1)], ident[:]),
                              "T" if dt == 7 else None)
                free[("xnb", par)] = ("T", tr)
                p.wait("vector", "T", tr)
                guard("vector", ("xnTbuf", gp))
                for bi in range(2):
                    dst = xnT[:, gp, bi:8:2, 128 * j:128 * (j + 1)]
                    ev = p.op("vector", lambda v, bi=bi, dst=dst: v.tensor_copy(
                        dst, psAB[bi][:, 0:512]), "V")
                    free[("psAB", bi)] = ("V", ev)

            pendLN = None
            for j in range(JPG):
                c = JPG * g + j
                par = c % 2
                guard("sync", ("xg", par))
                d = p.dma("sync", lambda s, c=c, par=par: s.dma_start(
                    out=xg[:, par, :], in_=x_in[CH * c:CH * (c + 1), :]))
                p.waitm("vector", d)
                p.waitm("scalar", d)
                sl = 8 * par
                s1 = sm[:, sl:sl + 1]
                s2 = sm[:, sl + 1:sl + 2]
                msq = sm[:, sl + 2:sl + 3]
                r1c = p.op("vector", lambda v, par=par, s1=s1: v.reduce_sum(
                    s1, xg[:, par, :], axis=AX), "V")
                p.wait("vector", "V", r1c)
                guard("scalar", ("accb0",))
                if c - 2 in vvs:
                    p.wait("scalar", "V", vvs[c - 2])
                if c - 2 in gam_s:
                    p.wait("scalar", "S", gam_s[c - 2])
                sq = p.op("scalar", lambda s, par=par, s2=s2: s.activation(
                    accb[:, par, :], xg[:, par, :], AF.Square, accum_out=s2), "S")
                mqc = p.op("vector", lambda v, s1=s1, msq=msq: v.scalar_tensor_tensor(
                    msq, s1, -1.0 / (DM * DM), s1, ALU.mult, ALU.mult), "V")
                ctxLN = {"c": c, "par": par, "j": j, "sq": sq, "mqc": mqc}
                if pendLN is not None:
                    emit_LN_B(pendLN)
                pendLN = ctxLN
            emit_LN_B(pendLN)
            xnT_done = p.now("V")
            p.wait("scalar", "V", xnT_done)
            spill_xnT[g] = p.dma("scalar", lambda s, g=g, gp=gp: s.dma_start(
                out=xnT_sp.rearrange("(dt p) T -> p dt T", p=CH)[:, :, TG * g:TG * (g + 1)],
                in_=xnT[:, gp, :, :].rearrange("p a b -> p (a b)")))

            p.wait("tensor", "V", xnT_done)
            p.wait("tensor", "V", wprep_done)
            for ct in range(8):
                bank = psCD[ct % 2]
                guard("tensor", ("psCD", ct % 2))
                guard("tensor", ("kTbuf", gp))
                for dt in range(8):
                    p.op("tensor", lambda t, bank=bank, ct=ct, dt=dt, gp=gp: t.matmul(
                        bank[:, 0:512], wT(0, dt, 128 * ct, 128), xnT[:, gp, dt, 0:512],
                        start=(dt == 0), stop=(dt == 7)), "T" if dt == 7 else None)
                mmT = p.now("T")
                ep = ct % 2
                p.wait("scalar", "T", mmT)
                guard("scalar", ("eexp", ep))
                ex = p.op("scalar", lambda s, bank=bank, ep=ep: s.activation(
                    eexp[:, ep, :], bank[:, 0:512], AF.Exp), "S")
                p.wait("vector", "S", ex)
                mnc = p.op("vector", lambda v, ep=ep: v.tensor_scalar(
                    eexp[:, ep, :], eexp[:, ep, :], 1.0, None, ALU.min), "V")
                p.wait("vector", "V", mnc)
                guard("vector", ("kTbuf", gp))
                kev = p.op("vector", lambda v, bank=bank, ep=ep, ct=ct, gp=gp:
                           v.scalar_tensor_tensor(kT_full(gp, ct), bank[:, 0:512], 0.0,
                                                  eexp[:, ep, :], ALU.max, ALU.add), "V")
                free[("eexp", ep)] = ("V", kev)
                free[("psCD", ct % 2)] = ("V", kev)
            kT_all = p.now("V")
            p.wait("scalar", "V", kT_all)
            spill_kT[g] = p.dma("scalar", lambda s, g=g, gp=gp: s.dma_start(
                out=kT_sp.rearrange("(ct p) T -> p ct T", p=CH)[:, :, TG * g:TG * (g + 1)],
                in_=kTgrp(gp)))

            guard("vector", ("vabuf", gp))
            on = p.op("vector", lambda v, gp=gp: v.memset(
                va[:, gp, :, :].rearrange("p j (h e) -> p j h e", e=D + 1)[:, :, :, D:D + 1],
                1.0), "V")

            for j in range(JPG):
                c = JPG * g + j
                par = c % 2
                guard("tensor", ("ktok", par))
                guard("tensor", ("psEF", 0))
                guard("tensor", ("psEF", 1))
                for dt in range(8):
                    for half in range(2):
                        p.op("tensor", lambda t, dt=dt, half=half, gp=gp, j=j: t.matmul(
                            psEF[half][:, 0:512], xnT[:, gp, dt, 128 * j:128 * (j + 1)],
                            wT(1, dt, 512 * half, 512), start=(dt == 0), stop=(dt == 7)),
                            "T" if (dt == 7 and half == 1) else None)
                kvT = p.now("T")
                p.wait("tensor", "V", kT_all)
                tr2 = None
                for ct in range(8):
                    bank = psAB[ct % 2]
                    fslot = (ct // 2) % 4
                    guard("tensor", ("psAB", ct % 2))
                    tr2 = p.op("tensor", lambda t, bank=bank, fslot=fslot, ct=ct,
                               gp=gp, j=j: t.transpose(
                                   bank[:, 128 * fslot:128 * (fslot + 1)],
                                   kT_full(gp, ct)[:, 128 * j:128 * (j + 1)], ident[:]),
                               "T" if ct == 7 else None)
                p.wait("vector", "T", tr2)
                for bi in range(2):
                    dst = ktok[:, par, :].rearrange("p (a b) -> p a b", b=128)[:, bi:8:2, :]
                    kv = p.op("vector", lambda v, bi=bi, dst=dst: v.tensor_copy(
                        dst, psAB[bi][:, 0:512]), "V")
                    free[("psAB", bi)] = ("V", kv)
                p.wait("vector", "T", kvT)
                p.wait("vector", "V", on)
                for half in range(2):
                    dst = va[:, gp, j, :].rearrange("p (h e) -> p h e", e=D + 1)[
                        :, 8 * half:8 * (half + 1), 0:D]
                    vv1 = p.op("vector", lambda v, half=half, dst=dst: v.tensor_copy(
                        dst, psEF[half][:, 0:512]), "V")
                    free[("psEF", half)] = ("V", vv1)
                ktv = p.now("V")

                p.wait("tensor", "V", ktv)
                if c > 0:
                    p.wait("tensor", "V", snap_ev[c - 1])
                if c == 0:
                    for bi in range(2):
                        p.op("tensor", lambda t, bi=bi: t.matmul(
                            psGH[bi][:, 500:501], ident[:], ident[:, 0:1],
                            start=True, stop=False, skip_group_check=True))
                for h in (0, 8, 1, 9, 2, 10, 3, 11, 4, 12, 5, 13, 6, 14, 7, 15):
                    o, up = state_out(h)
                    p.op("tensor", lambda t, h=h, o=o, up=up, par=par, gp=gp, j=j:
                         t.matmul(o, ktok[:, par, 64 * h:64 * (h + 1)], va_h(gp, j, h),
                                  start=False, stop=False, skip_group_check=True,
                                  tile_position=(0, 64) if up else None),
                         "T" if h == 15 else None)
                if c == NC - 1:
                    for bi in range(2):
                        p.op("tensor", lambda t, bi=bi: t.matmul(
                            psGH[bi][:, 500:501], ident[:], ident[:, 0:1],
                            start=False, stop=True, skip_group_check=True))
                st_done = p.now("T")
                free[("ktok", par)] = ("T", st_done)
                p.wait("vector", "T", st_done)
                spar = (c + 1) % 2
                guard("vector", ("snapst", spar))
                for bi in range(2):
                    p.op("vector", lambda v, bi=bi, spar=spar: v.tensor_copy(
                        snapst[:, spar, 260 * bi:260 * (bi + 1)], psGH[bi][:, 0:260]),
                        "V" if bi == 1 else None)
                snap_ev[c] = p.now("V")
                p.wait("scalar", "V", snap_ev[c])
                sd = p.dma("scalar", lambda s, c=c, spar=spar: s.dma_start(
                    out=snap_sp[c + 1], in_=snapst[:, spar, :]))
                free[("snapst", spar)] = sd
                pt = c
                if pt > 0:
                    if pt - 2 in gam_s:
                        p.wait("scalar", "S", gam_s[pt - 2])
                    p.wait("scalar", "V", vvs[pt])
                    for i in (2 * pt - 2, 2 * pt - 1):
                        gam_s[pt] = emit_greduce(i, pt % 2)
                for i in (2 * pt, 2 * pt + 1):
                    ap_, _col = gtiles[i]
                    par_ = i % 2
                    qeng = "scalar" if par_ == 0 else "gpsimd"
                    guard(qeng, ("wst", par_))
                    gq_d[i] = p.dma(qeng, lambda s, ap_=ap_, par_=par_: s.dma_start(
                        out=wst[:, par_, :], in_=ap_))
            p.wait("scalar", "V", p.now("V"))
            spill_va[g] = p.dma("scalar", lambda s, g=g, gp=gp: s.dma_start(
                out=va_sp[TG * g:TG * (g + 1), :].rearrange("(j p) n -> p j n", p=CH),
                in_=va[:, gp, :, :]))
            free[("vabuf", gp)] = spill_va[g]
            free[("xnTbuf", gp)] = spill_xnT[g]
            free[("kTbuf", gp)] = spill_kT[g]

        p.wait("scalar", "V", vvs[NC - 1])
        for i in (62, 63):
            gam_s[NC] = emit_greduce(i, NC % 2)

        p1T = p.now("T")

        p.wait("vector", "T", p1T)
        for bi in range(2):
            p.op("vector", lambda v, bi=bi: v.tensor_copy(
                agst[:, 260 * bi:260 * (bi + 1)], psGH[bi][:, 0:260]),
                "V" if bi == 1 else None)
        ag_ready = p.now("V")
        p.wait("gpsimd", "V", ag_ready)
        dg = p.dma("gpsimd", lambda g: g.dma_start(out=ag_in[:], in_=agst[:]))
        p.waitm("gpsimd", dg)
        p.op("gpsimd", lambda g: g.collective_compute(
            "AllGather", ALU.bypass, replica_groups=[list(range(8))],
            ins=[ag_in_t.ap().opt()], outs=[ag_out_t.ap().opt()]), "CC")

        g1, g2 = gacc[:, 64:65], gacc[:, 65:66]
        p.wait("vector", "V", p.now("V"))
        p.wait("vector", "S", p.now("S"))
        p.wait("vector", "G", p.now("G"))
        gv1 = p.op("vector", lambda v: v.reduce_sum(g1, gacc[:, 0:32], axis=AX), "V")
        gv = p.op("vector", lambda v: v.reduce_sum(g2, gacc[:, 32:64], axis=AX), "V")
        p.wait("tensor", "V", gv)
        p.wait("tensor", "V", ag_ready)
        p.op("tensor", lambda t: t.matmul(psG[0:1, 0:1], g1, onesc[:],
                                          start=True, stop=True))
        tg2 = p.op("tensor", lambda t: t.matmul(psG[0:1, 1:2], g2, onesc[:],
                                                start=True, stop=True), "T")
        p.wait("scalar", "T", tg2)
        p.op("scalar", lambda s: s.activation(
            gacc[0:1, 66:67], psG[0:1, 0:1], AF.Identity, scale=cvals[0:1, 2:3],
            bias=cvals[0:1, 3:4]))
        t2 = p.op("scalar", lambda s: s.activation(
            gacc[0:1, 67:68], psG[0:1, 1:2], AF.Identity, scale=cvals[0:1, 2:3],
            bias=cvals[0:1, 3:4]),
            "S")
        p.wait("tensor", "S", t2)
        p.op("tensor", lambda t: t.matmul(psG[:, 2:3], ones_r[:], gacc[0:1, 66:67],
                                          start=True, stop=True))
        bc = p.op("tensor", lambda t: t.matmul(psG[:, 3:4], ones_r[:], gacc[0:1, 67:68],
                                               start=True, stop=True), "T")
        p.wait("vector", "T", bc)
        p.op("vector", lambda v: v.tensor_copy(thrs[:, 0:1], psG[:, 2:3]))
        p.op("vector", lambda v: v.tensor_scalar(thrs[:, 1:2], psG[:, 2:3], -1.0, None,
                                                 ALU.mult))
        p.op("vector", lambda v: v.tensor_copy(thrs[:, 2:3], psG[:, 3:4]))
        thr_done = p.op("vector", lambda v: v.tensor_scalar(
            thrs[:, 3:4], psG[:, 3:4], -1.0, None, ALU.mult), "V")

        p.wait("sync", "T", p1T)
        p.waitm("sync", spill_xnT[0])
        p.waitm("sync", spill_xnT[NG - 2])
        dxs = {}
        dxs[0] = p.dma("sync", lambda s: s.dma_start(
            out=xnT[:, 0, :, :].rearrange("p a b -> p (a b)"),
            in_=xnT_sp.rearrange("(dt p) T -> p dt T", p=CH)[:, :, 0:TG]))

        p.wait("vector", "V", thr_done)
        w1bufs = [wst[:, 0, :], wst[:, 1, :], xg[:, 0, :], xg[:, 1, :]]
        w1keys = [("wst", 0), ("wst", 1), ("xgq", 0), ("xgq", 1)]
        w1fc = {}
        for i in range(32):
            fc2, r = i // 8, i % 8
            b4 = i % 4
            buf = w1bufs[b4]
            qe1 = ("sync", "scalar")[i % 2]
            guard(qe1, w1keys[b4])
            d = p.dma(qe1, lambda s, r=r, fc2=fc2, buf=buf: s.dma_start(
                out=buf,
                in_=W1T_in[128 * r:128 * (r + 1), 1024 * fc2:1024 * (fc2 + 1)]))
            p.waitm("vector", d)
            guard("vector", ("accb1",))
            q1c = p.op("vector", lambda v, buf=buf: v.tensor_scalar(
                accb[:, 1, :], buf, thrs[:, 1:2], -1.0, ALU.is_lt,
                ALU.mult), "V")
            p.wait("vector", "V", q1c)
            qz = p.op("vector", lambda v, buf=buf, r=r, fc2=fc2: v.scalar_tensor_tensor(
                w1qt(r, 1024 * fc2, 1024), buf, thrs[:, 0:1], accb[:, 1, :],
                ALU.is_gt, ALU.add), "V")
            free[w1keys[b4]] = ("V", qz)
            free[("accb1",)] = ("V", qz)
            if r == 7:
                w1fc[fc2] = p.now("V")
        w1_ready = p.now("V")

        w2bufs = [eexp[:, :, :].rearrange("p a b -> p (a b)"),
                  gr[:, :, :].rearrange("p a b -> p (a b)")[:, 0:1024]]
        w2_spill_last = None
        w2sp = {}
        for ft in range(32):
            par = ft % 2
            buf = w2bufs[par]
            qe2 = ("sync", "scalar")[ft % 2]
            guard(qe2, ("w2ld", par))
            if ft < 2:
                p.waitm(qe2, free.get(("eexp", 0)))
                p.waitm(qe2, free.get(("eexp", 1)))
            d = p.dma(qe2, lambda s, ft=ft, buf=buf: s.dma_start(
                out=buf, in_=W2T_in[128 * ft:128 * (ft + 1), :]))
            p.waitm("vector", d)
            guard("vector", ("accbq",))
            q1c = p.op("vector", lambda e, buf=buf: e.tensor_scalar(
                accb[:, 0, :], buf, thrs[:, 3:4], -1.0, ALU.is_lt,
                ALU.mult), "V")
            p.wait("vector", "V", q1c)
            guard("vector", ("w2q", par))
            qz = p.op("vector", lambda e, buf=buf, par=par: e.scalar_tensor_tensor(
                w2qv(par), buf, thrs[:, 2:3], accb[:, 0, :],
                ALU.is_gt, ALU.add), "V")
            free[("w2ld", par)] = ("V", qz)
            free[("accbq",)] = ("V", qz)
            p.wait("gpsimd", "V", qz)
            w2_spill_last = p.dma("gpsimd", lambda s, ft=ft, par=par: s.dma_start(
                out=w2qt_sp[128 * ft:128 * (ft + 1), :], in_=w2qv(par)))
            w2sp[ft] = w2_spill_last
            free[("w2q", par)] = w2_spill_last
        w2q_done = ("V", p.now("V"))
        free[("eexp", 0)] = w2q_done
        free[("eexp", 1)] = w2q_done
        free[("gr", 0)] = w2q_done
        free[("gr", 1)] = w2q_done
        free[("ybuf", 0)] = free.get(("w2q", 0))
        free[("ybuf", 1)] = free.get(("w2q", 1))
        for j in range(4):
            free[("xgh", j)] = ("V", w1_ready)
        free[("accs", 0)] = w2q_done
        free[("accs", 1)] = w2q_done
        free[("accs", 2)] = ("V", w1_ready)
        free[("accs", 3)] = ("V", w1_ready)
        free[("accb0",)] = w2q_done

        w2i = 0
        for g in range(NG):
            gp = g % 2
            p.waitm("tensor", dxs[g])
            if g == 0:
                p.waitm("scalar", spill_kT[NG - 1])
            hvs = {}
            for ftile in range(32):
                bank = psCD[ftile % 2]
                if g == 0:
                    p.wait("tensor", "V", w1fc[ftile // 8])
                guard("tensor", ("psCD", ftile % 2))
                for dt in range(8):
                    p.op("tensor", lambda t, bank=bank, ftile=ftile, dt=dt, gp=gp: t.matmul(
                        bank[:, 0:512], w1qt(dt, 128 * ftile, 128), xnT[:, gp, dt, 0:512],
                        start=(dt == 0), stop=(dt == 7)), "T" if dt == 7 else None)
                mmn = p.now("T")
                p.wait("scalar", "T", mmn)
                hv = p.op("scalar", lambda s, bank=bank, ftile=ftile: s.activation(
                    hT(ftile, 0, 512), bank[:, 0:512], AF.Relu,
                    bias=b1col[:, ftile:ftile + 1]), "S")
                hvs[ftile] = hv
                free[("psCD", ftile % 2)] = ("S", hv)
            free[("xnTbuf2", gp)] = ("T", p.now("T"))
            if g + 1 < NG:
                p.waitm("sync", spill_xnT[g + 1])
                guard("sync", ("xnTbuf2", (g + 1) % 2))
                dxs[g + 1] = p.dma("sync", lambda s, g=g: s.dma_start(
                    out=xnT[:, (g + 1) % 2, :, :].rearrange("p a b -> p (a b)"),
                    in_=xnT_sp.rearrange("(dt p) T -> p dt T", p=CH)[
                        :, :, TG * (g + 1):TG * (g + 2)]))

            for dmh in range(2):
                pb = [psE, psF, psG, psH] if dmh == 0 else [psC, psD, psE, psF]
                pbi = [2, 3, 4, 5] if dmh == 0 else [0, 1, 2, 3]
                if dmh == 1:
                    free[("o2b", 0)] = free.get(("psCD", 0))
                    free[("o2b", 1)] = free.get(("psCD", 1))
                dxxs = {}
                for j in range(JPG):
                    c = JPG * g + j
                    guard("scalar", ("xgh", j))
                    dxxs[j] = p.dma("scalar", lambda s, c=c, j=j, dmh=dmh: s.dma_start(
                        out=xgh(j), in_=x_in[CH * c:CH * (c + 1),
                                             512 * dmh:512 * (dmh + 1)]))
                for ft in range(32):
                    slot = w2i % NW2
                    qeng = "sync" if w2i % 2 == 0 else "gpsimd"
                    w2i += 1
                    guard(qeng, ("w2s", slot))
                    if g == 0 and dmh == 0:
                        p.waitm(qeng, w2sp[ft])
                    dw = p.dma(qeng, lambda s, ft=ft, dmh=dmh, slot=slot: s.dma_start(
                        out=w2slot(slot),
                        in_=w2qt_sp[128 * ft:128 * (ft + 1),
                                    512 * dmh:512 * (dmh + 1)]))
                    p.waitm("tensor", dw)
                    p.wait("tensor", "S", hvs[ft])
                    lm = None
                    for j in range(JPG):
                        if ft == 0:
                            guard("tensor", ("o2b", pbi[j]))
                        lm = p.op("tensor", lambda t, ft=ft, j=j, slot=slot, pb=pb: t.matmul(
                            pb[j][:, 0:512], hT(ft, 128 * j, 128), w2slot(slot),
                            start=(ft == 0), stop=(ft == 31)),
                            "T" if j == 3 else None)
                    free[("w2s", slot)] = ("T", lm)
                mm2p = p.now("T")
                for j in range(JPG):
                    c = JPG * g + j
                    p.waitm("vector", dxxs[j])
                    p.wait("vector", "T", mm2p)
                    guard("vector", ("accs", j))
                    a1 = p.op("vector", lambda v, j=j, dmh=dmh, pb=pb: v.tensor_tensor(
                        acc4(j), pb[j][:, 0:512],
                        b2bc[:, 512 * dmh:512 * (dmh + 1)], ALU.add), "V")
                    p.wait("vector", "V", a1)
                    a2 = p.op("vector", lambda v, j=j: v.tensor_tensor(
                        acc4(j), acc4(j), xgh(j), ALU.add), "V")
                    free[("o2b", pbi[j])] = ("V", a2)
                    free[("xgh", j)] = ("V", a2)
                    p.wait("scalar", "V", a2)
                    sa = p.dma("scalar", lambda s, c=c, j=j, dmh=dmh: s.dma_start(
                        out=acc_sp[CH * c:CH * (c + 1), 512 * dmh:512 * (dmh + 1)],
                        in_=acc4(j)))
                    free[("accs", j)] = sa
                if dmh == 1:
                    free[("psCD", 0)] = free[("o2b", 0)]
                    free[("psCD", 1)] = free[("o2b", 1)]
        ffn_T = p.now("T")
        free[("psGH", 0)] = free.get(("o2b", 4))
        free[("psGH", 1)] = free.get(("o2b", 5))
        for b_ in range(4):
            free[("nbank", b_)] = free.get(("o2b", b_))
            free[("sslot", b_)] = free.get(("o2b", b_))

        p.wait("tensor", "V", p.now("V"))
        p.wait("tensor", "S", p.now("S"))
        p.wait("vector", "T", ffn_T)
        prep_cast(wqT_in, 2)
        wqp_done = prep_cast(wpT_in, 3)
        p.wait("sync", "T", ffn_T)
        p.dma_barrier("sync")

        p.wait("gpsimd", "CC", 1)
        for r in range(8):
            guard("gpsimd", ("gr", r % 2))
            dgr = p.dma("gpsimd", lambda g, r=r: g.dma_start(
                out=gr[:, r % 2, :], in_=ag_out[CH * r:CH * (r + 1), :]))
            p.waitm("vector", dgr)
            p.wait("vector", "V", v_zero)
            p.waitm("vector", c_pm)
            cmb = p.op("vector", lambda v, r=r: v.scalar_tensor_tensor(
                pp[:, (r + 1) % 2, :], gr[:, r % 2, :], pmask[:, r:r + 1],
                pp[:, r % 2, :], ALU.mult, ALU.add), "V")
            p.wait("vector", "V", cmb)
            free[("gr", r % 2)] = ("V", cmb)
        prefix_done = p.op("vector", lambda v: v.tensor_copy(spre[:], pp[:, 0, :]), "V")

        y_of = {}

        for g in range(NG):
            gp = g % 2
            guard("sync", ("p2xnT", gp))
            guard("sync", ("p2kT", gp))
            guard("sync", ("p2va", gp))
            p.waitm("sync", spill_kT[g])
            p.waitm("sync", spill_va[g])
            dq = p.dma("sync", lambda s, g=g, gp=gp: s.dma_start(
                out=xnT[:, gp, :, :].rearrange("p a b -> p (a b)"),
                in_=xnT_sp.rearrange("(dt p) T -> p dt T", p=CH)[:, :, TG * g:TG * (g + 1)]))
            dk = p.dma("sync", lambda s, g=g, gp=gp: s.dma_start(
                out=kTgrp(gp),
                in_=kT_sp.rearrange("(ct p) T -> p ct T", p=CH)[:, :, TG * g:TG * (g + 1)]))
            dv = p.dma("sync", lambda s, g=g, gp=gp: s.dma_start(
                out=va[:, gp, :, :],
                in_=va_sp[TG * g:TG * (g + 1), :].rearrange("(j p) n -> p j n", p=CH)))
            p.waitm("tensor", dq)
            p.wait("tensor", "V", wqp_done)
            if g == 0:
                p.wait("tensor", "T", ffn_T)
            for ct in range(8):
                bank = psGH[ct % 2]
                guard("tensor", ("psGH", ct % 2))
                guard("tensor", ("p2qT", gp))
                for dt in range(8):
                    p.op("tensor", lambda t, bank=bank, ct=ct, dt=dt, gp=gp: t.matmul(
                        bank[:, 0:512], wT(2, dt, 128 * ct, 128), xnT[:, gp, dt, 0:512],
                        start=(dt == 0), stop=(dt == 7)), "T" if dt == 7 else None)
                mmq = p.now("T")
                ep = ct % 2
                p.wait("scalar", "T", mmq)
                guard("scalar", ("eexp", ep))
                ex = p.op("scalar", lambda s, bank=bank, ep=ep: s.activation(
                    eexp[:, ep, :], bank[:, 0:512], AF.Exp), "S")
                p.wait("vector", "S", ex)
                mnc = p.op("vector", lambda v, ep=ep: v.tensor_scalar(
                    eexp[:, ep, :], eexp[:, ep, :], 1.0, None, ALU.min), "V")
                p.wait("vector", "V", mnc)
                qe = p.op("vector", lambda v, bank=bank, ep=ep, ct=ct, gp=gp:
                          v.scalar_tensor_tensor(qT_full(gp, ct), bank[:, 0:512], 0.0,
                                                 eexp[:, ep, :], ALU.max, ALU.add), "V")
                free[("eexp", ep)] = ("V", qe)
                free[("psGH", ct % 2)] = ("V", qe)
            q_done = p.now("V")

            for j in range(JPG):
                c = JPG * g + j
                par = c % 2
                guard("sync", ("snld", par))
                dsn = p.dma("sync", lambda s, c=c, par=par: s.dma_start(
                    out=snld[:, par, :], in_=snap_sp[c]))
                guard("sync", ("accbr", par))
                guard("sync", ("accbr2", par))
                dac = p.dma("sync", lambda s, c=c, par=par: s.dma_start(
                    out=accb[:, par, :], in_=acc_sp[CH * c:CH * (c + 1), :]))
                p.waitm("vector", dsn)
                p.wait("vector", "V", prefix_done)
                guard("vector", ("saugT", par))
                sg = p.op("vector", lambda v, par=par: v.tensor_tensor(
                    saugT[:, par, :], snld[:, par, :], spre[:], ALU.add), "V")
                free[("snld", par)] = ("V", sg)

                p.waitm("tensor", dk)
                p.waitm("tensor", dv)
                p.wait("tensor", "V", q_done)
                p.wait("tensor", "V", sg)
                num_last = None
                for w in range(4):
                    scm = None
                    for b in range(4):
                        h = 4 * w + b
                        guard("tensor", ("sslot", b))
                        scm = p.op("tensor", lambda t, b=b, h=h, j=j, gp=gp:
                                   t.matmul(ps4[b][:, 260:388], kT_h(gp, h, j),
                                            qT_h(gp, h, j), start=True, stop=True),
                                   "T" if b == 3 else None)
                    p.wait("vector", "T", scm)
                    msv = None
                    for b in range(4):
                        h = 4 * w + b
                        guard("vector", ("mS", h))
                        msv = p.op("vector", lambda v, b=b, h=h, par=par:
                                   v.tensor_tensor(mS[:, par, h, :], ps4[b][:, 260:388],
                                                   maskT[:], ALU.mult),
                                   "V" if b == 3 else None)
                    for b in range(4):
                        free[("sslot", b)] = ("V", msv)
                    p.wait("tensor", "V", msv)
                    for b in range(4):
                        h = 4 * w + b
                        guard("tensor", ("nbank", b))
                        guard("tensor", ("nbankv", b))
                        o = ps4[b][:, 65 * w:65 * w + 65]
                        sb = 64 * (h % 2)
                        p.op("tensor", lambda t, o=o, h=h, j=j, gp=gp, sb=sb, par=par:
                             t.matmul(o, qT_h(gp, h, j),
                                      saugT[sb:sb + 64, par,
                                            65 * (h // 2):65 * (h // 2) + 65],
                                      start=True, stop=False))
                        num_last = p.op("tensor", lambda t, o=o, h=h, par=par, gp=gp, j=j:
                                        t.matmul(o, mS[:, par, h, :], va_h(gp, j, h),
                                                 start=False, stop=True),
                                        "T" if b == 3 else None)
                        free[("mS", h)] = ("T", p.now("T"))
                p.wait("vector", "T", num_last)
                guard("vector", ("recb", par))
                guard("vector", ("recbs", par))
                dxc = None
                for b in range(4):
                    dxc = p.op("vector", lambda v, b=b, par=par: v.tensor_scalar(
                        recb[:, par, 4 * b:4 * b + 4].rearrange("p (s o) -> p s o", o=1),
                        ps4[b][:, 64:324].rearrange(
                            "p (s o) -> p s o", o=65)[:, :, 0:1],
                        1e-6, None, ALU.add), "V" if b == 3 else None)
                p.wait("vector", "V", dxc)
                rcd = p.op("vector", lambda v, par=par: v.reciprocal(
                    recb[:, par, :], recb[:, par, :]), "V")
                p.wait("vector", "V", rcd)
                p.wait("scalar", "V", rcd)
                guard("scalar", ("ybuf", par))
                guard("vector", ("ybuf", par))
                pysc = {}
                for hp in range(8):
                    yeng = "scalar" if hp % 2 == 0 else "vector"
                    for hh in range(2):
                        h = 2 * hp + hh
                        bank = ps4[h % 4]
                        s_ = h // 4
                        if yeng == "scalar":
                            p.op("scalar", lambda s, bank=bank, h=h, s_=s_, par=par:
                                 s.activation(ktok[:, par, 64 * h:64 * (h + 1)],
                                              bank[:, 65 * s_:65 * s_ + 64],
                                              AF.Identity,
                                              scale=recb[:, par, 4 * (h % 4) + s_:
                                                         4 * (h % 4) + s_ + 1]),
                                 "S")
                        else:
                            p.op("vector", lambda v, bank=bank, h=h, s_=s_, par=par:
                                 v.tensor_scalar(ktok[:, par, 64 * h:64 * (h + 1)],
                                                 bank[:, 65 * s_:65 * s_ + 64],
                                                 recb[:, par, 4 * (h % 4) + s_:
                                                      4 * (h % 4) + s_ + 1],
                                                 None, ALU.mult), "V")
                    pysc[hp] = ("S" if yeng == "scalar" else "V",
                                p.now("S" if yeng == "scalar" else "V"))
                ysc_n = p.now("S")
                yscv_n = p.now("V")
                for b in range(4):
                    free[("nbank", b)] = ("S", ysc_n)
                    free[("nbankv", b)] = ("V", yscv_n)
                free[("recb", par)] = ("V", yscv_n)
                free[("recbs", par)] = ("S", ysc_n)
                tr = None
                for et in range(8):
                    p.wait("tensor", pysc[et][0], pysc[et][1])
                    bank = psAB[et % 2]
                    fslot = (et // 2) % 4
                    guard("tensor", ("psAB", et % 2))
                    tr = p.op("tensor", lambda t, bank=bank, fslot=fslot, et=et, par=par:
                              t.transpose(bank[:, 128 * fslot:128 * (fslot + 1)],
                                          ktok[:, par, 128 * et:128 * (et + 1)], ident[:]),
                              "T" if et == 7 else None)
                free[("ybuf", par)] = ("T", tr)
                p.wait("vector", "T", tr)
                for bi in range(2):
                    dst = yTb[:, par, bi:8:2, :]
                    ev = p.op("vector", lambda v, bi=bi, dst=dst: v.tensor_copy(
                        dst, psAB[bi][:, 0:512]), "V")
                    free[("psAB", bi)] = ("V", ev)
                yT_done = p.now("V")
                p.wait("tensor", "V", yT_done)
                guard("tensor", ("psGH", 0))
                guard("tensor", ("psGH", 1))
                prj = None
                for et in range(8):
                    for eh in range(2):
                        prj = p.op("tensor", lambda t, et=et, eh=eh, par=par: t.matmul(
                            psGH[eh][:, 0:512], yTb[:, par, et, :],
                            wT(3, et, 512 * eh, 512),
                            start=(et == 0), stop=(et == 7)),
                            "T" if (et == 7 and eh == 1) else None)
                free[("yTb", par)] = ("T", prj)
                p.wait("vector", "T", prj)
                p.waitm("vector", dac)
                guard("vector", ("accbr2", par))
                for eh in range(2):
                    p.op("vector", lambda v, eh=eh, par=par: v.tensor_tensor(
                        accb[:, par, 512 * eh:512 * (eh + 1)], psGH[eh][:, 0:512],
                        accb[:, par, 512 * eh:512 * (eh + 1)], ALU.add),
                        "V" if eh == 1 else None)
                fout = p.now("V")
                free[("psGH", 0)] = ("V", fout)
                free[("psGH", 1)] = ("V", fout)
                p.wait("scalar", "V", fout)
                so = p.dma("scalar", lambda s, c=c, par=par: s.dma_start(
                    out=out_d[CH * c:CH * (c + 1), :], in_=accb[:, par, :]))
                free[("accbr", par)] = ("V", fout)
                free[("accbr2", par)] = so
                free[("saugT", par)] = ("T", p.now("T"))
            free[("p2qT", gp)] = ("T", p.now("T"))
            free[("p2xnT", gp)] = ("T", p.now("T"))
            free[("p2kT", gp)] = ("T", p.now("T"))
            free[("p2va", gp)] = ("T", p.now("T"))

        p.dma_barrier("gpsimd")

        with nc.Block() as block:
            @block.sync
            def _(sync):
                for fn in p.q["sync"]:
                    fn(sync)

            @block.vector
            def _(vector):
                for fn in p.q["vector"]:
                    fn(vector)

            @block.scalar
            def _(scalar):
                for fn in p.q["scalar"]:
                    fn(scalar)

            @block.tensor
            def _(tensor):
                for fn in p.q["tensor"]:
                    fn(tensor)

            @block.gpsimd
            def _(gpsimd):
                for fn in p.q["gpsimd"]:
                    fn(gpsimd)

    return nc


def make_inputs(c, x_full, w_attn, w_proj, W1, b1, W2, b2, span=4096):
    b, g = divmod(c, 4)
    m = np.zeros((CH, 8), np.float32)
    for j in range(8):
        if j // 4 == b and j % 4 < g:
            m[:, j] = 1.0
    maskT = np.tril(np.ones((CH, CH), np.float32)).T.copy()
    return {
        "x": np.ascontiguousarray(x_full[b, g * span:(g + 1) * span]),
        "wqT": np.ascontiguousarray(w_attn[0:DM].T),
        "wkT": np.ascontiguousarray(w_attn[DM:2 * DM].T),
        "wvT": np.ascontiguousarray(w_attn[2 * DM:3 * DM].T),
        "wpT": np.ascontiguousarray(w_proj.T),
        "W1T": np.ascontiguousarray(W1.T),
        "b1": np.ascontiguousarray(b1.reshape(32, CH).T),
        "W2T": np.ascontiguousarray(W2.T),
        "b2": np.ascontiguousarray(b2),
        "ident": np.eye(CH, dtype=np.float32),
        "maskT": maskT,
        "pmask": m,
    }


def kernel(L0_attn_x, L0_attn_w_attn, L0_attn_w_proj, L0_ffn_W1, L0_ffn_b1,
           L0_ffn_W2, L0_ffn_b2):
    x = np.asarray(L0_attn_x, dtype=np.float32)
    w_attn = np.asarray(L0_attn_w_attn, dtype=np.float32)
    w_proj = np.asarray(L0_attn_w_proj, dtype=np.float32)
    W1 = np.asarray(L0_ffn_W1, dtype=np.float32)
    b1 = np.asarray(L0_ffn_b1, dtype=np.float32)
    W2 = np.asarray(L0_ffn_W2, dtype=np.float32)
    b2 = np.asarray(L0_ffn_b2, dtype=np.float32)
    span = 4096
    nc = build(span)
    in_maps = [make_inputs(c, x, w_attn, w_proj, W1, b1, W2, b2, span)
               for c in range(8)]
    res = run_bass_kernel_spmd(nc, in_maps, core_ids=list(range(8)))
    out = np.empty((2, 16384, DM), np.float32)
    for c in range(8):
        b, g = divmod(c, 4)
        out[b, g * span:(g + 1) * span] = res.results[c]["out"]
    return out


# version 29
# speedup vs baseline: 1.1103x; 1.0046x over previous
import contextlib
import numpy as np
import concourse.bass as bass
import concourse.mybir as mybir
from concourse.bass_utils import run_bass_kernel_spmd

F32 = mybir.dt.float32
BF16 = mybir.dt.bfloat16
AF = mybir.ActivationFunctionType
ALU = mybir.AluOpType
AX = mybir.AxisListType.X

DM = 1024
H = 16
D = 64
CH = 128
TG = 512
DFF = 4096
HB = 8 * (D + 1)
NW2 = 6


class Plan:
    ENGS = ("sync", "vector", "scalar", "tensor", "gpsimd")
    QKEY = {"sync": "DS", "scalar": "DA", "gpsimd": "DG"}
    KDMA = 6

    def __init__(self):
        self.q = {e: [] for e in self.ENGS}
        self.cnt = {}
        self.sems = {}
        self.waited = {e: {} for e in self.ENGS}
        self.dmai = {"DS": 0, "DA": 0, "DG": 0}

    def op(self, eng, fn, inc=None, inc_n=1):
        if inc is None:
            self.q[eng].append(fn)
            return None
        self.cnt[inc] = self.cnt.get(inc, 0) + inc_n
        sem = self.sems[inc]

        def wrapped(e, fn=fn, sem=sem, n=inc_n):
            fn(e).then_inc(sem, n)

        self.q[eng].append(wrapped)
        return self.cnt[inc]

    def dma(self, eng, fn):
        qk = self.QKEY[eng]
        i = self.dmai[qk]
        self.dmai[qk] = i + 1
        name = f"{qk}{i % self.KDMA}"
        prev = 16 * (i // self.KDMA)
        if prev:
            self.wait(eng, name, prev)
        self.cnt[name] = self.cnt.get(name, 0) + 16
        sem = self.sems[name]

        def wrapped(e, fn=fn, sem=sem):
            fn(e).then_inc(sem, 16)

        self.q[eng].append(wrapped)
        return (name, self.cnt[name])

    def dma_barrier(self, eng):
        ms = []
        for qk in ("DS", "DA", "DG"):
            for j in range(self.KDMA):
                name = f"{qk}{j}"
                v = self.cnt.get(name, 0)
                if v:
                    self.wait(eng, name, v)
                    ms.append((name, v))
        return ms

    def wait(self, eng, sem_name, val):
        if not val:
            return
        if self.waited[eng].get(sem_name, 0) >= val:
            return
        self.waited[eng][sem_name] = val
        sem = self.sems[sem_name]

        def w(e, sem=sem, v=val):
            e.wait_ge(sem, v)

        self.q[eng].append(w)

    def waitm(self, eng, m):
        if m is not None:
            self.wait(eng, m[0], m[1])

    def now(self, sem_name):
        return self.cnt.get(sem_name, 0)


def build(span=4096):
    NG = span // TG
    NC = span // CH
    JPG = TG // CH
    nc = bass.Bass("TRN2", target_bir_lowering=False, debug=False)

    x_in = nc.dram_tensor("x", [span, DM], F32, kind="ExternalInput").ap()
    wqT_in = nc.dram_tensor("wqT", [DM, DM], F32, kind="ExternalInput").ap()
    wkT_in = nc.dram_tensor("wkT", [DM, DM], F32, kind="ExternalInput").ap()
    wvT_in = nc.dram_tensor("wvT", [DM, DM], F32, kind="ExternalInput").ap()
    wpT_in = nc.dram_tensor("wpT", [DM, DM], F32, kind="ExternalInput").ap()
    W1T_in = nc.dram_tensor("W1T", [DM, DFF], F32, kind="ExternalInput").ap()
    b1_in = nc.dram_tensor("b1", [CH, 32], F32, kind="ExternalInput").ap()
    W2T_in = nc.dram_tensor("W2T", [DFF, DM], F32, kind="ExternalInput").ap()
    b2_in = nc.dram_tensor("b2", [DM], F32, kind="ExternalInput").ap()
    ident_in = nc.dram_tensor("ident", [CH, CH], F32, kind="ExternalInput").ap()
    maskT_in = nc.dram_tensor("maskT", [CH, CH], F32, kind="ExternalInput").ap()
    pmask_in = nc.dram_tensor("pmask", [CH, 8], F32, kind="ExternalInput").ap()
    out_d = nc.dram_tensor("out", [span, DM], F32, kind="ExternalOutput").ap()

    xnT_sp = nc.dram_tensor("xnT_sp", [DM, span], BF16).ap()
    kT_sp = nc.dram_tensor("kT_sp", [DM, span], BF16).ap()
    va_sp = nc.dram_tensor("va_sp", [span, H * (D + 1)], BF16).ap()
    snap_sp = nc.dram_tensor("snap_sp", [NC + 1, CH, HB], BF16).ap()
    acc_sp = nc.dram_tensor("acc_sp", [span, DM], F32).ap()
    w2qt_sp = nc.dram_tensor("w2qt_sp", [DFF, DM], BF16).ap()
    ag_in_t = nc.dram_tensor("ag_in", [CH, HB], F32)
    ag_out_t = nc.dram_tensor("ag_out", [CH * 8, HB], F32)
    ag_in = ag_in_t.ap()
    ag_out = ag_out_t.ap()

    with contextlib.ExitStack() as ctx:
        E = ctx.enter_context
        ident = E(nc.sbuf_tensor("ident_sb", [CH, CH], BF16))
        maskT = E(nc.sbuf_tensor("maskT_sb", [CH, CH], BF16))
        pmask = E(nc.sbuf_tensor("pmask_sb", [CH, 8], F32))
        ones_r = E(nc.sbuf_tensor("ones_r", [1, CH], F32))
        onesc = E(nc.sbuf_tensor("onesc", [CH, 1], F32))
        sm = E(nc.sbuf_tensor("sm", [CH, 16], F32))
        gacc = E(nc.sbuf_tensor("gacc", [CH, 72], F32))
        thrs = E(nc.sbuf_tensor("thrs", [CH, 4], F32))
        recb = E(nc.sbuf_tensor("recb", [CH, 2, 16], F32))
        cvals = E(nc.sbuf_tensor("cvals", [CH, 4], F32))
        b1col = E(nc.sbuf_tensor("b1col", [CH, 32], F32))
        b2bc = E(nc.sbuf_tensor("b2bc", [CH, DM], F32))
        xg = E(nc.sbuf_tensor("xg", [CH, 2, DM], F32))
        xnb = E(nc.sbuf_tensor("xnb", [CH, 2, DM], BF16))
        xnT = E(nc.sbuf_tensor("xnT", [CH, 2, 8, TG], BF16))
        va = E(nc.sbuf_tensor("va", [CH, 2, JPG, H * (D + 1)], BF16))
        ktok = E(nc.sbuf_tensor("ktok", [CH, 2, DM], BF16))
        poolP = E(nc.sbuf_tensor("poolP", [CH, 32768], BF16))
        poolQ = E(nc.sbuf_tensor("poolQ", [CH, 16384], BF16))
        w2s = E(nc.sbuf_tensor("w2s", [CH, NW2 * TG], BF16))
        wst = E(nc.sbuf_tensor("wst", [CH, 2, DM], F32))
        snapst = E(nc.sbuf_tensor("snapst", [CH, 2, HB], BF16))
        snld = E(nc.sbuf_tensor("snld", [CH, 2, HB], BF16))
        agst = E(nc.sbuf_tensor("agst", [CH, HB], F32))
        gr = E(nc.sbuf_tensor("gr", [CH, 2, HB], F32))
        pp = E(nc.sbuf_tensor("pp", [CH, 2, HB], F32))
        spre = E(nc.sbuf_tensor("spre", [CH, HB], BF16))
        saugT = E(nc.sbuf_tensor("saugT", [CH, 2, HB], BF16))
        mS = E(nc.sbuf_tensor("mS", [CH, 2, H, CH], BF16))
        yTb = E(nc.sbuf_tensor("yTb", [CH, 2, 8, CH], BF16))
        accb = E(nc.sbuf_tensor("accb", [CH, 2, DM], F32))
        eexp = E(nc.sbuf_tensor("eexp", [CH, 2, TG], F32))

        psA = E(nc.psum_tensor("psA", [CH, 1024], BF16))
        psB = E(nc.psum_tensor("psB", [CH, 1024], BF16))
        psC = E(nc.psum_tensor("psC", [CH, 512], F32))
        psD = E(nc.psum_tensor("psD", [CH, 512], F32))
        psE = E(nc.psum_tensor("psE", [CH, 512], F32))
        psF = E(nc.psum_tensor("psF", [CH, 512], F32))
        psG = E(nc.psum_tensor("psG", [CH, 512], F32))
        psH = E(nc.psum_tensor("psH", [CH, 512], F32))
        psAB = [psA, psB]
        psCD = [psC, psD]
        psEF = [psE, psF]
        psGH = [psG, psH]
        ps4 = [psC, psD, psE, psF]
        o2banks = [psE, psF, psG, psH]

        p = Plan()
        semnames = ["T", "V", "S", "G", "CC"]
        for qk in ("DS", "DA", "DG"):
            semnames += [f"{qk}{j}" for j in range(Plan.KDMA)]
        for s in semnames:
            p.sems[s] = E(nc.semaphore("sem" + s))
            p.cnt[s] = 0

        free = {}

        def guard(eng, key):
            p.waitm(eng, free.get(key))

        def wT(slot, dt, c0, n):
            return poolP[:, 8192 * slot + 1024 * dt + c0:8192 * slot + 1024 * dt + c0 + n]

        def w1qt(dt, f0, n):
            return poolP[:, 4096 * dt + f0:4096 * dt + f0 + n]

        def hT(ftile, t0, n):
            return poolQ[:, 512 * ftile + t0:512 * ftile + t0 + n]

        def qT_full(gp, ct):
            return poolQ[:, 4096 * gp + 512 * ct:4096 * gp + 512 * (ct + 1)]

        def qT_h(gp, h, j):
            base = 64 * (h % 2)
            return poolQ[base:base + 64,
                         4096 * gp + 512 * (h // 2) + 128 * j:
                         4096 * gp + 512 * (h // 2) + 128 * (j + 1)]

        def kT_full(gp, ct):
            return poolQ[:, 8192 + 4096 * gp + 512 * ct:8192 + 4096 * gp + 512 * (ct + 1)]

        def kT_h(gp, h, j):
            base = 64 * (h % 2)
            o = 8192 + 4096 * gp + 512 * (h // 2) + 128 * j
            return poolQ[base:base + 64, o:o + 128]

        def kTgrp(gp):
            return poolQ[:, 8192 + 4096 * gp:8192 + 4096 * (gp + 1)].rearrange(
                "p (ct t) -> p ct t", t=TG)

        def va_h(gp, j, h):
            return va[:, gp, j, 65 * h:65 * h + 65]

        def acc4(j):
            return accb[:, j // 2, 512 * (j % 2):512 * (j % 2) + 512]

        def w2slot(s):
            return w2s[:, 512 * s:512 * (s + 1)]

        def w2qv(par):
            return ktok[:, par, :]

        def xgh(j):
            return xg[:, j // 2, 512 * (j % 2):512 * (j % 2) + 512]

        def state_out(h):
            bank = psGH[h // 8]
            base = 64 * (h % 2)
            c0 = 65 * ((h // 2) % 4)
            return bank[base:base + 64, c0:c0 + 65], (h % 2 > 0)

        c_ident = p.dma("gpsimd", lambda g: g.dma_start(out=ident[:], in_=ident_in))
        p.dma("gpsimd", lambda g: g.dma_start(out=maskT[:], in_=maskT_in))
        c_pm = p.dma("gpsimd", lambda g: g.dma_start(out=pmask[:], in_=pmask_in))
        c_b1 = p.dma("gpsimd", lambda g: g.dma_start(out=b1col[:], in_=b1_in))
        c_b2 = p.dma("gpsimd", lambda g: g.dma_start(out=b2bc[0:1, :], in_=b2_in[None, :]))
        p.op("vector", lambda v: v.memset(ones_r[:], 1.0))
        m_ones = p.op("vector", lambda v: v.memset(onesc[:], 1.0), "V")
        p.op("vector", lambda v: v.memset(cvals[:, 0:1], 1e-5))
        p.op("vector", lambda v: v.memset(cvals[:, 1:2], -0.5))
        p.op("vector", lambda v: v.memset(cvals[:, 2:3], 0.5 / (DFF * DM)))
        cv_done = p.op("vector", lambda v: v.memset(cvals[:, 3:4], 0.5e-5), "V")
        v_zero = p.op("vector", lambda v: v.memset(pp[:, 0, :], 0.0), "V")

        p.waitm("tensor", c_b2)
        p.wait("tensor", "V", m_ones)
        p.op("tensor", lambda t: t.matmul(psC[:, 0:512], ones_r[:], b2bc[0:1, 0:512],
                                          start=True, stop=True))
        bm = p.op("tensor", lambda t: t.matmul(psD[:, 0:512], ones_r[:],
                                               b2bc[0:1, 512:1024],
                                               start=True, stop=True), "T")
        p.wait("vector", "T", bm)
        p.op("vector", lambda v: v.tensor_copy(b2bc[:, 0:512], psC[:, 0:512]))
        bv = p.op("vector", lambda v: v.tensor_copy(b2bc[:, 512:1024], psD[:, 0:512]),
                  "V")
        free[("psCD", 0)] = ("V", bv)
        free[("psCD", 1)] = ("V", bv)

        def prep_cast(w_ap, slot, qe="sync"):
            cd = None
            for dt in range(8):
                par = dt % 2
                guard(qe, ("wst", par))
                d = p.dma(qe, lambda s, dt=dt, par=par, w_ap=w_ap: s.dma_start(
                    out=wst[:, par, :], in_=w_ap[128 * dt:128 * (dt + 1), :]))
                p.waitm("vector", d)
                cd = p.op("vector", lambda v, par=par, slot=slot, dt=dt: v.tensor_copy(
                    wT(slot, dt, 0, 1024), wst[:, par, :]), "V")
                free[("wst", par)] = ("V", cd)
            return cd

        prep_cast(wkT_in, 0, qe="gpsimd")
        wprep_done = prep_cast(wvT_in, 1, qe="gpsimd")

        gtiles = []
        for r in range(8):
            for fc2 in range(4):
                gtiles.append((W1T_in[128 * r:128 * (r + 1),
                                      1024 * fc2:1024 * (fc2 + 1)], 4 * r + fc2))
        for ft in range(32):
            gtiles.append((W2T_in[128 * ft:128 * (ft + 1), :], 32 + ft))

        spill_xnT = {}
        spill_kT = {}
        spill_va = {}
        snap_ev = {}
        xn_done = {}
        vvs = {}
        gam_s = {}
        gq_d = {}

        def emit_greduce(i, dumpar):
            ap_, col = gtiles[i]
            par_ = i % 2
            p.waitm("scalar", gq_d[i])
            rr = p.op("scalar", lambda s, par_=par_, col=col, dumpar=dumpar:
                      s.activation(accb[:, dumpar, :], wst[:, par_, :], AF.Abs,
                                   accum_out=gacc[:, col:col + 1]), "S")
            free[("wst", par_)] = ("S", rr)
            return rr

        p.wait("scalar", "V", v_zero)
        z0 = p.op("vector", lambda v: v.memset(snapst[:, 0, :], 0.0), "V")
        p.wait("scalar", "V", z0)
        sp0 = p.dma("scalar", lambda s: s.dma_start(out=snap_sp[0], in_=snapst[:, 0, :]))
        free[("snapst", 0)] = sp0

        for g in range(NG):
            gp = g % 2
            def emit_LN_B(bc):
                c, par, j = bc["c"], bc["par"], bc["j"]
                sl = 8 * par
                s1, s2 = sm[:, sl:sl + 1], sm[:, sl + 1:sl + 2]
                msq, var = sm[:, sl + 2:sl + 3], sm[:, sl + 3:sl + 4]
                lnv, rstd = sm[:, sl + 4:sl + 5], sm[:, sl + 5:sl + 6]
                nbias = sm[:, sl + 6:sl + 7]
                p.wait("vector", "V", bc["mqc"])
                p.wait("vector", "S", bc["sq"])
                vv = p.op("vector", lambda v, s2=s2, msq=msq, var=var:
                          v.scalar_tensor_tensor(var, s2, 1.0 / DM, msq,
                                                 ALU.mult, ALU.add), "V")
                vvs[c] = vv
                p.wait("scalar", "V", vv)
                p.wait("scalar", "V", cv_done)
                lnc = p.op("scalar", lambda s, var=var, lnv=lnv: s.activation(
                    lnv, var, AF.Ln, bias=cvals[:, 0:1]), "S")
                p.wait("scalar", "S", lnc)
                rs = p.op("scalar", lambda s, lnv=lnv, rstd=rstd: s.activation(
                    rstd, lnv, AF.Exp, scale=cvals[:, 1:2]), "S")
                p.wait("scalar", "S", rs)
                p.wait("vector", "S", rs)
                nb = p.op("vector", lambda v, s1=s1, rstd=rstd, nbias=nbias:
                          v.scalar_tensor_tensor(nbias, s1, -1.0 / DM, rstd,
                                                 ALU.mult, ALU.mult), "V")
                p.wait("scalar", "V", nb)
                guard("scalar", ("xnb", par))
                xn_done[c] = p.op("scalar", lambda s, par=par, rstd=rstd, nbias=nbias:
                                  s.activation(xnb[:, par, :], xg[:, par, :], AF.Identity,
                                               bias=nbias, scale=rstd), "S")
                free[("xg", par)] = ("S", xn_done[c])
                p.wait("tensor", "S", xn_done[c])
                p.waitm("tensor", c_ident)
                guard("tensor", ("xnTbuf", gp))
                tr = None
                for dt in range(8):
                    bank = psAB[dt % 2]
                    fslot = (dt // 2) % 4
                    guard("tensor", ("psAB", dt % 2))
                    tr = p.op("tensor", lambda t, bank=bank, fslot=fslot, dt=dt, par=par:
                              t.transpose(bank[:, 128 * fslot:128 * (fslot + 1)],
                                          xnb[:, par, 128 * dt:128 * (dt + 1)], ident[:]),
                              "T" if dt == 7 else None)
                free[("xnb", par)] = ("T", tr)
                p.wait("vector", "T", tr)
                guard("vector", ("xnTbuf", gp))
                for bi in range(2):
                    dst = xnT[:, gp, bi:8:2, 128 * j:128 * (j + 1)]
                    ev = p.op("vector", lambda v, bi=bi, dst=dst: v.tensor_copy(
                        dst, psAB[bi][:, 0:512]), "V")
                    free[("psAB", bi)] = ("V", ev)

            pendLN = None
            for j in range(JPG):
                c = JPG * g + j
                par = c % 2
                guard("sync", ("xg", par))
                d = p.dma("sync", lambda s, c=c, par=par: s.dma_start(
                    out=xg[:, par, :], in_=x_in[CH * c:CH * (c + 1), :]))
                p.waitm("vector", d)
                p.waitm("scalar", d)
                sl = 8 * par
                s1 = sm[:, sl:sl + 1]
                s2 = sm[:, sl + 1:sl + 2]
                msq = sm[:, sl + 2:sl + 3]
                r1c = p.op("vector", lambda v, par=par, s1=s1: v.reduce_sum(
                    s1, xg[:, par, :], axis=AX), "V")
                p.wait("vector", "V", r1c)
                guard("scalar", ("accb0",))
                if c - 2 in vvs:
                    p.wait("scalar", "V", vvs[c - 2])
                if c - 2 in gam_s:
                    p.wait("scalar", "S", gam_s[c - 2])
                sq = p.op("scalar", lambda s, par=par, s2=s2: s.activation(
                    accb[:, par, :], xg[:, par, :], AF.Square, accum_out=s2), "S")
                mqc = p.op("vector", lambda v, s1=s1, msq=msq: v.scalar_tensor_tensor(
                    msq, s1, -1.0 / (DM * DM), s1, ALU.mult, ALU.mult), "V")
                ctxLN = {"c": c, "par": par, "j": j, "sq": sq, "mqc": mqc}
                if pendLN is not None:
                    emit_LN_B(pendLN)
                pendLN = ctxLN
            emit_LN_B(pendLN)
            xnT_done = p.now("V")
            p.wait("scalar", "V", xnT_done)
            spill_xnT[g] = p.dma("scalar", lambda s, g=g, gp=gp: s.dma_start(
                out=xnT_sp.rearrange("(dt p) T -> p dt T", p=CH)[:, :, TG * g:TG * (g + 1)],
                in_=xnT[:, gp, :, :].rearrange("p a b -> p (a b)")))

            p.wait("tensor", "V", xnT_done)
            p.wait("tensor", "V", wprep_done)
            for ct in range(8):
                bank = psCD[ct % 2]
                guard("tensor", ("psCD", ct % 2))
                guard("tensor", ("kTbuf", gp))
                for dt in range(8):
                    p.op("tensor", lambda t, bank=bank, ct=ct, dt=dt, gp=gp: t.matmul(
                        bank[:, 0:512], wT(0, dt, 128 * ct, 128), xnT[:, gp, dt, 0:512],
                        start=(dt == 0), stop=(dt == 7)), "T" if dt == 7 else None)
                mmT = p.now("T")
                ep = ct % 2
                p.wait("scalar", "T", mmT)
                guard("scalar", ("eexp", ep))
                ex = p.op("scalar", lambda s, bank=bank, ep=ep: s.activation(
                    eexp[:, ep, :], bank[:, 0:512], AF.Exp), "S")
                p.wait("vector", "S", ex)
                mnc = p.op("vector", lambda v, ep=ep: v.tensor_scalar(
                    eexp[:, ep, :], eexp[:, ep, :], 1.0, None, ALU.min), "V")
                p.wait("vector", "V", mnc)
                guard("vector", ("kTbuf", gp))
                kev = p.op("vector", lambda v, bank=bank, ep=ep, ct=ct, gp=gp:
                           v.scalar_tensor_tensor(kT_full(gp, ct), bank[:, 0:512], 0.0,
                                                  eexp[:, ep, :], ALU.max, ALU.add), "V")
                free[("eexp", ep)] = ("V", kev)
                free[("psCD", ct % 2)] = ("V", kev)
            kT_all = p.now("V")
            p.wait("scalar", "V", kT_all)
            spill_kT[g] = p.dma("scalar", lambda s, g=g, gp=gp: s.dma_start(
                out=kT_sp.rearrange("(ct p) T -> p ct T", p=CH)[:, :, TG * g:TG * (g + 1)],
                in_=kTgrp(gp)))

            guard("vector", ("vabuf", gp))
            on = p.op("vector", lambda v, gp=gp: v.memset(
                va[:, gp, :, :].rearrange("p j (h e) -> p j h e", e=D + 1)[:, :, :, D:D + 1],
                1.0), "V")

            for j in range(JPG):
                c = JPG * g + j
                par = c % 2
                guard("tensor", ("ktok", par))
                guard("tensor", ("psEF", 0))
                guard("tensor", ("psEF", 1))
                for dt in range(8):
                    for half in range(2):
                        p.op("tensor", lambda t, dt=dt, half=half, gp=gp, j=j: t.matmul(
                            psEF[half][:, 0:512], xnT[:, gp, dt, 128 * j:128 * (j + 1)],
                            wT(1, dt, 512 * half, 512), start=(dt == 0), stop=(dt == 7)),
                            "T" if (dt == 7 and half == 1) else None)
                kvT = p.now("T")
                p.wait("tensor", "V", kT_all)
                tr2 = None
                for ct in range(8):
                    bank = psAB[ct % 2]
                    fslot = (ct // 2) % 4
                    guard("tensor", ("psAB", ct % 2))
                    tr2 = p.op("tensor", lambda t, bank=bank, fslot=fslot, ct=ct,
                               gp=gp, j=j: t.transpose(
                                   bank[:, 128 * fslot:128 * (fslot + 1)],
                                   kT_full(gp, ct)[:, 128 * j:128 * (j + 1)], ident[:]),
                               "T" if ct == 7 else None)
                p.wait("vector", "T", tr2)
                for bi in range(2):
                    dst = ktok[:, par, :].rearrange("p (a b) -> p a b", b=128)[:, bi:8:2, :]
                    kv = p.op("vector", lambda v, bi=bi, dst=dst: v.tensor_copy(
                        dst, psAB[bi][:, 0:512]), "V")
                    free[("psAB", bi)] = ("V", kv)
                p.wait("vector", "T", kvT)
                p.wait("vector", "V", on)
                for half in range(2):
                    dst = va[:, gp, j, :].rearrange("p (h e) -> p h e", e=D + 1)[
                        :, 8 * half:8 * (half + 1), 0:D]
                    vv1 = p.op("vector", lambda v, half=half, dst=dst: v.tensor_copy(
                        dst, psEF[half][:, 0:512]), "V")
                    free[("psEF", half)] = ("V", vv1)
                ktv = p.now("V")

                p.wait("tensor", "V", ktv)
                if c > 0:
                    p.wait("tensor", "V", snap_ev[c - 1])
                if c == 0:
                    for bi in range(2):
                        p.op("tensor", lambda t, bi=bi: t.matmul(
                            psGH[bi][:, 500:501], ident[:], ident[:, 0:1],
                            start=True, stop=False, skip_group_check=True))
                for h in (0, 8, 1, 9, 2, 10, 3, 11, 4, 12, 5, 13, 6, 14, 7, 15):
                    o, up = state_out(h)
                    p.op("tensor", lambda t, h=h, o=o, up=up, par=par, gp=gp, j=j:
                         t.matmul(o, ktok[:, par, 64 * h:64 * (h + 1)], va_h(gp, j, h),
                                  start=False, stop=False, skip_group_check=True,
                                  tile_position=(0, 64) if up else None),
                         "T" if h == 15 else None)
                if c == NC - 1:
                    for bi in range(2):
                        p.op("tensor", lambda t, bi=bi: t.matmul(
                            psGH[bi][:, 500:501], ident[:], ident[:, 0:1],
                            start=False, stop=True, skip_group_check=True))
                st_done = p.now("T")
                free[("ktok", par)] = ("T", st_done)
                p.wait("vector", "T", st_done)
                spar = (c + 1) % 2
                guard("vector", ("snapst", spar))
                for bi in range(2):
                    p.op("vector", lambda v, bi=bi, spar=spar: v.tensor_copy(
                        snapst[:, spar, 260 * bi:260 * (bi + 1)], psGH[bi][:, 0:260]),
                        "V" if bi == 1 else None)
                snap_ev[c] = p.now("V")
                p.wait("scalar", "V", snap_ev[c])
                sd = p.dma("scalar", lambda s, c=c, spar=spar: s.dma_start(
                    out=snap_sp[c + 1], in_=snapst[:, spar, :]))
                free[("snapst", spar)] = sd
                pt = c
                if pt > 0:
                    if pt - 2 in gam_s:
                        p.wait("scalar", "S", gam_s[pt - 2])
                    p.wait("scalar", "V", vvs[pt])
                    for i in (2 * pt - 2, 2 * pt - 1):
                        gam_s[pt] = emit_greduce(i, pt % 2)
                for i in (2 * pt, 2 * pt + 1):
                    ap_, _col = gtiles[i]
                    par_ = i % 2
                    qeng = "scalar" if par_ == 0 else "gpsimd"
                    guard(qeng, ("wst", par_))
                    gq_d[i] = p.dma(qeng, lambda s, ap_=ap_, par_=par_: s.dma_start(
                        out=wst[:, par_, :], in_=ap_))
            p.wait("scalar", "V", p.now("V"))
            spill_va[g] = p.dma("scalar", lambda s, g=g, gp=gp: s.dma_start(
                out=va_sp[TG * g:TG * (g + 1), :].rearrange("(j p) n -> p j n", p=CH),
                in_=va[:, gp, :, :]))
            free[("vabuf", gp)] = spill_va[g]
            free[("xnTbuf", gp)] = spill_xnT[g]
            free[("kTbuf", gp)] = spill_kT[g]

        p.wait("scalar", "V", vvs[NC - 1])
        for i in (62, 63):
            gam_s[NC] = emit_greduce(i, NC % 2)

        p1T = p.now("T")

        p.wait("vector", "T", p1T)
        for bi in range(2):
            p.op("vector", lambda v, bi=bi: v.tensor_copy(
                agst[:, 260 * bi:260 * (bi + 1)], psGH[bi][:, 0:260]),
                "V" if bi == 1 else None)
        ag_ready = p.now("V")
        p.wait("gpsimd", "V", ag_ready)
        dg = p.dma("gpsimd", lambda g: g.dma_start(out=ag_in[:], in_=agst[:]))
        p.waitm("gpsimd", dg)
        p.op("gpsimd", lambda g: g.collective_compute(
            "AllGather", ALU.bypass, replica_groups=[list(range(8))],
            ins=[ag_in_t.ap().opt()], outs=[ag_out_t.ap().opt()]), "CC")

        g1, g2 = gacc[:, 64:65], gacc[:, 65:66]
        p.wait("vector", "V", p.now("V"))
        p.wait("vector", "S", p.now("S"))
        p.wait("vector", "G", p.now("G"))
        gv1 = p.op("vector", lambda v: v.reduce_sum(g1, gacc[:, 0:32], axis=AX), "V")
        gv = p.op("vector", lambda v: v.reduce_sum(g2, gacc[:, 32:64], axis=AX), "V")
        p.wait("tensor", "V", gv)
        p.wait("tensor", "V", ag_ready)
        p.op("tensor", lambda t: t.matmul(psG[0:1, 0:1], g1, onesc[:],
                                          start=True, stop=True))
        tg2 = p.op("tensor", lambda t: t.matmul(psG[0:1, 1:2], g2, onesc[:],
                                                start=True, stop=True), "T")
        p.wait("scalar", "T", tg2)
        p.op("scalar", lambda s: s.activation(
            gacc[0:1, 66:67], psG[0:1, 0:1], AF.Identity, scale=cvals[0:1, 2:3],
            bias=cvals[0:1, 3:4]))
        t2 = p.op("scalar", lambda s: s.activation(
            gacc[0:1, 67:68], psG[0:1, 1:2], AF.Identity, scale=cvals[0:1, 2:3],
            bias=cvals[0:1, 3:4]),
            "S")
        p.wait("tensor", "S", t2)
        p.op("tensor", lambda t: t.matmul(psG[:, 2:3], ones_r[:], gacc[0:1, 66:67],
                                          start=True, stop=True))
        bc = p.op("tensor", lambda t: t.matmul(psG[:, 3:4], ones_r[:], gacc[0:1, 67:68],
                                               start=True, stop=True), "T")
        p.wait("vector", "T", bc)
        p.op("vector", lambda v: v.tensor_copy(thrs[:, 0:1], psG[:, 2:3]))
        p.op("vector", lambda v: v.tensor_scalar(thrs[:, 1:2], psG[:, 2:3], -1.0, None,
                                                 ALU.mult))
        p.op("vector", lambda v: v.tensor_copy(thrs[:, 2:3], psG[:, 3:4]))
        thr_done = p.op("vector", lambda v: v.tensor_scalar(
            thrs[:, 3:4], psG[:, 3:4], -1.0, None, ALU.mult), "V")

        p.wait("sync", "T", p1T)
        p.waitm("sync", spill_xnT[0])
        p.waitm("sync", spill_xnT[NG - 2])
        dxs = {}
        dxs[0] = p.dma("sync", lambda s: s.dma_start(
            out=xnT[:, 0, :, :].rearrange("p a b -> p (a b)"),
            in_=xnT_sp.rearrange("(dt p) T -> p dt T", p=CH)[:, :, 0:TG]))

        p.wait("vector", "V", thr_done)
        w1bufs = [wst[:, 0, :], wst[:, 1, :], xg[:, 0, :], xg[:, 1, :]]
        w1keys = [("wst", 0), ("wst", 1), ("xgq", 0), ("xgq", 1)]
        w1fc = {}
        for i in range(32):
            fc2, r = i // 8, i % 8
            b4 = i % 4
            buf = w1bufs[b4]
            qe1 = ("sync", "scalar")[i % 2]
            guard(qe1, w1keys[b4])
            d = p.dma(qe1, lambda s, r=r, fc2=fc2, buf=buf: s.dma_start(
                out=buf,
                in_=W1T_in[128 * r:128 * (r + 1), 1024 * fc2:1024 * (fc2 + 1)]))
            p.waitm("vector", d)
            guard("vector", ("accb1",))
            q1c = p.op("vector", lambda v, buf=buf: v.tensor_scalar(
                accb[:, 1, :], buf, thrs[:, 1:2], -1.0, ALU.is_lt,
                ALU.mult), "V")
            p.wait("vector", "V", q1c)
            qz = p.op("vector", lambda v, buf=buf, r=r, fc2=fc2: v.scalar_tensor_tensor(
                w1qt(r, 1024 * fc2, 1024), buf, thrs[:, 0:1], accb[:, 1, :],
                ALU.is_gt, ALU.add), "V")
            free[w1keys[b4]] = ("V", qz)
            free[("accb1",)] = ("V", qz)
            if r == 7:
                w1fc[fc2] = p.now("V")
        w1_ready = p.now("V")

        w2bufs = [eexp[:, :, :].rearrange("p a b -> p (a b)"),
                  gr[:, :, :].rearrange("p a b -> p (a b)")[:, 0:1024]]
        w2_spill_last = None
        w2sp = {}
        for ft in range(32):
            par = ft % 2
            buf = w2bufs[par]
            qe2 = ("sync", "scalar")[ft % 2]
            guard(qe2, ("w2ld", par))
            if ft < 2:
                p.waitm(qe2, free.get(("eexp", 0)))
                p.waitm(qe2, free.get(("eexp", 1)))
            d = p.dma(qe2, lambda s, ft=ft, buf=buf: s.dma_start(
                out=buf, in_=W2T_in[128 * ft:128 * (ft + 1), :]))
            p.waitm("vector", d)
            guard("vector", ("accbq",))
            q1c = p.op("vector", lambda e, buf=buf: e.tensor_scalar(
                accb[:, 0, :], buf, thrs[:, 3:4], -1.0, ALU.is_lt,
                ALU.mult), "V")
            p.wait("vector", "V", q1c)
            guard("vector", ("w2q", par))
            qz = p.op("vector", lambda e, buf=buf, par=par: e.scalar_tensor_tensor(
                w2qv(par), buf, thrs[:, 2:3], accb[:, 0, :],
                ALU.is_gt, ALU.add), "V")
            free[("w2ld", par)] = ("V", qz)
            free[("accbq",)] = ("V", qz)
            p.wait("gpsimd", "V", qz)
            w2_spill_last = p.dma("gpsimd", lambda s, ft=ft, par=par: s.dma_start(
                out=w2qt_sp[128 * ft:128 * (ft + 1), :], in_=w2qv(par)))
            w2sp[ft] = w2_spill_last
            free[("w2q", par)] = w2_spill_last
        w2q_done = ("V", p.now("V"))
        free[("eexp", 0)] = w2q_done
        free[("eexp", 1)] = w2q_done
        free[("gr", 0)] = w2q_done
        free[("gr", 1)] = w2q_done
        free[("ybuf", 0)] = free.get(("w2q", 0))
        free[("ybuf", 1)] = free.get(("w2q", 1))
        for j in range(4):
            free[("xgh", j)] = ("V", w1_ready)
        free[("accs", 0)] = w2q_done
        free[("accs", 1)] = w2q_done
        free[("accs", 2)] = ("V", w1_ready)
        free[("accs", 3)] = ("V", w1_ready)
        free[("accb0",)] = w2q_done

        w2i = 0
        for g in range(NG):
            gp = g % 2
            p.waitm("tensor", dxs[g])
            if g == 0:
                p.waitm("scalar", spill_kT[NG - 1])
            hvs = {}
            for ftile in range(32):
                bank = psCD[ftile % 2]
                if g == 0:
                    p.wait("tensor", "V", w1fc[ftile // 8])
                guard("tensor", ("psCD", ftile % 2))
                for dt in range(8):
                    p.op("tensor", lambda t, bank=bank, ftile=ftile, dt=dt, gp=gp: t.matmul(
                        bank[:, 0:512], w1qt(dt, 128 * ftile, 128), xnT[:, gp, dt, 0:512],
                        start=(dt == 0), stop=(dt == 7)), "T" if dt == 7 else None)
                mmn = p.now("T")
                p.wait("scalar", "T", mmn)
                hv = p.op("scalar", lambda s, bank=bank, ftile=ftile: s.activation(
                    hT(ftile, 0, 512), bank[:, 0:512], AF.Relu,
                    bias=b1col[:, ftile:ftile + 1]), "S")
                hvs[ftile] = hv
                free[("psCD", ftile % 2)] = ("S", hv)
            free[("xnTbuf2", gp)] = ("T", p.now("T"))
            if g + 1 < NG:
                p.waitm("sync", spill_xnT[g + 1])
                guard("sync", ("xnTbuf2", (g + 1) % 2))
                dxs[g + 1] = p.dma("sync", lambda s, g=g: s.dma_start(
                    out=xnT[:, (g + 1) % 2, :, :].rearrange("p a b -> p (a b)"),
                    in_=xnT_sp.rearrange("(dt p) T -> p dt T", p=CH)[
                        :, :, TG * (g + 1):TG * (g + 2)]))

            for dmh in range(2):
                pb = [psE, psF, psG, psH] if dmh == 0 else [psC, psD, psE, psF]
                pbi = [2, 3, 4, 5] if dmh == 0 else [0, 1, 2, 3]
                if dmh == 1:
                    free[("o2b", 0)] = free.get(("psCD", 0))
                    free[("o2b", 1)] = free.get(("psCD", 1))
                dxxs = {}
                for j in range(JPG):
                    c = JPG * g + j
                    guard("scalar", ("xgh", j))
                    dxxs[j] = p.dma("scalar", lambda s, c=c, j=j, dmh=dmh: s.dma_start(
                        out=xgh(j), in_=x_in[CH * c:CH * (c + 1),
                                             512 * dmh:512 * (dmh + 1)]))
                for ft in range(32):
                    slot = w2i % NW2
                    qeng = "sync" if w2i % 2 == 0 else "gpsimd"
                    w2i += 1
                    guard(qeng, ("w2s", slot))
                    if g == 0 and dmh == 0:
                        p.waitm(qeng, w2sp[ft])
                    dw = p.dma(qeng, lambda s, ft=ft, dmh=dmh, slot=slot: s.dma_start(
                        out=w2slot(slot),
                        in_=w2qt_sp[128 * ft:128 * (ft + 1),
                                    512 * dmh:512 * (dmh + 1)]))
                    p.waitm("tensor", dw)
                    p.wait("tensor", "S", hvs[ft])
                    lm = None
                    for j in range(JPG):
                        if ft == 0:
                            guard("tensor", ("o2b", pbi[j]))
                        lm = p.op("tensor", lambda t, ft=ft, j=j, slot=slot, pb=pb: t.matmul(
                            pb[j][:, 0:512], hT(ft, 128 * j, 128), w2slot(slot),
                            start=(ft == 0), stop=(ft == 31)),
                            "T" if j == 3 else None)
                    free[("w2s", slot)] = ("T", lm)
                mm2p = p.now("T")
                for j in range(JPG):
                    c = JPG * g + j
                    p.waitm("vector", dxxs[j])
                    p.wait("vector", "T", mm2p)
                    guard("vector", ("accs", j))
                    a1 = p.op("vector", lambda v, j=j, dmh=dmh, pb=pb: v.tensor_tensor(
                        acc4(j), pb[j][:, 0:512],
                        b2bc[:, 512 * dmh:512 * (dmh + 1)], ALU.add), "V")
                    p.wait("vector", "V", a1)
                    a2 = p.op("vector", lambda v, j=j: v.tensor_tensor(
                        acc4(j), acc4(j), xgh(j), ALU.add), "V")
                    free[("o2b", pbi[j])] = ("V", a2)
                    free[("xgh", j)] = ("V", a2)
                    p.wait("scalar", "V", a2)
                    sa = p.dma("scalar", lambda s, c=c, j=j, dmh=dmh: s.dma_start(
                        out=acc_sp[CH * c:CH * (c + 1), 512 * dmh:512 * (dmh + 1)],
                        in_=acc4(j)))
                    free[("accs", j)] = sa
                if dmh == 1:
                    free[("psCD", 0)] = free[("o2b", 0)]
                    free[("psCD", 1)] = free[("o2b", 1)]
        ffn_T = p.now("T")
        free[("psGH", 0)] = free.get(("o2b", 4))
        free[("psGH", 1)] = free.get(("o2b", 5))
        for b_ in range(4):
            free[("nbank", b_)] = free.get(("o2b", b_))
            free[("sslot", b_)] = free.get(("o2b", b_))

        p.wait("tensor", "V", p.now("V"))
        p.wait("tensor", "S", p.now("S"))
        p.wait("vector", "T", ffn_T)
        prep_cast(wqT_in, 2)
        wqp_done = prep_cast(wpT_in, 3)
        p.wait("sync", "T", ffn_T)
        p.dma_barrier("sync")

        p.wait("gpsimd", "CC", 1)
        for r in range(8):
            guard("gpsimd", ("gr", r % 2))
            dgr = p.dma("gpsimd", lambda g, r=r: g.dma_start(
                out=gr[:, r % 2, :], in_=ag_out[CH * r:CH * (r + 1), :]))
            p.waitm("vector", dgr)
            p.wait("vector", "V", v_zero)
            p.waitm("vector", c_pm)
            cmb = p.op("vector", lambda v, r=r: v.scalar_tensor_tensor(
                pp[:, (r + 1) % 2, :], gr[:, r % 2, :], pmask[:, r:r + 1],
                pp[:, r % 2, :], ALU.mult, ALU.add), "V")
            p.wait("vector", "V", cmb)
            free[("gr", r % 2)] = ("V", cmb)
        prefix_done = p.op("vector", lambda v: v.tensor_copy(spre[:], pp[:, 0, :]), "V")

        y_of = {}

        for g in range(NG):
            gp = g % 2
            guard("sync", ("p2xnT", gp))
            guard("sync", ("p2kT", gp))
            guard("sync", ("p2va", gp))
            p.waitm("sync", spill_kT[g])
            p.waitm("sync", spill_va[g])
            dq = p.dma("sync", lambda s, g=g, gp=gp: s.dma_start(
                out=xnT[:, gp, :, :].rearrange("p a b -> p (a b)"),
                in_=xnT_sp.rearrange("(dt p) T -> p dt T", p=CH)[:, :, TG * g:TG * (g + 1)]))
            dk = p.dma("sync", lambda s, g=g, gp=gp: s.dma_start(
                out=kTgrp(gp),
                in_=kT_sp.rearrange("(ct p) T -> p ct T", p=CH)[:, :, TG * g:TG * (g + 1)]))
            dv = p.dma("sync", lambda s, g=g, gp=gp: s.dma_start(
                out=va[:, gp, :, :],
                in_=va_sp[TG * g:TG * (g + 1), :].rearrange("(j p) n -> p j n", p=CH)))
            p.waitm("tensor", dq)
            p.wait("tensor", "V", wqp_done)
            if g == 0:
                p.wait("tensor", "T", ffn_T)
            for ct in range(8):
                bank = psGH[ct % 2]
                guard("tensor", ("psGH", ct % 2))
                guard("tensor", ("p2qT", gp))
                for dt in range(8):
                    p.op("tensor", lambda t, bank=bank, ct=ct, dt=dt, gp=gp: t.matmul(
                        bank[:, 0:512], wT(2, dt, 128 * ct, 128), xnT[:, gp, dt, 0:512],
                        start=(dt == 0), stop=(dt == 7)), "T" if dt == 7 else None)
                mmq = p.now("T")
                ep = ct % 2
                p.wait("scalar", "T", mmq)
                guard("scalar", ("eexp", ep))
                ex = p.op("scalar", lambda s, bank=bank, ep=ep: s.activation(
                    eexp[:, ep, :], bank[:, 0:512], AF.Exp), "S")
                p.wait("vector", "S", ex)
                mnc = p.op("vector", lambda v, ep=ep: v.tensor_scalar(
                    eexp[:, ep, :], eexp[:, ep, :], 1.0, None, ALU.min), "V")
                p.wait("vector", "V", mnc)
                qe = p.op("vector", lambda v, bank=bank, ep=ep, ct=ct, gp=gp:
                          v.scalar_tensor_tensor(qT_full(gp, ct), bank[:, 0:512], 0.0,
                                                 eexp[:, ep, :], ALU.max, ALU.add), "V")
                free[("eexp", ep)] = ("V", qe)
                free[("psGH", ct % 2)] = ("V", qe)
            q_done = p.now("V")

            for j in range(JPG):
                c = JPG * g + j
                par = c % 2
                guard("sync", ("snld", par))
                dsn = p.dma("sync", lambda s, c=c, par=par: s.dma_start(
                    out=snld[:, par, :], in_=snap_sp[c]))
                guard("sync", ("accbr", par))
                guard("sync", ("accbr2", par))
                dac = p.dma("sync", lambda s, c=c, par=par: s.dma_start(
                    out=accb[:, par, :], in_=acc_sp[CH * c:CH * (c + 1), :]))
                p.waitm("vector", dsn)
                p.wait("vector", "V", prefix_done)
                guard("vector", ("saugT", par))
                sg = p.op("vector", lambda v, par=par: v.tensor_tensor(
                    saugT[:, par, :], snld[:, par, :], spre[:], ALU.add), "V")
                free[("snld", par)] = ("V", sg)

                p.waitm("tensor", dk)
                p.waitm("tensor", dv)
                p.wait("tensor", "V", q_done)
                p.wait("tensor", "V", sg)
                num_last = None
                for w in range(4):
                    scm = None
                    for b in range(4):
                        h = 4 * w + b
                        guard("tensor", ("sslot", b))
                        scm = p.op("tensor", lambda t, b=b, h=h, j=j, gp=gp:
                                   t.matmul(ps4[b][:, 260:388], kT_h(gp, h, j),
                                            qT_h(gp, h, j), start=True, stop=True),
                                   "T" if b == 3 else None)
                    p.wait("vector", "T", scm)
                    msv = None
                    for b in range(4):
                        h = 4 * w + b
                        guard("vector", ("mS", h))
                        msv = p.op("vector", lambda v, b=b, h=h, par=par:
                                   v.tensor_tensor(mS[:, par, h, :], ps4[b][:, 260:388],
                                                   maskT[:], ALU.mult),
                                   "V" if b == 3 else None)
                    for b in range(4):
                        free[("sslot", b)] = ("V", msv)
                    p.wait("tensor", "V", msv)
                    for b in range(4):
                        h = 4 * w + b
                        guard("tensor", ("nbank", b))
                        guard("tensor", ("nbankv", b))
                        o = ps4[b][:, 65 * w:65 * w + 65]
                        sb = 64 * (h % 2)
                        p.op("tensor", lambda t, o=o, h=h, j=j, gp=gp, sb=sb, par=par:
                             t.matmul(o, qT_h(gp, h, j),
                                      saugT[sb:sb + 64, par,
                                            65 * (h // 2):65 * (h // 2) + 65],
                                      start=True, stop=False))
                        num_last = p.op("tensor", lambda t, o=o, h=h, par=par, gp=gp, j=j:
                                        t.matmul(o, mS[:, par, h, :], va_h(gp, j, h),
                                                 start=False, stop=True),
                                        "T" if b == 3 else None)
                        free[("mS", h)] = ("T", p.now("T"))
                p.wait("vector", "T", num_last)
                guard("vector", ("recb", par))
                guard("vector", ("recbs", par))
                dxc = None
                for b in range(4):
                    dxc = p.op("vector", lambda v, b=b, par=par: v.tensor_scalar(
                        recb[:, par, 4 * b:4 * b + 4].rearrange("p (s o) -> p s o", o=1),
                        ps4[b][:, 64:324].rearrange(
                            "p (s o) -> p s o", o=65)[:, :, 0:1],
                        1e-6, None, ALU.add), "V" if b == 3 else None)
                p.wait("vector", "V", dxc)
                rcd = p.op("vector", lambda v, par=par: v.reciprocal(
                    recb[:, par, :], recb[:, par, :]), "V")
                p.wait("vector", "V", rcd)
                p.wait("scalar", "V", rcd)
                guard("scalar", ("ybuf", par))
                guard("vector", ("ybuf", par))
                pysc = {}
                for hp in range(8):
                    yeng = "scalar" if hp % 2 == 0 else "vector"
                    for hh in range(2):
                        h = 2 * hp + hh
                        bank = ps4[h % 4]
                        s_ = h // 4
                        if yeng == "scalar":
                            p.op("scalar", lambda s, bank=bank, h=h, s_=s_, par=par:
                                 s.activation(ktok[:, par, 64 * h:64 * (h + 1)],
                                              bank[:, 65 * s_:65 * s_ + 64],
                                              AF.Identity,
                                              scale=recb[:, par, 4 * (h % 4) + s_:
                                                         4 * (h % 4) + s_ + 1]),
                                 "S")
                        else:
                            p.op("vector", lambda v, bank=bank, h=h, s_=s_, par=par:
                                 v.tensor_scalar(ktok[:, par, 64 * h:64 * (h + 1)],
                                                 bank[:, 65 * s_:65 * s_ + 64],
                                                 recb[:, par, 4 * (h % 4) + s_:
                                                      4 * (h % 4) + s_ + 1],
                                                 None, ALU.mult), "V")
                    pysc[hp] = ("S" if yeng == "scalar" else "V",
                                p.now("S" if yeng == "scalar" else "V"))
                ysc_n = p.now("S")
                yscv_n = p.now("V")
                for b in range(4):
                    free[("nbank", b)] = ("S", ysc_n)
                    free[("nbankv", b)] = ("V", yscv_n)
                free[("recb", par)] = ("V", yscv_n)
                free[("recbs", par)] = ("S", ysc_n)
                tr = None
                for et in range(8):
                    p.wait("tensor", pysc[et][0], pysc[et][1])
                    bank = psAB[et % 2]
                    fslot = (et // 2) % 4
                    guard("tensor", ("psAB", et % 2))
                    tr = p.op("tensor", lambda t, bank=bank, fslot=fslot, et=et, par=par:
                              t.transpose(bank[:, 128 * fslot:128 * (fslot + 1)],
                                          ktok[:, par, 128 * et:128 * (et + 1)], ident[:]),
                              "T" if et == 7 else None)
                free[("ybuf", par)] = ("T", tr)
                p.wait("vector", "T", tr)
                for bi in range(2):
                    dst = yTb[:, par, bi:8:2, :]
                    ev = p.op("vector", lambda v, bi=bi, dst=dst: v.tensor_copy(
                        dst, psAB[bi][:, 0:512]), "V")
                    free[("psAB", bi)] = ("V", ev)
                yT_done = p.now("V")
                p.wait("tensor", "V", yT_done)
                guard("tensor", ("psGH", 0))
                guard("tensor", ("psGH", 1))
                prj = None
                for et in range(8):
                    for eh in range(2):
                        prj = p.op("tensor", lambda t, et=et, eh=eh, par=par: t.matmul(
                            psGH[eh][:, 0:512], yTb[:, par, et, :],
                            wT(3, et, 512 * eh, 512),
                            start=(et == 0), stop=(et == 7)),
                            "T" if (et == 7 and eh == 1) else None)
                free[("yTb", par)] = ("T", prj)
                p.wait("vector", "T", prj)
                p.waitm("vector", dac)
                guard("vector", ("accbr2", par))
                for eh in range(2):
                    p.op("vector", lambda v, eh=eh, par=par: v.tensor_tensor(
                        accb[:, par, 512 * eh:512 * (eh + 1)], psGH[eh][:, 0:512],
                        accb[:, par, 512 * eh:512 * (eh + 1)], ALU.add),
                        "V" if eh == 1 else None)
                fout = p.now("V")
                free[("psGH", 0)] = ("V", fout)
                free[("psGH", 1)] = ("V", fout)
                p.wait("scalar", "V", fout)
                so = p.dma("scalar", lambda s, c=c, par=par: s.dma_start(
                    out=out_d[CH * c:CH * (c + 1), :], in_=accb[:, par, :]))
                free[("accbr", par)] = ("V", fout)
                free[("accbr2", par)] = so
                free[("saugT", par)] = ("T", p.now("T"))
            free[("p2qT", gp)] = ("T", p.now("T"))
            free[("p2xnT", gp)] = ("T", p.now("T"))
            free[("p2kT", gp)] = ("T", p.now("T"))
            free[("p2va", gp)] = ("T", p.now("T"))

        p.dma_barrier("gpsimd")

        with nc.Block() as block:
            @block.sync
            def _(sync):
                for fn in p.q["sync"]:
                    fn(sync)

            @block.vector
            def _(vector):
                for fn in p.q["vector"]:
                    fn(vector)

            @block.scalar
            def _(scalar):
                for fn in p.q["scalar"]:
                    fn(scalar)

            @block.tensor
            def _(tensor):
                for fn in p.q["tensor"]:
                    fn(tensor)

            @block.gpsimd
            def _(gpsimd):
                for fn in p.q["gpsimd"]:
                    fn(gpsimd)

    return nc


def make_inputs(c, x_full, w_attn, w_proj, W1, b1, W2, b2, span=4096):
    b, g = divmod(c, 4)
    m = np.zeros((CH, 8), np.float32)
    for j in range(8):
        if j // 4 == b and j % 4 < g:
            m[:, j] = 1.0
    maskT = np.tril(np.ones((CH, CH), np.float32)).T.copy()
    return {
        "x": np.ascontiguousarray(x_full[b, g * span:(g + 1) * span]),
        "wqT": np.ascontiguousarray(w_attn[0:DM].T),
        "wkT": np.ascontiguousarray(w_attn[DM:2 * DM].T),
        "wvT": np.ascontiguousarray(w_attn[2 * DM:3 * DM].T),
        "wpT": np.ascontiguousarray(w_proj.T),
        "W1T": np.ascontiguousarray(W1.T),
        "b1": np.ascontiguousarray(b1.reshape(32, CH).T),
        "W2T": np.ascontiguousarray(W2.T),
        "b2": np.ascontiguousarray(b2),
        "ident": np.eye(CH, dtype=np.float32),
        "maskT": maskT,
        "pmask": m,
    }


def kernel(L0_attn_x, L0_attn_w_attn, L0_attn_w_proj, L0_ffn_W1, L0_ffn_b1,
           L0_ffn_W2, L0_ffn_b2):
    x = np.asarray(L0_attn_x, dtype=np.float32)
    w_attn = np.asarray(L0_attn_w_attn, dtype=np.float32)
    w_proj = np.asarray(L0_attn_w_proj, dtype=np.float32)
    W1 = np.asarray(L0_ffn_W1, dtype=np.float32)
    b1 = np.asarray(L0_ffn_b1, dtype=np.float32)
    W2 = np.asarray(L0_ffn_W2, dtype=np.float32)
    b2 = np.asarray(L0_ffn_b2, dtype=np.float32)
    span = 4096
    nc = build(span)
    in_maps = [make_inputs(c, x, w_attn, w_proj, W1, b1, W2, b2, span)
               for c in range(8)]
    res = run_bass_kernel_spmd(nc, in_maps, core_ids=list(range(8)))
    out = np.empty((2, 16384, DM), np.float32)
    for c in range(8):
        b, g = divmod(c, 4)
        out[b, g * span:(g + 1) * span] = res.results[c]["out"]
    return out
